# version 07
# baseline (speedup 1.0000x reference)
import contextlib
import numpy as np
import ml_dtypes
import concourse.bass as bass
import concourse.mybir as mybir
from concourse.bass_utils import run_bass_kernel_spmd

F32 = mybir.dt.float32
BF16 = mybir.dt.bfloat16
ALU = mybir.AluOpType
AF = mybir.ActivationFunctionType
AX = mybir.AxisListType.X

B, S, D, H, DH, BW = 4, 4096, 1024, 16, 64, 128
T, HALO, C = 2048, 3, 128
NCH = T // C
I_FFN = 4096
PI = float(np.pi)
TWO_PI = float(2 * np.pi)
THRQ = [0.25 ** 2, 0.75 ** 2, 1.25 ** 2, 1.75 ** 2, 2.5 ** 2, 3.5 ** 2, 5.0 ** 2]

_CACHE = {}


def _host_maps(inputs):
    x = np.ascontiguousarray(np.asarray(inputs["L0_attn_hidden"], np.float32))
    perm = np.concatenate([np.arange(0, BW, 2), np.arange(1, BW, 2)])
    cond_in_W = np.ascontiguousarray(np.asarray(inputs["L0_attn_cond_in_W"], np.float32)[:, perm])
    delta_W = np.ascontiguousarray(np.asarray(inputs["L0_attn_delta_W"], np.float32)[perm, :])
    cnw = np.ascontiguousarray(
        np.asarray(inputs["L0_attn_cond_norm_w"], np.float32)[perm].reshape(BW, 1))
    theta = np.asarray(inputs["L0_attn_rot_theta"], np.float32).reshape(1, 64)
    ctrl_W = np.ascontiguousarray(np.asarray(inputs["L0_attn_ctrl_W"], np.float32))
    ctrl_b = np.asarray(inputs["L0_attn_ctrl_b"], np.float32).reshape(-1)
    cb = np.zeros((128, 34), np.float32)
    for cc in range(24):
        cb[:, cc] = ctrl_b[128 * cc:128 * cc + 128]
    cb[0:16, 24] = ctrl_b[3072:3088]
    cb[0:16, 25] = ctrl_b[3088:3104]
    for g in range(8):
        cb[:, 26 + g] = ctrl_b[3104 + 128 * g:3104 + 128 * g + 128]
    ramp = np.asarray(inputs["L0_attn_qkv_ramp_logit"], np.float32)
    rampt = np.zeros((128, 24), np.float32)
    for kind in range(3):
        for fc in range(8):
            rampt[:, kind * 8 + fc] = ramp[kind, 128 * fc:128 * fc + 128]
    beta_ramp = np.asarray(inputs["L0_attn_beta_ramp_logit"], np.float32).reshape(H, 1)
    convw = np.zeros((128, 96), np.float32)
    for kind, key in enumerate(("q", "k", "v")):
        w = np.asarray(inputs[f"L0_attn_{key}_conv_w"], np.float32)
        for fc in range(8):
            convw[:, (kind * 8 + fc) * 4:(kind * 8 + fc) * 4 + 4] = \
                w[128 * fc:128 * fc + 128]
    onw2 = np.concatenate([np.asarray(inputs["L0_attn_out_norm_w"], np.float32).reshape(DH, 1)] * 2)
    Wo = np.ascontiguousarray(np.asarray(inputs["L0_attn_out_proj_W"], np.float32))
    bo = np.asarray(inputs["L0_attn_out_proj_b"], np.float32).reshape(1, D)
    gw = np.ascontiguousarray(np.asarray(inputs["L0_ffn_gate_w"], np.float32))
    uw = np.ascontiguousarray(np.asarray(inputs["L0_ffn_up_w"], np.float32))
    dw = np.ascontiguousarray(np.asarray(inputs["L0_ffn_down_w"], np.float32))
    s_in = float(np.asarray(inputs["L0_ffn_s_in"]).reshape(-1)[0])
    s_in_d = float(np.asarray(inputs["L0_ffn_s_in_down"]).reshape(-1)[0])

    cst = {}
    cst["ident"] = np.eye(128).astype(ml_dtypes.bfloat16)
    cst["ident_f"] = np.eye(128).astype(np.float32)
    tri = np.tril(np.ones((128, 128), np.float32))
    cst["trioneB"] = np.ascontiguousarray(tri.T).astype(ml_dtypes.bfloat16)
    cst["trioneF"] = np.ascontiguousarray(tri.T).astype(np.float32)
    jj, tt2 = np.meshgrid(np.arange(128), np.arange(128), indexing="ij")
    cst["maskMc"] = np.where(jj < tt2, 0.0, -1e30).astype(np.float32)
    cst["maskAc"] = np.where(jj <= tt2, 0.0, -1e30).astype(np.float32)
    cst["iotar"] = np.arange(1, 129, dtype=np.float32).reshape(1, 128)
    cst["ones1f"] = np.ones((1, 512), np.float32)
    cst["ones1b"] = np.ones((1, 512), ml_dtypes.bfloat16)
    cst["onescb"] = np.ones((128, 1), ml_dtypes.bfloat16)
    cst["i64c"] = np.eye(64).astype(np.float32)
    e127 = np.zeros((128, 1), np.float32); e127[127, 0] = 1.0
    cst["e127"] = e127
    e2 = np.zeros((128, 1), np.float32); e2[2, 0] = 1.0
    cst["e2"] = e2

    maps = []
    for i in range(8):
        b_, h_ = i % 4, i // 4
        xs = np.zeros((T + HALO, D), np.float32)
        if h_ == 0:
            xs[HALO:] = x[b_, 0:T]
        else:
            xs[:] = x[b_, h_ * T - HALO:h_ * T + T]
        sel = np.zeros((128, 8), np.float32)
        if h_ == 1:
            sel[:, i - 4] = 1.0
        m = dict(
            x_sh=xs, sel=sel,
            mask=np.full((128, 1), 1.0 if h_ == 1 else 0.0, np.float32),
            cond_in_W=cond_in_W, delta_W=delta_W, cnw=cnw, theta=theta,
            ctrl_W=ctrl_W, cbias=cb, rampq=rampt, beta_ramp=beta_ramp,
            convw=convw, onw2=onw2, Wo=Wo, bo=bo, gw=gw, uw=uw, dw=dw,
            s_in=np.full((128, 1), s_in, np.float32),
            s_in_d=np.full((128, 1), s_in_d, np.float32),
        )
        m.update(cst)
        maps.append(m)
    return maps


class Ser:
    def __init__(self, nc, E):
        self.nc = nc
        self.E = E
        self.tok = E(nc.semaphore("tok"))
        self.dsem = E(nc.semaphore("dsem"))
        self.gsem = E(nc.semaphore("gsem"))
        self.n = 0
        self.dn = 0
        self.gn = 0
        self.cn = 0
        self.cur = None
        self.last = None

    def _wrap(self, e):
        ser = self

        class W:
            def __getattr__(_s, name):
                if name in ("nop", "engine_nop"):
                    return lambda *a, **k: None
                fn = getattr(e, name)
                if name in ("wait_ge", "then_inc"):
                    return fn

                def call(*a, **k):
                    e.wait_ge(ser.tok, ser.n)
                    r = fn(*a, **k)
                    r.then_inc(ser.tok, 1)
                    ser.n += 1
                    return r
                return call
        return W()

    @property
    def v(self):
        return self._wrap(self.nc.vector)

    @property
    def s(self):
        return self._wrap(self.nc.scalar)

    @property
    def t(self):
        return self._wrap(self.nc.tensor)

    def dma(self, out, in_, cast=False):
        if cast:
            e, sem = self.nc.gpsimd, self.gsem
            self.gn += 16
            tgt = self.gn
        else:
            e, sem = self.nc.sync, self.dsem
            self.dn += 16
            tgt = self.dn
        e.wait_ge(self.tok, self.n)
        e.dma_start(out=out, in_=in_).then_inc(sem, 16)
        e.wait_ge(sem, tgt)
        e.nop().then_inc(self.tok, 1)
        self.n += 1

    def cc(self, kind, ins, outs):
        e = self.nc.gpsimd
        csem = self.E(self.nc.semaphore("csem%d" % self.cn))
        self.cn += 1
        e.wait_ge(self.tok, self.n)
        e.collective_compute(kind, ALU.bypass, replica_groups=[list(range(8))],
                             ins=ins, outs=outs).then_inc(csem, 1)
        e.wait_ge(csem, 1)
        e.nop().then_inc(self.tok, 1)
        self.n += 1


def emit_quant(Sr, at, outq, N, q, gs6, gsi, bf16_s):
    v = Sr.v
    nb = N // 16

    def nop():
        v.engine_nop()

    av_all = at.rearrange("p (b k) -> p b k", k=16)
    v.tensor_reduce(out=q["b0"][:, :nb], in_=av_all, axis=AX, op=ALU.max,
                    apply_absolute_value=True)
    nop(); nop()
    v.tensor_scalar(out=q["b1"][:, :nb], in0=q["b0"][:, :nb], scalar1=gs6,
                    scalar2=448.0, op0=ALU.mult, op1=ALU.min)
    nop()
    v.memset(q["b2"][:, :nb], 2.0 ** -9)
    pp, cur = [q["b2"], q["b3"]], 0
    for i in range(-5, 9):
        v.tensor_scalar(out=q["b4"][:, :nb], in0=q["b1"][:, :nb],
                        scalar1=float(2.0 ** i), scalar2=1.0, op0=ALU.is_ge, op1=ALU.add)
        nop()
        v.tensor_tensor(out=pp[1 - cur][:, :nb], in0=pp[cur][:, :nb],
                        in1=q["b4"][:, :nb], op=ALU.mult)
        nop()
        cur = 1 - cur
    q2, spare = pp[cur], pp[1 - cur]
    v.reciprocal(out=q["b4"][:, :nb], in_=q2[:, :nb])
    nop()
    v.tensor_mul(out=spare[:, :nb], in0=q["b4"][:, :nb], in1=q["b1"][:, :nb])
    nop()
    v.memset(q["b1"][:, :nb], 0.0)
    pp2, cur2 = [q["b1"], q["b4"]], 0
    for j in range(16):
        v.scalar_tensor_tensor(out=pp2[1 - cur2][:, :nb], in0=spare[:, :nb],
                               scalar=j + 0.5, in1=pp2[cur2][:, :nb],
                               op0=ALU.is_ge, op1=ALU.add)
        nop()
        cur2 = 1 - cur2
    acc = pp2[cur2]
    v.tensor_tensor(out=spare[:, :nb], in0=acc[:, :nb], in1=q2[:, :nb], op=ALU.mult)
    nop()
    v.tensor_scalar(out=q["b0"][:, :nb], in0=spare[:, :nb], scalar1=gsi,
                    scalar2=None, op0=ALU.mult)
    nop()
    v.tensor_scalar(out=q["b4"][:, :nb], in0=q["b0"][:, :nb], scalar1=1e-30,
                    scalar2=None, op0=ALU.max)
    nop()
    v.reciprocal(out=q["b1"][:, :nb], in_=q["b4"][:, :nb])
    nop(); nop()
    for hh in range(N // 512):
        c0, nb0 = 512 * hh, 32 * hh
        av = at[:, c0:c0 + 512].rearrange("p (b k) -> p b k", k=16)
        uv = q["u"].rearrange("p (b k) -> p b k", k=16)
        v.tensor_tensor(out=uv, in0=av,
                        in1=q["b1"][:, nb0:nb0 + 32].unsqueeze(2).broadcast_to([128, 32, 16]),
                        op=ALU.mult)
        sdst = q["sb"] if bf16_s else q["s"]
        v.tensor_tensor(out=sdst, in0=q["u"], in1=q["u"], op=ALU.mult)
        v.tensor_scalar(out=q["qA"], in0=sdst, scalar1=THRQ[0], scalar2=None, op0=ALU.is_ge)
        pa_, cu = [q["qA"], q["qC"]], 0
        for thr in THRQ[1:4]:
            v.scalar_tensor_tensor(out=pa_[1 - cu], in0=sdst, scalar=thr,
                                   in1=pa_[cu], op0=ALU.is_ge, op1=ALU.add)
            cu = 1 - cu
        qa, sparea = pa_[cu], pa_[1 - cu]
        v.tensor_scalar(out=q["qB"], in0=sdst, scalar1=THRQ[4], scalar2=None, op0=ALU.is_ge)
        pb_, cu2 = [q["qB"], q["qD"]], 0
        for thr in (THRQ[5], THRQ[6], THRQ[6]):
            v.scalar_tensor_tensor(out=pb_[1 - cu2], in0=sdst, scalar=thr,
                                   in1=pb_[cu2], op0=ALU.is_ge, op1=ALU.add)
            cu2 = 1 - cu2
        qb = pb_[cu2]
        v.scalar_tensor_tensor(out=sparea, in0=qa, scalar=0.5, in1=qb,
                               op0=ALU.mult, op1=ALU.add)
        v.tensor_scalar(out=q["qB"], in0=q["u"], scalar1=0.0, scalar2=2.0,
                        op0=ALU.is_ge, op1=ALU.mult)
        qv = q["qd"].rearrange("p (b k) -> p b k", k=16)
        v.tensor_tensor(out=qv, in0=sparea.rearrange("p (b k) -> p b k", k=16),
                        in1=q["b0"][:, nb0:nb0 + 32].unsqueeze(2).broadcast_to([128, 32, 16]),
                        op=ALU.mult)
        v.tensor_scalar(out=q["qD"], in0=q["qB"], scalar1=1.0, scalar2=None, op0=ALU.subtract)
        nop()
        v.tensor_tensor(out=outq[:, c0:c0 + 512], in0=q["qd"], in1=q["qD"], op=ALU.mult)


def build_nc():
    nc = bass.Bass("TRN2", target_bir_lowering=False)
    stk = contextlib.ExitStack()
    E = stk.enter_context

    d = {}
    specs = dict(
        x_sh=([T + HALO, D], F32), sel=([128, 8], F32), mask=([128, 1], F32),
        cond_in_W=([D, BW], F32), delta_W=([BW, 3 * D + H], F32), cnw=([BW, 1], F32),
        theta=([1, 64], F32), ctrl_W=([D, 4128], F32), cbias=([128, 34], F32),
        rampq=([128, 24], F32), beta_ramp=([H, 1], F32), convw=([128, 96], F32),
        onw2=([128, 1], F32), Wo=([D, D], F32), bo=([1, D], F32),
        gw=([I_FFN, D], F32), uw=([I_FFN, D], F32), dw=([D, I_FFN], F32),
        s_in=([128, 1], F32), s_in_d=([128, 1], F32),
        ident=([128, 128], BF16), ident_f=([128, 128], F32),
        trioneB=([128, 128], BF16), trioneF=([128, 128], F32),
        maskMc=([128, 128], F32), maskAc=([128, 128], F32), iotar=([1, 128], F32),
        ones1f=([1, 512], F32), ones1b=([1, 512], BF16), onescb=([128, 1], BF16),
        i64c=([64, 64], F32), e127=([128, 1], F32), e2=([128, 1], F32),
    )
    for k, (shp, dt) in specs.items():
        d[k] = nc.dram_tensor(k, shp, dt, kind="ExternalInput").ap()
    y_out = nc.dram_tensor("y_sh", [T, D], F32, kind="ExternalOutput").ap()

    dr = dict(
        oloc=nc.dram_tensor("oloc_d", [T, D], BF16).ap(),
        gneg=nc.dram_tensor("gneg_d", [NCH * H * 64, 128], BF16).ap(),
        ptqc=nc.dram_tensor("ptqc_d", [NCH * H * 64, 128], BF16).ap(),
        gffn=nc.dram_tensor("gffn_d", [T, D], BF16).ap(),
        hq=nc.dram_tensor("hq_d", [T, I_FFN], BF16).ap(),
        gwq=nc.dram_tensor("gwq_d", [D, I_FFN], BF16).ap(),
        uwq=nc.dram_tensor("uwq_d", [D, I_FFN], BF16).ap(),
        dwq=nc.dram_tensor("dwq_d", [I_FFN, D], BF16).ap(),
    )
    dr["cc1i"] = nc.dram_tensor("cc1_in", [4, 128], F32)
    dr["cc1o"] = nc.dram_tensor("cc1_out", [32, 128], F32)
    dr["cc2i"] = nc.dram_tensor("cc2_in", [H * 64, 64], F32)
    dr["cc2o"] = nc.dram_tensor("cc2_out", [8 * H * 64, 64], F32)

    sb = {}

    def A(name, shape, dt):
        sb[name] = E(nc.sbuf_tensor("s_" + name, shape, dt))

    A("bigA", [128, 16384], BF16)
    A("bigB", [128, 16384], BF16)
    A("bigC", [128, 16384], BF16)
    A("bigD", [128, 16384], BF16)
    A("bigE", [128, 8192], BF16)
    A("cndf", [128, 2048], BF16)
    A("condW", [128, 1024], BF16)
    A("deltW", [128, 3088], BF16)
    A("convw", [128, 96], F32)
    A("rampq", [128, 24], F32)
    A("rampb", [16, 1], F32)
    A("cbias", [128, 34], F32)
    A("tabc", [128, 64], F32)
    A("tabs", [128, 64], F32)
    A("wslab", [128, 1024], BF16)
    A("xt", [128, 1024], F32)
    A("xrow", [128, 1024], F32)
    A("selw", [128, 8], F32)
    A("maskc", [128, 1], F32)
    A("gs6", [128, 1], F32); A("gsi", [128, 1], F32)
    A("gs6d", [128, 1], F32); A("gsid", [128, 1], F32)
    A("gs6w", [128, 1], F32); A("gsiw", [128, 1], F32)
    A("sc1", [128, 1], F32); A("sc2", [128, 1], F32); A("sc3", [128, 1], F32)
    A("ceps5", [128, 1], F32); A("ceps6", [128, 1], F32); A("cone", [128, 1], F32)
    A("onescf", [128, 1], F32); A("lamC", [1, 16], F32); A("e127", [128, 1], F32)
    A("e2", [128, 1], F32)
    A("sc4", [128, 1], F32)
    A("ln1", [128, 1], F32); A("ln2", [128, 1], F32)
    A("iotar", [1, 128], F32)
    A("trioneB", [128, 128], BF16)
    A("trioneF", [128, 128], F32)
    A("ident", [128, 128], BF16)
    A("ident_f", [128, 128], F32)
    A("maskM", [128, 128], F32)
    A("maskA", [128, 128], F32)
    A("ones1f", [1, 512], F32)
    A("ones1b", [1, 512], BF16)
    A("onescb", [128, 1], BF16)
    A("i64b", [64, 64], F32)
    A("onwc", [128, 1], F32)
    A("bo16", [1, 1024], BF16)
    A("thetar", [1, 64], F32)
    A("z0row", [1, 128], F32)
    A("tt1", [128, 128], F32); A("tt2", [128, 128], F32)
    A("tt3", [128, 128], F32); A("tt4", [128, 64], F32)
    A("upk", [128, 128], BF16)
    A("haloz", [4, 128], F32); A("halo2", [4, 128], F32)
    A("condh", [128, 3], BF16)
    A("xnhf", [128, 24], BF16)
    A("pre", [128, 520], F32)
    A("tails", [128, 72], F32)
    A("convo", [128, 512], F32)
    A("hid", [128, 512], F32)
    A("dbt", [128, 128], F32)
    A("lng", [128, 16], F32); A("lnb", [128, 16], F32); A("beta", [128, 16], F32)
    A("lam", [128, 16], F32); A("l1q", [128, 16], F32); A("l1k", [128, 16], F32)
    A("rowM", [128, 16], F32); A("rowA", [128, 16], F32); A("colv", [128, 16], F32)
    A("coefk", [128, 16], F32); A("coefp", [128, 16], F32)
    A("t16a", [128, 16], F32); A("t16b", [128, 16], F32)
    A("eAall", [128, 256], F32)
    A("gCc", [64, 16], F32)
    A("pk2a", [128, 2], F32); A("pk2b", [128, 2], F32); A("pk2c", [128, 2], F32)
    A("stk2", [2, 128], F32); A("stk2r", [2, 128], F32); A("stk2q", [2, 128], F32)
    A("sq16", [128, 128], BF16)
    A("AqS", [128, 512], BF16)
    A("Ua", [128, 512], BF16); A("Ub", [128, 512], BF16)
    A("Pa", [128, 512], BF16); A("Pb", [128, 512], BF16)
    A("PT", [128, 512], BF16)
    A("kpt", [128, 256], BF16)
    A("t64", [64, 64], F32)
    A("pk64", [64, 512], BF16)
    A("opre", [128, 64], F32); A("opre2", [128, 64], F32)
    A("ot64", [128, 64], BF16)

    ps = {}
    for name, dt in (("pa", F32), ("pb", F32), ("pc", F32), ("pd", F32),
                     ("pe", F32), ("pf", F32)):
        ps[name] = E(nc.psum_tensor(name, [128, 512], dt))
    ps["pt"] = E(nc.psum_tensor("pt", [128, 512], BF16))

    blk = E(nc.Block())

    @blk.sync
    def _(sync):
        Sr = Ser(nc, E)
        _emit(nc, Sr, d, y_out, sb, ps, dr)
        if Sr.last is not None:
            Sr.last.then_inc(Sr.tok, 1)
            Sr.n += 1
        nc.sync.wait_ge(Sr.tok, Sr.n)

    stk.close()
    return nc


def _emit(nc, Sr, d, y_out, sb, ps, dr):
    pa, pb, pc, pd, pe, pf = (ps[k][:, :] for k in ("pa", "pb", "pc", "pd", "pe", "pf"))
    pt = ps["pt"][:, :]
    bigA, bigB, bigC, bigD, bigE = (sb[k][:, :] for k in
                                    ("bigA", "bigB", "bigC", "bigD", "bigE"))
    bigC_f = bigC.bitcast(F32)
    bigD_f = bigD.bitcast(F32)
    xt2v = bigD_f[:, 0:1024]
    vw = {"qu": bigD_f[:, 1024:1536], "qqd": bigD_f[:, 1536:2048],
          "qs": bigD_f[:, 2048:2560], "qqA": bigD_f[:, 2560:3072],
          "qqB": bigD_f[:, 3072:3584], "qqC": bigD_f[:, 3584:4096],
          "qqD": bigD_f[:, 4096:4608],
          "qb0": bigD_f[:, 4608:4672], "qb1": bigD_f[:, 4672:4736],
          "qb2": bigD_f[:, 4736:4800], "qb3": bigD_f[:, 4800:4864],
          "qb4": bigD_f[:, 4864:4928],
          "convb": bigD[:, 9856:10368], "xb16": bigD[:, 10368:11392]}
    p_sb = bigC_f[:, 0:2048]
    z_sb = bigC_f[:, 2048:4096]
    ident = sb["ident"][:, :]
    identf = sb["ident_f"][:, :]

    def ap(n):
        return sb[n][:, :]

    qd = dict(b0=vw["qb0"], b1=vw["qb1"], b2=vw["qb2"], b3=vw["qb3"], b4=vw["qb4"],
              u=vw["qu"], qd=vw["qqd"], s=vw["qs"],
              sb=vw["qs"].bitcast(BF16)[:, 0:512],
              qA=vw["qqA"], qB=vw["qqB"], qC=vw["qqC"], qD=vw["qqD"])

    def vnop(k=2):
        v = Sr.v
        for _ in range(k):
            v.engine_nop()

    def snop():
        s_ = Sr.s
        s_.nop(); s_.nop()

    def idslice(base, n):
        return sb["ident"][base:base + n, base:base + n]

    def idslicef(base, n):
        return sb["ident_f"][base:base + n, base:base + n]

    v = Sr.v
    v.memset(sb["ceps5"][:, :], 1e-5)
    v.memset(sb["ceps6"][:, :], 1e-6)
    v.memset(sb["cone"][:, :], 1.0)
    v.memset(sb["onescf"][:, :], 1.0)
    v.memset(sb["pk2a"][:, :], 1.0)
    v.memset(sb["pk2b"][:, :], 1.0)
    v.memset(sb["pk2c"][:, :], 1.0)
    vnop()
    for nm in ("ident", "ident_f", "trioneB", "trioneF", "iotar", "ones1f",
               "ones1b", "onescb", "convw", "cbias", "rampq"):
        Sr.dma(ap(nm), d[nm])
    Sr.dma(ap("selw"), d["sel"])
    Sr.dma(ap("e127"), d["e127"])
    Sr.dma(ap("e2"), d["e2"])
    Sr.dma(ap("maskM"), d["maskMc"])
    Sr.dma(ap("maskA"), d["maskAc"])
    Sr.dma(ap("i64b"), d["i64c"])
    Sr.dma(ap("maskc"), d["mask"])
    Sr.dma(ap("thetar"), d["theta"])
    Sr.dma(sb["onwc"][:, :], d["onw2"])
    Sr.dma(sb["condW"][:, :].rearrange("p (dc w) -> p dc w", dc=8),
           d["cond_in_W"].rearrange("(dc p) w -> p dc w", p=128), cast=True)
    Sr.dma(ap("sc1"), d["cnw"])
    for c0, w in ((0, 1024), (1024, 1024), (2048, 1024), (3072, 16)):
        Sr.dma(sb["xt"][:, 0:w], d["delta_W"][:, c0:c0 + w])
        v = Sr.v
        v.tensor_scalar(out=sb["deltW"][:, c0:c0 + w], in0=sb["xt"][:, 0:w],
                        scalar1=sb["sc1"][:, 0:1], scalar2=None, op0=ALU.mult)
        vnop()
    s_ = Sr.s
    s_.activation(out=xt2v[:, 0:24], in_=sb["rampq"][:, 0:24], func=AF.Tanh, scale=0.5)
    snop()
    v = Sr.v
    v.tensor_scalar(out=ap("rampq"), in0=xt2v[:, 0:24], scalar1=0.5,
                    scalar2=0.5, op0=ALU.mult, op1=ALU.add)
    vnop()
    Sr.dma(sb["xt"][0:16, 0:1], d["beta_ramp"])
    s_ = Sr.s
    s_.activation(out=xt2v[0:16, 0:1], in_=sb["xt"][0:16, 0:1], func=AF.Tanh, scale=0.5)
    snop()
    v = Sr.v
    v.tensor_scalar(out=ap("rampb"), in0=xt2v[0:16, 0:1], scalar1=0.5,
                    scalar2=0.5, op0=ALU.mult, op1=ALU.add)
    vnop()
    Sr.dma(ap("sc1"), d["s_in"])
    Sr.dma(ap("sc2"), d["s_in_d"])
    v = Sr.v
    v.tensor_scalar(out=ap("gs6"), in0=ap("sc1"), scalar1=1.0 / 6.0, scalar2=None, op0=ALU.mult)
    v.tensor_scalar(out=ap("gs6d"), in0=ap("sc2"), scalar1=1.0 / 6.0, scalar2=None, op0=ALU.mult)
    vnop()
    v.reciprocal(out=ap("gsi"), in_=ap("sc1"))
    vnop()
    v.reciprocal(out=ap("gsid"), in_=ap("sc2"))
    vnop()
    Sr.dma(sb["bo16"][:, :], d["bo"], cast=True)
    t_ = Sr.t
    t_.matmul(out=pf[:, 0:64], lhsT=sb["iotar"][0:1, :], rhs=ap("thetar"),
              start=True, stop=True)
    _rot_table(Sr, sb, pf[:, 0:64], "tabs", 0.0)
    t_ = Sr.t
    t_.matmul(out=pf[:, 64:128], lhsT=sb["iotar"][0:1, :], rhs=ap("thetar"),
              start=True, stop=True)
    _rot_table(Sr, sb, pf[:, 64:128], "tabc", PI / 2.0)

    for tc in range(NCH):
        Sr.dma(ap("xt"), d["x_sh"][HALO + 128 * tc:HALO + 128 * (tc + 1), :])
        _ln(Sr, sb, ap("xt"), xt2v, 128)
        v = Sr.v
        v.tensor_copy(out=vw["xb16"], in_=xt2v)
        _tr_to(Sr, sb, ps, vw["xb16"], bigA, tc, 8)
        emit_quant(Sr, xt2v, vw["xb16"], 1024, qd,
                   sb["gs6"][:, 0:1], sb["gsi"][:, 0:1], bf16_s=False)
        _tr_to(Sr, sb, ps, vw["xb16"], bigB, tc, 8)
    Sr.dma(sb["xt"][0:3, :], d["x_sh"][0:HALO, :])
    _ln(Sr, sb, sb["xt"][0:3, :], sb["xt"][0:3, :], 3)
    v = Sr.v
    v.tensor_copy(out=vw["xb16"][0:3, :], in_=sb["xt"][0:3, :])
    vnop()
    for dc in range(8):
        t_ = Sr.t
        t_.transpose(out=ps["pt"][:, 0:3], in_=vw["xb16"][0:3, 128 * dc:128 * (dc + 1)],
                     identity=idslice(0, 3))
        v = Sr.v
        v.tensor_copy(out=sb["xnhf"][:, 3 * dc:3 * dc + 3], in_=ps["pt"][:, 0:3])
        vnop()

    for tc in range(NCH):
        t_ = Sr.t
        for dc in range(8):
            t_.matmul(out=pf[:, 0:128],
                      lhsT=bigA[:, 2048 * dc + 128 * tc:2048 * dc + 128 * (tc + 1)],
                      rhs=sb["condW"][:, 128 * dc:128 * (dc + 1)],
                      start=(dc == 0), stop=(dc == 7))
        v = Sr.v
        v.tensor_copy(out=p_sb[:, 128 * tc:128 * (tc + 1)], in_=pf[:, 0:128])
    v = Sr.v
    v.memset(sb["z0row"][:, :], 0.0)
    vnop()
    _rot_pass(Sr, sb, ps, p_sb, z_sb)
    Sr.dma(dr["cc1i"].ap()[0:3, :], z_sb[125:128, 128 * 15:128 * 16])
    Sr.dma(dr["cc1i"].ap()[3:4, :], z_sb[127:128, 128 * 15:128 * 16])
    Sr.cc("AllGather", [dr["cc1i"].ap().opt()], [dr["cc1o"].ap().opt()])
    Sr.dma(sb["xt"][0:4, 0:1024].rearrange("p (r c) -> p r c", c=128),
           dr["cc1o"].ap().rearrange("(r p) c -> p r c", p=4))
    v = Sr.v
    v.memset(sb["haloz"][0:3, :], 0.0)
    vnop()
    accp, cur = [sb["haloz"], sb["halo2"]], 0
    for r in range(8):
        v.scalar_tensor_tensor(out=accp[1 - cur][0:3, :],
                               in0=sb["xt"][0:3, 128 * r:128 * (r + 1)],
                               scalar=sb["selw"][0:3, r:r + 1], in1=accp[cur][0:3, :],
                               op0=ALU.mult, op1=ALU.add)
        vnop()
        cur = 1 - cur
    zrecv = accp[cur][0:3, :]
    zscr = accp[1 - cur]
    t_ = Sr.t
    t_.matmul(out=ps["pf"][0:1, 128:256], lhsT=sb["e2"][0:3, 0:1], rhs=zrecv,
              start=True, stop=True)
    v = Sr.v
    v.tensor_copy(out=sb["z0row"][:, :], in_=ps["pf"][0:1, 128:256])
    vnop()
    v.tensor_tensor(out=zscr[0:3, :], in0=zrecv, in1=zrecv, op=ALU.mult)
    vnop()
    v.tensor_reduce(out=sb["ln1"][0:3, :], in_=zscr[0:3, :], axis=AX, op=ALU.add)
    vnop()
    s_ = Sr.s
    s_.activation(out=sb["ln2"][0:3, :], in_=sb["ln1"][0:3, :], func=AF.Ln,
                  scale=1.0 / 128.0, bias=sb["ceps6"][0:3, 0:1])
    snop()
    s_ = Sr.s
    s_.activation(out=sb["ln1"][0:3, :], in_=sb["ln2"][0:3, :], func=AF.Exp, scale=-0.5)
    snop()
    v = Sr.v
    v.tensor_scalar(out=zscr[0:3, :], in0=zrecv, scalar1=sb["ln1"][0:3, 0:1],
                    scalar2=None, op0=ALU.mult)
    vnop()
    v.tensor_copy(out=vw["xb16"][0:3, 0:128], in_=zscr[0:3, :])
    vnop()
    t_ = Sr.t
    t_.transpose(out=ps["pt"][:, 0:3], in_=vw["xb16"][0:3, 0:128], identity=idslice(0, 3))
    v = Sr.v
    v.tensor_copy(out=ap("condh"), in_=ps["pt"][:, 0:3])
    vnop()
    _rot_pass(Sr, sb, ps, p_sb, z_sb)
    for tc in range(NCH):
        zc = z_sb[:, 128 * tc:128 * (tc + 1)]
        v = Sr.v
        v.tensor_tensor(out=ap("tt1"), in0=zc, in1=zc, op=ALU.mult)
        v.tensor_reduce(out=ap("ln1"), in_=ap("tt1"), axis=AX, op=ALU.add)
        vnop()
        s_ = Sr.s
        s_.activation(out=ap("ln2"), in_=ap("ln1"), func=AF.Ln, scale=1.0 / 128.0,
                      bias=sb["ceps6"][:, 0:1])
        snop()
        s_ = Sr.s
        s_.activation(out=ap("ln1"), in_=ap("ln2"), func=AF.Exp, scale=-0.5)
        snop()
        v = Sr.v
        v.tensor_scalar(out=ap("upk"), in0=zc, scalar1=sb["ln1"][:, 0:1],
                        scalar2=None, op0=ALU.mult)
        vnop()
        t_ = Sr.t
        t_.transpose(out=pt[:, 0:128], in_=ap("upk"), identity=ident)
        v = Sr.v
        v.tensor_copy(out=sb["cndf"][:, 128 * tc:128 * (tc + 1)], in_=pt[:, 0:128])
        vnop()

    _ffn(Sr, sb, ps, d, dr, qd, vw)

    for sl in range(4):
        _p4_slab(Sr, sb, ps, d, dr, sl, vw)
        _p5_slab(Sr, sb, ps, d, dr, sl, vw)

    _p7(Sr, sb, ps, d, dr)

    _p8(Sr, sb, ps, d, dr, y_out, vw)


def _rot_table(Sr, sb, psrc, dst, phase):
    v = Sr.v
    v.tensor_scalar(out=sb["tt2"][:, 0:64], in0=psrc, scalar1=phase, scalar2=None, op0=ALU.add)
    v.engine_nop(); v.engine_nop()
    v.tensor_scalar(out=sb["tt3"][:, 0:64], in0=sb["tt2"][:, 0:64],
                    scalar1=1.0 / TWO_PI, scalar2=None, op0=ALU.mult)
    v.engine_nop(); v.engine_nop()
    v.memset(sb["tt1"][:, 0:64], 0.0)
    accp, cur = [sb["tt1"], sb["tt4"]], 0
    for k in range(1, 12):
        v.scalar_tensor_tensor(out=accp[1 - cur][:, 0:64], in0=sb["tt3"][:, 0:64],
                               scalar=float(k), in1=accp[cur][:, 0:64],
                               op0=ALU.is_ge, op1=ALU.add)
        v.engine_nop()
        cur = 1 - cur
    fl = accp[cur]
    v.scalar_tensor_tensor(out=sb["tt3"][:, 0:64], in0=fl[:, 0:64], scalar=-TWO_PI,
                           in1=sb["tt2"][:, 0:64], op0=ALU.mult, op1=ALU.add)
    v.engine_nop(); v.engine_nop()
    v.tensor_scalar(out=sb["tt1"][:, 0:64], in0=sb["tt3"][:, 0:64], scalar1=PI,
                    scalar2=-TWO_PI, op0=ALU.is_ge, op1=ALU.mult)
    v.engine_nop(); v.engine_nop()
    v.tensor_tensor(out=sb["tt2"][:, 0:64], in0=sb["tt3"][:, 0:64],
                    in1=sb["tt1"][:, 0:64], op=ALU.add)
    v.engine_nop(); v.engine_nop()
    s_ = Sr.s
    s_.activation(out=sb[dst][:, :], in_=sb["tt2"][:, 0:64], func=AF.Sin)
    s_.nop(); s_.nop()


def _ln(Sr, sb, src, dst, P):
    v = Sr.v
    v.tensor_reduce(out=sb["ln1"][0:P, :], in_=src, axis=AX, op=ALU.add)
    v.engine_nop(); v.engine_nop()
    v.tensor_scalar(out=sb["ln2"][0:P, :], in0=sb["ln1"][0:P, :], scalar1=1.0 / 1024.0,
                    scalar2=None, op0=ALU.mult)
    v.engine_nop(); v.engine_nop()
    v.tensor_scalar(out=sb["xrow"][0:P, :], in0=src, scalar1=sb["ln2"][0:P, 0:1],
                    scalar2=None, op0=ALU.subtract)
    v.engine_nop(); v.engine_nop()
    v.tensor_tensor(out=sb["convo"][0:P, :], in0=sb["xrow"][0:P, 0:512],
                    in1=sb["xrow"][0:P, 0:512], op=ALU.mult)
    v.tensor_tensor(out=sb["hid"][0:P, :], in0=sb["xrow"][0:P, 512:1024],
                    in1=sb["xrow"][0:P, 512:1024], op=ALU.mult)
    v.tensor_reduce(out=sb["sc3"][0:P, :], in_=sb["convo"][0:P, :], axis=AX, op=ALU.add)
    v.tensor_reduce(out=sb["sc4"][0:P, :], in_=sb["hid"][0:P, :], axis=AX, op=ALU.add)
    v.engine_nop(); v.engine_nop()
    v.tensor_tensor(out=sb["ln1"][0:P, :], in0=sb["sc3"][0:P, :], in1=sb["sc4"][0:P, :],
                    op=ALU.add)
    v.engine_nop(); v.engine_nop()
    s_ = Sr.s
    s_.activation(out=sb["ln2"][0:P, :], in_=sb["ln1"][0:P, :], func=AF.Ln,
                  scale=1.0 / 1024.0, bias=sb["ceps5"][0:P, 0:1])
    s_.nop(); s_.nop()
    s_ = Sr.s
    s_.activation(out=sb["ln1"][0:P, :], in_=sb["ln2"][0:P, :], func=AF.Exp, scale=-0.5)
    s_.nop(); s_.nop()
    v = Sr.v
    v.tensor_scalar(out=dst, in0=sb["xrow"][0:P, :], scalar1=sb["ln1"][0:P, 0:1],
                    scalar2=None, op0=ALU.mult)
    v.engine_nop(); v.engine_nop()


def _tr_to(Sr, sb, ps, src_bf, dst, tc, ndc):
    for dc in range(ndc):
        t_ = Sr.t
        t_.transpose(out=ps["pt"][:, 0:128], in_=src_bf[:, 128 * dc:128 * (dc + 1)],
                     identity=sb["ident"][:, :])
        v = Sr.v
        v.tensor_copy(out=dst[:, 2048 * dc + 128 * tc:2048 * dc + 128 * (tc + 1)],
                      in_=ps["pt"][:, 0:128])


def _rot_pass(Sr, sb, ps, p_sb, z_sb):
    pf = ps["pf"][:, :]
    for tc in range(NCH):
        pcv = p_sb[:, 128 * tc:128 * (tc + 1)]
        re, im = pcv[:, 0:64], pcv[:, 64:128]
        v = Sr.v
        v.tensor_tensor(out=sb["tt1"][:, 0:64], in0=sb["tabc"][:, :], in1=re, op=ALU.mult)
        v.tensor_tensor(out=sb["tt2"][:, 0:64], in0=sb["tabs"][:, :], in1=im, op=ALU.mult)
        v.tensor_tensor(out=sb["upk"][:, 0:64], in0=sb["tt1"][:, 0:64],
                        in1=sb["tt2"][:, 0:64], op=ALU.add)
        v.tensor_tensor(out=sb["tt1"][:, 0:64], in0=sb["tabc"][:, :], in1=im, op=ALU.mult)
        v.tensor_tensor(out=sb["tt2"][:, 0:64], in0=sb["tabs"][:, :], in1=re, op=ALU.mult)
        v.tensor_tensor(out=sb["upk"][:, 64:128], in0=sb["tt1"][:, 0:64],
                        in1=sb["tt2"][:, 0:64], op=ALU.subtract)
        t_ = Sr.t
        t_.matmul(out=pf[:, 0:128], lhsT=sb["trioneB"][:, :], rhs=sb["upk"][:, :],
                  start=True, stop=False)
        t_.matmul(out=pf[:, 0:128], lhsT=sb["ones1f"][0:1, 0:128], rhs=sb["z0row"][:, :],
                  start=False, stop=True)
        v = Sr.v
        v.tensor_tensor(out=sb["tt1"][:, 0:64], in0=sb["tabc"][:, :], in1=pf[:, 0:64], op=ALU.mult)
        v.tensor_tensor(out=sb["tt2"][:, 0:64], in0=sb["tabs"][:, :], in1=pf[:, 64:128], op=ALU.mult)
        v.tensor_tensor(out=z_sb[:, 128 * tc:128 * tc + 64], in0=sb["tt1"][:, 0:64],
                        in1=sb["tt2"][:, 0:64], op=ALU.subtract)
        v.tensor_tensor(out=sb["tt1"][:, 0:64], in0=sb["tabs"][:, :], in1=pf[:, 0:64], op=ALU.mult)
        v.tensor_tensor(out=sb["tt2"][:, 0:64], in0=sb["tabc"][:, :], in1=pf[:, 64:128], op=ALU.mult)
        v.tensor_tensor(out=z_sb[:, 128 * tc + 64:128 * (tc + 1)], in0=sb["tt1"][:, 0:64],
                        in1=sb["tt2"][:, 0:64], op=ALU.add)
        t_ = Sr.t
        t_.matmul(out=pf[0:1, 128:256], lhsT=sb["e127"][:, 0:1],
                  rhs=z_sb[:, 128 * tc:128 * (tc + 1)], start=True, stop=True)
        v = Sr.v
        v.tensor_copy(out=sb["z0row"][:, :], in_=pf[0:1, 128:256])
        v.engine_nop(); v.engine_nop()


def _ffn(Sr, sb, ps, d, dr, qdict, vw):
    pa, pb, pf = ps["pa"][:, :], ps["pb"][:, :], ps["pf"][:, :]
    bigB, bigC, bigE = sb["bigB"][:, :], sb["bigC"][:, :], sb["bigE"][:, :]

    def vnop(k=2):
        v = Sr.v
        for _ in range(k):
            v.engine_nop()

    for (src, dst, R, Cw) in ((d["gw"], dr["gwq"], I_FFN, D),
                              (d["uw"], dr["uwq"], I_FFN, D),
                              (d["dw"], dr["dwq"], D, I_FFN)):
        v = Sr.v
        v.memset(sb["sc1"][:, :], 0.0)
        vnop()
        accp, cur = [sb["sc1"], sb["sc2"]], 0
        for r in range(R // 128):
            for p_ in range(Cw // 1024):
                Sr.dma(sb["xrow"][:, :], src[128 * r:128 * (r + 1), 1024 * p_:1024 * (p_ + 1)])
                v = Sr.v
                v.tensor_reduce(out=sb["sc3"][:, :], in_=sb["xrow"][:, :], axis=AX,
                                op=ALU.max, apply_absolute_value=True)
                vnop()
                v.tensor_tensor(out=accp[1 - cur][:, :], in0=accp[cur][:, :],
                                in1=sb["sc3"][:, :], op=ALU.max)
                vnop()
                cur = 1 - cur
        t_ = Sr.t
        t_.transpose(out=ps["pe"][0:1, 0:128], in_=accp[cur][:, 0:1], identity=sb["ident_f"][:, :])
        v = Sr.v
        v.tensor_reduce(out=sb["sc3"][0:1, :], in_=ps["pe"][0:1, 0:128], axis=AX, op=ALU.max)
        vnop()
        v.reciprocal(out=sb["sc4"][0:1, :], in_=sb["sc3"][0:1, :])
        vnop()
        v.tensor_scalar(out=sb["sc3"][0:1, :], in0=sb["sc4"][0:1, :], scalar1=2688.0,
                        scalar2=None, op0=ALU.mult)
        vnop()
        t_ = Sr.t
        t_.matmul(out=ps["pe"][:, 0:1], lhsT=sb["ones1f"][0:1, 0:128],
                  rhs=sb["sc3"][0:1, 0:1], start=True, stop=True)
        v = Sr.v
        v.tensor_scalar(out=sb["gs6w"][:, :], in0=ps["pe"][:, 0:1], scalar1=1.0 / 6.0,
                        scalar2=None, op0=ALU.mult)
        vnop()
        v.tensor_copy(out=sb["sc4"][:, :], in_=ps["pe"][:, 0:1])
        vnop()
        v.reciprocal(out=sb["gsiw"][:, :], in_=sb["sc4"][:, :])
        vnop()
        for r in range(R // 128):
            for p_ in range(Cw // 1024):
                Sr.dma(sb["xrow"][:, :], src[128 * r:128 * (r + 1), 1024 * p_:1024 * (p_ + 1)])
                emit_quant(Sr, sb["xrow"][:, :], vw["xb16"][:, :], 1024, qdict,
                           sb["gs6w"][:, 0:1], sb["gsiw"][:, 0:1], bf16_s=False)
                for sub in range(8):
                    t_ = Sr.t
                    t_.transpose(out=ps["pt"][:, 0:128],
                                 in_=vw["xb16"][:, 128 * sub:128 * (sub + 1)],
                                 identity=sb["ident"][:, :])
                    v = Sr.v
                    v.tensor_copy(out=sb["wslab"][:, 128 * sub:128 * (sub + 1)],
                                  in_=ps["pt"][:, 0:128])
                if Cw == D:
                    outap = dst.rearrange("(dc p) i -> p dc i", p=128)[:, :, 128 * r:128 * (r + 1)]
                else:
                    outap = dst.rearrange("(ic p) o -> p ic o", p=128)[:, 8 * p_:8 * (p_ + 1),
                                                                      128 * r:128 * (r + 1)]
                Sr.dma(outap, sb["wslab"][:, :].rearrange("p (s c) -> p s c", c=128))
    for islab in range(8):
        for dc in range(8):
            Sr.dma(bigE[:, 512 * dc:512 * (dc + 1)],
                   dr["gwq"][128 * dc:128 * (dc + 1), 512 * islab:512 * (islab + 1)])
            Sr.dma(bigE[:, 4096 + 512 * dc:4096 + 512 * (dc + 1)],
                   dr["uwq"][128 * dc:128 * (dc + 1), 512 * islab:512 * (islab + 1)])
        for tc in range(NCH):
            t_ = Sr.t
            for dc in range(8):
                t_.matmul(out=pa, lhsT=bigB[:, 2048 * dc + 128 * tc:2048 * dc + 128 * (tc + 1)],
                          rhs=bigE[:, 512 * dc:512 * (dc + 1)],
                          start=(dc == 0), stop=(dc == 7))
            for dc in range(8):
                t_.matmul(out=pb, lhsT=bigB[:, 2048 * dc + 128 * tc:2048 * dc + 128 * (tc + 1)],
                          rhs=bigE[:, 4096 + 512 * dc:4096 + 512 * (dc + 1)],
                          start=(dc == 0), stop=(dc == 7))
            s_ = Sr.s
            s_.activation(out=sb["hid"][:, :], in_=pa, func=AF.Silu)
            v = Sr.v
            v.tensor_tensor(out=sb["convo"][:, :], in0=sb["hid"][:, :], in1=pb, op=ALU.mult)
            emit_quant(Sr, sb["convo"][:, :], vw["convb"][:, :], 512, qdict,
                       sb["gs6d"][:, 0:1], sb["gsid"][:, 0:1], bf16_s=True)
            Sr.dma(dr["hq"][128 * tc:128 * (tc + 1), 512 * islab:512 * (islab + 1)],
                   vw["convb"][:, :])
    bigD = sb["bigD"][:, :]
    for ic in range(32):
        Sr.dma(bigC[:, 512 * ic:512 * (ic + 1)],
               dr["dwq"][128 * ic:128 * (ic + 1), 0:512])
        Sr.dma(bigD[:, 512 * ic:512 * (ic + 1)],
               dr["dwq"][128 * ic:128 * (ic + 1), 512:1024])
    for tc in range(NCH):
        Sr.dma(bigE[:, 0:4096], dr["hq"][128 * tc:128 * (tc + 1), :])
        for sub in range(32):
            t_ = Sr.t
            t_.transpose(out=ps["pt"][:, 0:128], in_=bigE[:, 128 * sub:128 * (sub + 1)],
                         identity=sb["ident"][:, :])
            v = Sr.v
            v.tensor_copy(out=bigE[:, 4096 + 128 * sub:4096 + 128 * (sub + 1)],
                          in_=ps["pt"][:, 0:128])
        for js in range(2):
            t_ = Sr.t
            wsrc = bigC if js == 0 else bigD
            for ic in range(32):
                t_.matmul(out=pa, lhsT=bigE[:, 4096 + 128 * ic:4096 + 128 * (ic + 1)],
                          rhs=wsrc[:, 512 * ic:512 * (ic + 1)],
                          start=(ic == 0), stop=(ic == 31))
            s_ = Sr.s
            s_.activation(out=sb["wslab"][:, 0:512], in_=pa, func=AF.Silu)
            Sr.dma(dr["gffn"][128 * tc:128 * (tc + 1), 512 * js:512 * (js + 1)],
                   sb["wslab"][:, 0:512])


CCDEF = [(cc, 128 * cc, 128) for cc in range(24)] + [(24, 3072, 32)] + \
        [(25 + g, 3104 + 128 * g, 128) for g in range(8)]


def _p4_slab(Sr, sb, ps, d, dr, sl, vw):
    pa, pb, pc, pd = ps["pa"][:, :], ps["pb"][:, :], ps["pc"][:, :], ps["pd"][:, :]
    bigA, bigB, bigC, bigD = (sb[k][:, :] for k in ("bigA", "bigB", "bigC", "bigD"))

    def vnop(k=2):
        v = Sr.v
        for _ in range(k):
            v.engine_nop()

    for (cc, c0, csz) in CCDEF:
        Sr.dma(sb["wslab"][:, 0:8 * csz].rearrange("p (dc c) -> p dc c", c=csz),
               d["ctrl_W"].rearrange("(dc p) c -> p dc c", p=128)[:, :, c0:c0 + csz],
               cast=True)
        t_ = Sr.t
        for dc in range(8):
            t_.matmul(out=pa[0:csz, 0:512], lhsT=sb["wslab"][:, csz * dc:csz * (dc + 1)],
                      rhs=bigA[:, 2048 * dc + 512 * sl:2048 * dc + 512 * (sl + 1)],
                      start=(dc == 0), stop=(dc == 7))
        if cc < 24:
            kind, fc = cc // 8, cc % 8
            t_ = Sr.t
            t_.matmul(out=pb[0:128, 0:512], lhsT=sb["deltW"][:, 128 * cc:128 * (cc + 1)],
                      rhs=sb["cndf"][:, 512 * sl:512 * (sl + 1)], start=True, stop=True)
            v = Sr.v
            v.tensor_scalar(out=sb["convo"][:, :], in0=pa[0:128, 0:512],
                            scalar1=sb["cbias"][:, cc:cc + 1], scalar2=None, op0=ALU.add)
            v.scalar_tensor_tensor(out=sb["pre"][:, 3:515], in0=pb[0:128, 0:512],
                                   scalar=sb["rampq"][:, cc:cc + 1], in1=sb["convo"][:, :],
                                   op0=ALU.mult, op1=ALU.add)
            if sl == 0:
                t_ = Sr.t
                for dc in range(8):
                    t_.matmul(out=pc[0:csz, 0:3], lhsT=sb["wslab"][:, csz * dc:csz * (dc + 1)],
                              rhs=sb["xnhf"][:, 3 * dc:3 * dc + 3],
                              start=(dc == 0), stop=(dc == 7))
                t_.matmul(out=pd[0:128, 0:3], lhsT=sb["deltW"][:, 128 * cc:128 * (cc + 1)],
                          rhs=sb["condh"][:, :], start=True, stop=True)
                v = Sr.v
                v.tensor_scalar(out=sb["tt1"][:, 0:3], in0=pc[0:128, 0:3],
                                scalar1=sb["cbias"][:, cc:cc + 1], scalar2=None, op0=ALU.add)
                vnop()
                v.scalar_tensor_tensor(out=sb["tt2"][:, 0:3], in0=pd[0:128, 0:3],
                                       scalar=sb["rampq"][:, cc:cc + 1], in1=sb["tt1"][:, 0:3],
                                       op0=ALU.mult, op1=ALU.add)
                vnop()
                v.tensor_scalar(out=sb["tails"][:, 3 * cc:3 * cc + 3], in0=sb["tt2"][:, 0:3],
                                scalar1=sb["maskc"][:, 0:1], scalar2=None, op0=ALU.mult)
                vnop()
            v = Sr.v
            v.tensor_copy(out=sb["pre"][:, 0:3], in_=sb["tails"][:, 3 * cc:3 * cc + 3])
            vnop()
            base = (kind * 8 + fc) * 4
            v.tensor_scalar(out=sb["hid"][:, :], in0=sb["pre"][:, 0:512],
                            scalar1=sb["convw"][:, base:base + 1], scalar2=None, op0=ALU.mult)
            v.scalar_tensor_tensor(out=sb["convo"][:, :], in0=sb["pre"][:, 1:513],
                                   scalar=sb["convw"][:, base + 1:base + 2],
                                   in1=sb["hid"][:, :], op0=ALU.mult, op1=ALU.add)
            v.scalar_tensor_tensor(out=sb["hid"][:, :], in0=sb["pre"][:, 2:514],
                                   scalar=sb["convw"][:, base + 2:base + 3],
                                   in1=sb["convo"][:, :], op0=ALU.mult, op1=ALU.add)
            v.scalar_tensor_tensor(out=sb["convo"][:, :], in0=sb["pre"][:, 3:515],
                                   scalar=sb["convw"][:, base + 3:base + 4],
                                   in1=sb["hid"][:, :], op0=ALU.mult, op1=ALU.add)
            v.tensor_copy(out=sb["tails"][:, 3 * cc:3 * cc + 3], in_=sb["pre"][:, 512:515])
            vnop()
            s_ = Sr.s
            s_.activation(out=vw["convb"][:, :], in_=sb["convo"][:, :], func=AF.Silu)
            v = Sr.v
            if kind == 0:
                dst = bigB[:, 2048 * fc + 512 * sl:2048 * fc + 512 * (sl + 1)]
            elif kind == 1:
                dst = bigD[:, 512 * fc:512 * (fc + 1)]
            else:
                dst = bigD[:, 4096 + 512 * fc:4096 + 512 * (fc + 1)]
            v.tensor_copy(out=dst, in_=vw["convb"][:, :])
        elif cc == 24:
            t_ = Sr.t
            for dc in range(8):
                t_.matmul(out=pc[0:16, 0:512], lhsT=sb["wslab"][:, 32 * dc:32 * dc + 16],
                          rhs=bigA[:, 2048 * dc + 512 * sl:2048 * dc + 512 * (sl + 1)],
                          start=(dc == 0), stop=(dc == 7))
            for dc in range(8):
                t_.matmul(out=pd[0:16, 0:512], lhsT=sb["wslab"][:, 32 * dc + 16:32 * dc + 32],
                          rhs=bigA[:, 2048 * dc + 512 * sl:2048 * dc + 512 * (sl + 1)],
                          start=(dc == 0), stop=(dc == 7))
            t_.matmul(out=pb[0:16, 0:512], lhsT=sb["deltW"][:, 3072:3088],
                      rhs=sb["cndf"][:, 512 * sl:512 * (sl + 1)], start=True, stop=True)
            v = Sr.v
            v.tensor_scalar(out=sb["convo"][0:16, :], in0=pc[0:16, 0:512],
                            scalar1=sb["cbias"][0:16, 24:25], scalar2=None, op0=ALU.add)
            v.tensor_scalar(out=sb["hid"][0:16, :], in0=pd[0:16, 0:512],
                            scalar1=sb["cbias"][0:16, 25:26], scalar2=None, op0=ALU.add)
            vnop()
            v.scalar_tensor_tensor(out=sb["pre"][0:16, 0:512], in0=pb[0:16, 0:512],
                                   scalar=sb["rampb"][:, 0:1], in1=sb["hid"][0:16, :],
                                   op0=ALU.mult, op1=ALU.add)
            vnop()
            for tcs in range(4):
                t_ = Sr.t
                t_.transpose(out=ps["pe"][:, 0:16],
                             in_=sb["convo"][0:16, 128 * tcs:128 * (tcs + 1)],
                             identity=sb["ident_f"][0:16, 0:16])
                v = Sr.v
                v.tensor_copy(out=sb["dbt"][:, 32 * tcs:32 * tcs + 16], in_=ps["pe"][:, 0:16])
                vnop()
                t_ = Sr.t
                t_.transpose(out=ps["pe"][:, 16:32],
                             in_=sb["pre"][0:16, 128 * tcs:128 * (tcs + 1)],
                             identity=sb["ident_f"][0:16, 0:16])
                v = Sr.v
                v.tensor_copy(out=sb["dbt"][:, 32 * tcs + 16:32 * tcs + 32],
                              in_=ps["pe"][:, 16:32])
                vnop()
        else:
            gc = cc - 25
            v = Sr.v
            v.tensor_scalar(out=sb["convo"][:, :], in0=pa[0:128, 0:512],
                            scalar1=sb["cbias"][:, cc + 1:cc + 2], scalar2=None, op0=ALU.add)
            s_ = Sr.s
            s_.activation(out=vw["convb"][:, :], in_=sb["convo"][:, :], func=AF.Silu)
            for tcs in range(4):
                t_ = Sr.t
                t_.transpose(out=ps["pt"][:, 0:128],
                             in_=vw["convb"][:, 128 * tcs:128 * (tcs + 1)],
                             identity=sb["ident"][:, :])
                v = Sr.v
                v.tensor_copy(out=bigC[:, 1024 * (4 * sl + tcs) + 128 * gc:
                                       1024 * (4 * sl + tcs) + 128 * (gc + 1)],
                              in_=ps["pt"][:, 0:128])


def _p5_slab(Sr, sb, ps, d, dr, sl, vw):
    pa, pb, pc, pd, pe, pf = (ps[k][:, :] for k in ("pa", "pb", "pc", "pd", "pe", "pf"))
    pt = ps["pt"][:, :]
    bigB, bigD = sb["bigB"][:, :], sb["bigD"][:, :]

    def vnop(k=2):
        v = Sr.v
        for _ in range(k):
            v.engine_nop()

    def snop():
        s_ = Sr.s
        s_.nop(); s_.nop()

    for tcs in range(4):
        tcg = 4 * sl + tcs
        dec = sb["dbt"][:, 32 * tcs:32 * tcs + 16]
        bet = sb["dbt"][:, 32 * tcs + 16:32 * tcs + 32]
        s_ = Sr.s
        s_.activation(out=sb["t16a"][:, :], in_=dec, func=AF.Exp, scale=-1.0)
        snop()
        s_ = Sr.s
        s_.activation(out=sb["t16b"][:, :], in_=sb["t16a"][:, :], func=AF.Ln,
                      bias=sb["cone"][:, 0:1])
        snop()
        v = Sr.v
        v.tensor_scalar(out=sb["lng"][:, :], in0=sb["t16b"][:, :], scalar1=-1.0,
                        scalar2=None, op0=ALU.mult)
        vnop()
        s_ = Sr.s
        s_.activation(out=sb["t16a"][:, :], in_=bet, func=AF.Exp, scale=-1.0)
        snop()
        s_ = Sr.s
        s_.activation(out=sb["t16b"][:, :], in_=sb["t16a"][:, :], func=AF.Ln,
                      bias=sb["cone"][:, 0:1])
        snop()
        v = Sr.v
        v.tensor_scalar(out=sb["lnb"][:, :], in0=sb["t16b"][:, :], scalar1=-1.0,
                        scalar2=None, op0=ALU.mult)
        vnop()
        s_ = Sr.s
        s_.activation(out=sb["beta"][:, :], in_=sb["lnb"][:, :], func=AF.Exp)
        snop()
        t_ = Sr.t
        t_.matmul(out=pf[:, 0:16], lhsT=sb["trioneF"][:, :], rhs=sb["lng"][:, :],
                  start=True, stop=True)
        v = Sr.v
        v.tensor_copy(out=sb["lam"][:, :], in_=pf[:, 0:16])
        vnop()
        for fc in range(8):
            qt = bigB[:, 2048 * fc + 128 * tcg:2048 * fc + 128 * (tcg + 1)]
            kt = bigD[:, 512 * fc + 128 * tcs:512 * fc + 128 * (tcs + 1)]
            v = Sr.v
            v.tensor_tensor(out=sb["sq16"][:, :], in0=qt, in1=qt, op=ALU.mult)
            t_ = Sr.t
            t_.matmul(out=pf[:, 16 + 2 * fc:17 + 2 * fc], lhsT=sb["sq16"][0:64, :],
                      rhs=sb["onescb"][0:64, 0:1], start=True, stop=True)
            t_.matmul(out=pf[:, 17 + 2 * fc:18 + 2 * fc], lhsT=sb["sq16"][64:128, :],
                      rhs=sb["onescb"][64:128, 0:1], start=True, stop=True,
                      tile_position=(64, 0))
            v = Sr.v
            v.tensor_tensor(out=sb["sq16"][:, :], in0=kt, in1=kt, op=ALU.mult)
            t_ = Sr.t
            t_.matmul(out=pf[:, 32 + 2 * fc:33 + 2 * fc], lhsT=sb["sq16"][0:64, :],
                      rhs=sb["onescb"][0:64, 0:1], start=True, stop=True)
            t_.matmul(out=pf[:, 33 + 2 * fc:34 + 2 * fc], lhsT=sb["sq16"][64:128, :],
                      rhs=sb["onescb"][64:128, 0:1], start=True, stop=True,
                      tile_position=(64, 0))
        s_ = Sr.s
        s_.activation(out=sb["l1q"][:, :], in_=pf[:, 16:32], func=AF.Ln,
                      bias=sb["ceps6"][:, 0:1])
        snop()
        s_ = Sr.s
        s_.activation(out=sb["l1k"][:, :], in_=pf[:, 32:48], func=AF.Ln,
                      bias=sb["ceps6"][:, 0:1])
        snop()
        v = Sr.v
        v.scalar_tensor_tensor(out=sb["t16a"][:, :], in0=sb["l1k"][:, :], scalar=-0.5,
                               in1=sb["lam"][:, :], op0=ALU.mult, op1=ALU.add)
        vnop()
        v.tensor_tensor(out=sb["rowM"][:, :], in0=sb["t16a"][:, :], in1=sb["lnb"][:, :],
                        op=ALU.add)
        vnop()
        v.scalar_tensor_tensor(out=sb["rowA"][:, :], in0=sb["l1q"][:, :], scalar=-0.5,
                               in1=sb["lam"][:, :], op0=ALU.mult, op1=ALU.add)
        vnop()
        v.scalar_tensor_tensor(out=sb["t16b"][:, :], in0=sb["l1k"][:, :], scalar=0.5,
                               in1=sb["lam"][:, :], op0=ALU.mult, op1=ALU.add)
        vnop()
        v.tensor_scalar(out=sb["colv"][:, :], in0=sb["t16b"][:, :], scalar1=-1.0,
                        scalar2=None, op0=ALU.mult)
        vnop()
        s_ = Sr.s
        s_.activation(out=sb["coefk"][:, :], in_=sb["rowM"][:, :], func=AF.Exp)
        snop()
        s_ = Sr.s
        s_.activation(out=sb["eAall"][:, 16 * tcg:16 * (tcg + 1)], in_=sb["rowA"][:, :],
                      func=AF.Exp)
        snop()
        t_ = Sr.t
        t_.matmul(out=pe[0:1, 0:16], lhsT=sb["onescf"][:, 0:1], rhs=sb["lng"][:, :],
                  start=True, stop=True)
        v = Sr.v
        v.tensor_copy(out=sb["lamC"][:, :], in_=pe[0:1, 0:16])
        vnop()
        t_ = Sr.t
        t_.matmul(out=pf[:, 48:64], lhsT=sb["ones1f"][0:1, 0:128], rhs=sb["lamC"][:, :],
                  start=True, stop=True)
        v = Sr.v
        v.tensor_tensor(out=sb["t16a"][:, :], in0=pf[:, 48:64], in1=sb["colv"][:, :],
                        op=ALU.add)
        vnop()
        s_ = Sr.s
        s_.activation(out=sb["coefp"][:, :], in_=sb["t16a"][:, :], func=AF.Exp)
        snop()
        s_ = Sr.s
        s_.activation(out=sb["gCc"][:, :], in_=ps["pf"][0:64, 48:64], func=AF.Exp)
        snop()
        for g4 in range(4):
            for hs in range(4):
                h = 4 * g4 + hs
                half, fc = h % 2, h // 2
                qf_t = bigB[64 * half:64 * half + 64,
                            2048 * fc + 128 * tcg:2048 * fc + 128 * (tcg + 1)]
                kf_t = bigD[64 * half:64 * half + 64,
                            512 * fc + 128 * tcs:512 * fc + 128 * (tcs + 1)]
                vf_t = bigD[64 * half:64 * half + 64,
                            4096 + 512 * fc + 128 * tcs:4096 + 512 * fc + 128 * (tcs + 1)]
                tp = (64 * half, 0)
                t_ = Sr.t
                t_.matmul(out=pc[:, 0:128], lhsT=kf_t, rhs=kf_t, start=True, stop=True,
                          tile_position=tp)
                t_.matmul(out=pc[:, 128:256], lhsT=kf_t, rhs=qf_t, start=True, stop=True,
                          tile_position=tp)
                v = Sr.v
                v.tensor_copy(out=sb["pk2a"][:, 1:2], in_=sb["colv"][:, h:h + 1])
                v.tensor_copy(out=sb["pk2b"][:, 0:1], in_=sb["rowM"][:, h:h + 1])
                v.tensor_copy(out=sb["pk2c"][:, 0:1], in_=sb["rowA"][:, h:h + 1])
                vnop()
                t_ = Sr.t
                t_.transpose(out=pb[0:2, 0:128], in_=sb["pk2a"][:, :],
                             identity=sb["ident_f"][:, :])
                t_.transpose(out=pb[0:2, 128:256], in_=sb["pk2b"][:, :],
                             identity=sb["ident_f"][:, :])
                t_.transpose(out=pb[0:2, 256:384], in_=sb["pk2c"][:, :],
                             identity=sb["ident_f"][:, :])
                v = Sr.v
                v.tensor_copy(out=sb["stk2"][:, :], in_=pb[0:2, 0:128])
                v.tensor_copy(out=sb["stk2r"][:, :], in_=pb[0:2, 128:256])
                v.tensor_copy(out=sb["stk2q"][:, :], in_=pb[0:2, 256:384])
                vnop()
                t_ = Sr.t
                t_.matmul(out=pe[:, 0:128], lhsT=sb["stk2"][0:2, :], rhs=sb["stk2r"][0:2, :],
                          start=True, stop=True)
                t_.matmul(out=pe[:, 128:256], lhsT=sb["stk2"][0:2, :], rhs=sb["stk2q"][0:2, :],
                          start=True, stop=True)
                v = Sr.v
                v.scalar_tensor_tensor(out=sb["tt1"][:, :], in0=pe[:, 0:128], scalar=1.0,
                                       in1=sb["maskM"][:, :], op0=ALU.mult, op1=ALU.add)
                s_ = Sr.s
                s_.activation(out=sb["tt2"][:, :], in_=sb["tt1"][:, :], func=AF.Exp)
                v = Sr.v
                v.tensor_scalar(out=sb["tt3"][:, :], in0=sb["tt2"][:, :], scalar1=-1.0,
                                scalar2=None, op0=ALU.mult)
                v.tensor_tensor(out=sb["Pa"][:, 128 * hs:128 * (hs + 1)],
                                in0=sb["tt3"][:, :], in1=pc[:, 0:128], op=ALU.mult)
                v.scalar_tensor_tensor(out=sb["tt1"][:, :], in0=pe[:, 128:256], scalar=1.0,
                                       in1=sb["maskA"][:, :], op0=ALU.mult, op1=ALU.add)
                s_ = Sr.s
                s_.activation(out=sb["tt2"][:, :], in_=sb["tt1"][:, :], func=AF.Exp)
                v = Sr.v
                v.tensor_tensor(out=sb["AqS"][:, 128 * hs:128 * (hs + 1)],
                                in0=sb["tt2"][:, :], in1=pc[:, 128:256], op=ALU.mult)
                t_ = Sr.t
                t_.transpose(out=pt[:, 0:64], in_=vf_t,
                             identity=sb["ident"][64 * half:64 * half + 64,
                                                  64 * half:64 * half + 64],
                             tile_position=tp)
                v = Sr.v
                v.tensor_scalar(out=sb["Ua"][:, 128 * hs:128 * hs + 64], in0=pt[:, 0:64],
                                scalar1=sb["beta"][:, h:h + 1], scalar2=None, op0=ALU.mult)
                t_ = Sr.t
                t_.transpose(out=pt[:, 64:128], in_=kf_t,
                             identity=sb["ident"][64 * half:64 * half + 64,
                                                  64 * half:64 * half + 64],
                             tile_position=tp)
                v = Sr.v
                v.tensor_scalar(out=sb["Ua"][:, 128 * hs + 64:128 * (hs + 1)],
                                in0=pt[:, 64:128], scalar1=sb["coefk"][:, h:h + 1],
                                scalar2=None, op0=ALU.mult)
                v.tensor_scalar(out=sb["kpt"][:, 64 * hs:64 * (hs + 1)], in0=pt[:, 64:128],
                                scalar1=sb["coefp"][:, h:h + 1], scalar2=None, op0=ALU.mult)
            Pcur, Pnxt = sb["Pa"][:, :], sb["Pb"][:, :]
            Ucur, Unxt = sb["Ua"][:, :], sb["Ub"][:, :]
            for lvl in range(7):
                t_ = Sr.t
                for hs in range(4):
                    t_.matmul(out=pd[:, 128 * hs:128 * (hs + 1)],
                              lhsT=Pcur[:, 128 * hs:128 * (hs + 1)],
                              rhs=Ucur[:, 128 * hs:128 * (hs + 1)], start=True, stop=True)
                v = Sr.v
                v.tensor_tensor(out=Unxt, in0=pd, in1=Ucur, op=ALU.add)
                Ucur, Unxt = Unxt, Ucur
                if lvl < 6:
                    t_ = Sr.t
                    for hs in range(4):
                        t_.transpose(out=pt[:, 128 * hs:128 * (hs + 1)],
                                     in_=Pcur[:, 128 * hs:128 * (hs + 1)],
                                     identity=sb["ident"][:, :])
                    v = Sr.v
                    v.tensor_copy(out=sb["PT"][:, :], in_=pt)
                    t_ = Sr.t
                    for hs in range(4):
                        t_.matmul(out=pe[:, 128 * hs:128 * (hs + 1)],
                                  lhsT=sb["PT"][:, 128 * hs:128 * (hs + 1)],
                                  rhs=Pcur[:, 128 * hs:128 * (hs + 1)], start=True, stop=True)
                    v = Sr.v
                    v.tensor_copy(out=Pnxt, in_=pe)
                    Pcur, Pnxt = Pnxt, Pcur
            Ufin = Ucur
            t_ = Sr.t
            for hs in range(4):
                t_.matmul(out=pd[:, 64 * hs:64 * (hs + 1)],
                          lhsT=sb["AqS"][:, 128 * hs:128 * (hs + 1)],
                          rhs=Ufin[:, 128 * hs:128 * hs + 64], start=True, stop=True)
            for hs in range(4):
                t_.matmul(out=pe[0:64, 128 * hs:128 * (hs + 1)],
                          lhsT=Ufin[:, 128 * hs + 64:128 * (hs + 1)],
                          rhs=sb["AqS"][:, 128 * hs:128 * (hs + 1)], start=True, stop=True)
            for hs in range(4):
                t_.matmul(out=pf[0:64, 64 * hs:64 * (hs + 1)],
                          lhsT=sb["kpt"][:, 64 * hs:64 * (hs + 1)],
                          rhs=Ufin[:, 128 * hs:128 * hs + 64], start=True, stop=True)
            for hs in range(4):
                t_.matmul(out=pc[0:64, 64 * hs:64 * (hs + 1)],
                          lhsT=Ufin[:, 128 * hs + 64:128 * (hs + 1)],
                          rhs=sb["kpt"][:, 64 * hs:64 * (hs + 1)], start=True, stop=True)
            v = Sr.v
            v.tensor_copy(out=vw["convb"][:, 0:256], in_=pd[:, 0:256])
            Sr.dma(dr["oloc"][128 * tcg:128 * (tcg + 1), 256 * g4:256 * (g4 + 1)],
                   vw["convb"][:, 0:256])
            v = Sr.v
            v.tensor_scalar(out=sb["pk64"][:, :], in0=pe[0:64, 0:512], scalar1=-1.0,
                            scalar2=None, op0=ALU.mult)
            Sr.dma(dr["gneg"][64 * (16 * tcg + 4 * g4):64 * (16 * tcg + 4 * g4) + 256, :]
                   .rearrange("(hh r) c -> r hh c", hh=4),
                   sb["pk64"][:, :].rearrange("p (hh c) -> p hh c", hh=4))
            for hs in range(4):
                h = 4 * g4 + hs
                v = Sr.v
                v.tensor_scalar(out=sb["t64"][:, :], in0=sb["i64b"][:, :],
                                scalar1=sb["gCc"][:, h:h + 1], scalar2=None, op0=ALU.mult)
                vnop()
                v.scalar_tensor_tensor(out=sb["pk64"][:, 128 * hs:128 * hs + 64],
                                       in0=pc[0:64, 64 * hs:64 * (hs + 1)], scalar=-1.0,
                                       in1=sb["t64"][:, :], op0=ALU.mult, op1=ALU.add)
                vnop()
                v.tensor_copy(out=sb["pk64"][:, 128 * hs + 64:128 * (hs + 1)],
                              in_=pf[0:64, 64 * hs:64 * (hs + 1)])
                vnop()
            Sr.dma(dr["ptqc"][64 * (16 * tcg + 4 * g4):64 * (16 * tcg + 4 * g4) + 256, :]
                   .rearrange("(hh r) c -> r hh c", hh=4),
                   sb["pk64"][:, :].rearrange("p (hh c) -> p hh c", hh=4))


def _p7(Sr, sb, ps, d, dr):
    pd, pe, pf = ps["pd"][:, :], ps["pe"][:, :], ps["pf"][:, :]
    pt = ps["pt"][:, :]
    bigA, bigB, bigC, bigE = (sb[k][:, :] for k in ("bigA", "bigB", "bigC", "bigE"))
    ovl = {"ptq": sb["cndf"][:, 0:1024], "gng": sb["cndf"][:, 1024:2048],
           "Sc1": sb["Ua"][:, :], "Sc2": sb["Ub"][:, :],
           "Tc1": sb["Pa"][:, :], "Tc2": sb["Pb"][:, :],
           "Strue": sb["PT"][:, :], "Srec": sb["AqS"][:, :],
           "S16p": sb["pre"][:, 0:512]}

    def vnop(k=2):
        v = Sr.v
        for _ in range(k):
            v.engine_nop()

    def snop():
        s_ = Sr.s
        s_.nop(); s_.nop()

    def load_ptq(ch):
        Sr.dma(ovl["ptq"].rearrange("p (j c) -> p j c", j=8),
               dr["ptqc"][64 * 16 * ch:64 * 16 * (ch + 1), :]
               .rearrange("(j two r) c -> (two r) j c", j=8, two=2))

    v = Sr.v
    v.memset(ovl["Sc1"], 0.0)
    vnop()
    Scur, Snxt = ovl["Sc1"], ovl["Sc2"]
    for ch in range(NCH):
        load_ptq(ch)
        t_ = Sr.t
        for h in range(H):
            two, j = h % 2, h // 2
            t_.matmul(out=pd[64 * two:64 * two + 64, 64 * j:64 * (j + 1)],
                      lhsT=ovl["ptq"][64 * two:64 * two + 64, 128 * j:128 * j + 64],
                      rhs=Scur[64 * two:64 * two + 64, 64 * j:64 * (j + 1)],
                      start=True, stop=True, tile_position=(64 * two, 64 * two))
        for h in range(H):
            two, j = h % 2, h // 2
            v = Sr.v
            v.tensor_tensor(out=Snxt[64 * two:64 * two + 64, 64 * j:64 * (j + 1)],
                            in0=pd[64 * two:64 * two + 64, 64 * j:64 * (j + 1)],
                            in1=ovl["ptq"][64 * two:64 * two + 64, 128 * j + 64:128 * (j + 1)],
                            op=ALU.add)
            if ch == NCH - 1:
                v.tensor_tensor(out=ovl["S16p"][64 * two:64 * two + 64, 64 * j:64 * (j + 1)],
                                in0=pd[64 * two:64 * two + 64, 64 * j:64 * (j + 1)],
                                in1=ovl["ptq"][64 * two:64 * two + 64, 128 * j + 64:128 * (j + 1)],
                                op=ALU.add)
        v = Sr.v
        v.tensor_copy(out=bigE[:, 512 * ch:512 * (ch + 1)], in_=Snxt)
        Scur, Snxt = Snxt, Scur
    for two in range(2):
        Sr.dma(dr["cc2i"].ap().rearrange("(j two r) v -> r j two v", j=8, two=2)
               [:, :, two:two + 1, :],
               ovl["S16p"][64 * two:64 * two + 64, :]
               .rearrange("r (j v) -> r j v", j=8).unsqueeze(2))
    Sr.cc("AllGather", [dr["cc2i"].ap().opt()], [dr["cc2o"].ap().opt()])
    v = Sr.v
    v.memset(sb["convo"][:, :], 0.0)
    vnop()
    accp, cur = [sb["convo"], sb["hid"]], 0
    for r8 in range(8):
        Sr.dma(ovl["S16p"].rearrange("p (j v) -> p j v", j=8),
               dr["cc2o"].ap()[1024 * r8:1024 * (r8 + 1), :]
               .rearrange("(j two r) v -> (two r) j v", j=8, two=2))
        v = Sr.v
        v.scalar_tensor_tensor(out=accp[1 - cur][:, :], in0=ovl["S16p"],
                               scalar=sb["selw"][:, r8:r8 + 1], in1=accp[cur][:, :],
                               op0=ALU.mult, op1=ALU.add)
        cur = 1 - cur
    v = Sr.v
    v.tensor_copy(out=ovl["Srec"], in_=accp[cur][:, :])
    vnop()
    Tcur, Tnxt = ovl["Tc1"], ovl["Tc2"]
    v = Sr.v
    v.tensor_copy(out=Tcur, in_=ovl["Srec"])
    vnop()
    for ch in range(NCH):
        if ch == 0:
            v = Sr.v
            v.tensor_copy(out=ovl["Strue"], in_=ovl["Srec"])
            vnop()
        else:
            v = Sr.v
            v.tensor_tensor(out=ovl["Strue"], in0=bigE[:, 512 * (ch - 1):512 * ch],
                            in1=Tcur, op=ALU.add)
            vnop()
        Sr.dma(ovl["gng"].rearrange("p (j c) -> p j c", j=8),
               dr["gneg"][64 * 16 * ch:64 * 16 * (ch + 1), :]
               .rearrange("(j two r) c -> (two r) j c", j=8, two=2))
        Sr.dma(sb["wslab"][:, :], dr["oloc"][128 * ch:128 * (ch + 1), :])
        for wave in range(2):
            t_ = Sr.t
            for hh in range(8):
                h = 8 * wave + hh
                two, j = h % 2, h // 2
                St = ovl["Strue"][64 * two:64 * two + 64, 64 * j:64 * (j + 1)]
                qf_t = bigB[64 * two:64 * two + 64,
                            2048 * j + 128 * ch:2048 * j + 128 * (ch + 1)]
                t_.matmul(out=pd[:, 64 * hh:64 * (hh + 1)], lhsT=qf_t, rhs=St,
                          start=True, stop=True, tile_position=(64 * two, 0))
                t_.matmul(out=pe[:, 64 * hh:64 * (hh + 1)],
                          lhsT=ovl["gng"][64 * two:64 * two + 64, 128 * j:128 * (j + 1)],
                          rhs=St, start=True, stop=True, tile_position=(64 * two, 0))
            for hh in range(8):
                h = 8 * wave + hh
                v = Sr.v
                v.tensor_scalar(out=sb["opre"][:, :], in0=pd[:, 64 * hh:64 * (hh + 1)],
                                scalar1=sb["eAall"][:, 16 * ch + h:16 * ch + h + 1],
                                scalar2=None, op0=ALU.mult)
                v.tensor_tensor(out=sb["opre2"][:, :], in0=sb["opre"][:, :],
                                in1=pe[:, 64 * hh:64 * (hh + 1)], op=ALU.add)
                v.tensor_tensor(out=sb["opre"][:, :], in0=sb["opre2"][:, :],
                                in1=sb["wslab"][:, 64 * h:64 * (h + 1)], op=ALU.add)
                v.tensor_tensor(out=sb["tt1"][:, 0:64], in0=sb["opre"][:, :],
                                in1=sb["opre"][:, :], op=ALU.mult)
                v.tensor_reduce(out=sb["ln1"][:, :], in_=sb["tt1"][:, 0:64], axis=AX,
                                op=ALU.add)
                vnop()
                s_ = Sr.s
                s_.activation(out=sb["ln2"][:, :], in_=sb["ln1"][:, :], func=AF.Ln,
                              scale=1.0 / 64.0, bias=sb["ceps6"][:, 0:1])
                snop()
                s_ = Sr.s
                s_.activation(out=sb["ln1"][:, :], in_=sb["ln2"][:, :], func=AF.Exp,
                              scale=-0.5)
                snop()
                v = Sr.v
                v.tensor_scalar(out=sb["opre2"][:, :], in0=sb["opre"][:, :],
                                scalar1=sb["ln1"][:, 0:1], scalar2=None, op0=ALU.mult)
                vnop()
                v.tensor_tensor(out=sb["ot64"][:, :], in0=sb["opre2"][:, :],
                                in1=bigC[:, 1024 * ch + 64 * h:1024 * ch + 64 * (h + 1)],
                                op=ALU.mult)
                vnop()
                two = h % 2
                t_ = Sr.t
                t_.transpose(out=pt[64 * two:64 * two + 64, 0:128], in_=sb["ot64"][:, :],
                             identity=sb["ident"][:, :], tile_position=(0, 64 * two))
                v = Sr.v
                v.tensor_scalar(out=bigA[64 * two:64 * two + 64,
                                         2048 * (h // 2) + 128 * ch:
                                         2048 * (h // 2) + 128 * (ch + 1)],
                                in0=pt[64 * two:64 * two + 64, 0:128],
                                scalar1=sb["onwc"][64 * two:64 * two + 64, 0:1],
                                scalar2=None, op0=ALU.mult)
        load_ptq(ch)
        t_ = Sr.t
        for h in range(H):
            two, j = h % 2, h // 2
            t_.matmul(out=pf[64 * two:64 * two + 64, 64 * j:64 * (j + 1)],
                      lhsT=ovl["ptq"][64 * two:64 * two + 64, 128 * j:128 * j + 64],
                      rhs=Tcur[64 * two:64 * two + 64, 64 * j:64 * (j + 1)],
                      start=True, stop=True, tile_position=(64 * two, 64 * two))
        v = Sr.v
        v.tensor_copy(out=Tnxt, in_=pf)
        Tcur, Tnxt = Tnxt, Tcur


def _p8(Sr, sb, ps, d, dr, y_out, vw):
    pa = ps["pa"][:, :]
    bigA, bigD = sb["bigA"][:, :], sb["bigD"][:, :]

    def vnop(k=2):
        v = Sr.v
        for _ in range(k):
            v.engine_nop()

    for jc in range(8):
        Sr.dma(sb["xrow"][:, :], d["Wo"][128 * jc:128 * (jc + 1), :])
        v = Sr.v
        v.tensor_copy(out=vw["xb16"][:, :], in_=sb["xrow"][:, :])
        for dc in range(8):
            t_ = Sr.t
            t_.transpose(out=ps["pt"][:, 0:128], in_=vw["xb16"][:, 128 * dc:128 * (dc + 1)],
                         identity=sb["ident"][:, :])
            v = Sr.v
            v.tensor_copy(out=bigD[:, 1024 * dc + 128 * jc:1024 * dc + 128 * (jc + 1)],
                          in_=ps["pt"][:, 0:128])
    for tc in range(NCH):
        Sr.dma(sb["xrow"][:, :], d["x_sh"][HALO + 128 * tc:HALO + 128 * (tc + 1), :])
        Sr.dma(sb["wslab"][:, :], dr["gffn"][128 * tc:128 * (tc + 1), :])
        for js in range(2):
            t_ = Sr.t
            for dc in range(8):
                t_.matmul(out=pa, lhsT=bigA[:, 2048 * dc + 128 * tc:2048 * dc + 128 * (tc + 1)],
                          rhs=bigD[:, 1024 * dc + 512 * js:1024 * dc + 512 * (js + 1)],
                          start=(dc == 0), stop=False)
            t_.matmul(out=pa, lhsT=sb["ones1b"][0:1, 0:128], rhs=sb["bo16"][0:1, 512 * js:512 * (js + 1)],
                      start=False, stop=True)
            v = Sr.v
            v.tensor_tensor(out=sb["convo"][:, :], in0=pa, in1=sb["wslab"][:, 512 * js:512 * (js + 1)],
                            op=ALU.mult)
            v.tensor_tensor(out=sb["xt"][:, 512 * js:512 * (js + 1)], in0=sb["convo"][:, :],
                            in1=sb["xrow"][:, 512 * js:512 * (js + 1)], op=ALU.add)
        Sr.dma(y_out[128 * tc:128 * (tc + 1), :], sb["xt"][:, :])


def kernel(**inputs):
    if "nc" not in _CACHE:
        _CACHE["nc"] = build_nc()
    nc = _CACHE["nc"]
    maps = _host_maps(inputs)
    res = run_bass_kernel_spmd(nc, maps, core_ids=list(range(8)))
    out = np.zeros((B, S, D), np.float32)
    for i in range(8):
        b_, h_ = i % 4, i // 4
        out[b_, h_ * T:(h_ + 1) * T] = res.results[i]["y_sh"]
    return out


# version 10
# speedup vs baseline: 1.1227x; 1.1227x over previous
import contextlib
import numpy as np
import ml_dtypes
import concourse.bass as bass
import concourse.mybir as mybir
from concourse.bass_utils import run_bass_kernel_spmd

F32 = mybir.dt.float32
BF16 = mybir.dt.bfloat16
ALU = mybir.AluOpType
AF = mybir.ActivationFunctionType
AX = mybir.AxisListType.X

B, S, D, H, DH, BW = 4, 4096, 1024, 16, 64, 128
T, HALO, C = 2048, 3, 128
NCH = T // C
I_FFN = 4096
PI = float(np.pi)
TWO_PI = float(2 * np.pi)
THRQ = [0.25 ** 2, 0.75 ** 2, 1.25 ** 2, 1.75 ** 2, 2.5 ** 2, 3.5 ** 2, 5.0 ** 2]

_CACHE = {}


def _host_maps(inputs):
    x = np.ascontiguousarray(np.asarray(inputs["L0_attn_hidden"], np.float32))
    perm = np.concatenate([np.arange(0, BW, 2), np.arange(1, BW, 2)])
    cond_in_W = np.ascontiguousarray(np.asarray(inputs["L0_attn_cond_in_W"], np.float32)[:, perm])
    delta_W = np.ascontiguousarray(np.asarray(inputs["L0_attn_delta_W"], np.float32)[perm, :])
    cnw = np.ascontiguousarray(
        np.asarray(inputs["L0_attn_cond_norm_w"], np.float32)[perm].reshape(BW, 1))
    theta = np.asarray(inputs["L0_attn_rot_theta"], np.float32).reshape(1, 64)
    ctrl_W = np.ascontiguousarray(np.asarray(inputs["L0_attn_ctrl_W"], np.float32))
    ctrl_b = np.asarray(inputs["L0_attn_ctrl_b"], np.float32).reshape(-1)
    cb = np.zeros((128, 34), np.float32)
    for cc in range(24):
        cb[:, cc] = ctrl_b[128 * cc:128 * cc + 128]
    cb[0:16, 24] = ctrl_b[3072:3088]
    cb[0:16, 25] = ctrl_b[3088:3104]
    for g in range(8):
        cb[:, 26 + g] = ctrl_b[3104 + 128 * g:3104 + 128 * g + 128]
    ramp = np.asarray(inputs["L0_attn_qkv_ramp_logit"], np.float32)
    rampt = np.zeros((128, 24), np.float32)
    for kind in range(3):
        for fc in range(8):
            rampt[:, kind * 8 + fc] = ramp[kind, 128 * fc:128 * fc + 128]
    beta_ramp = np.asarray(inputs["L0_attn_beta_ramp_logit"], np.float32).reshape(H, 1)
    convw = np.zeros((128, 96), np.float32)
    for kind, key in enumerate(("q", "k", "v")):
        w = np.asarray(inputs[f"L0_attn_{key}_conv_w"], np.float32)
        for fc in range(8):
            convw[:, (kind * 8 + fc) * 4:(kind * 8 + fc) * 4 + 4] = \
                w[128 * fc:128 * fc + 128]
    onw2 = np.concatenate([np.asarray(inputs["L0_attn_out_norm_w"], np.float32).reshape(DH, 1)] * 2)
    Wo = np.ascontiguousarray(np.asarray(inputs["L0_attn_out_proj_W"], np.float32))
    bo = np.asarray(inputs["L0_attn_out_proj_b"], np.float32).reshape(1, D)
    gw = np.ascontiguousarray(np.asarray(inputs["L0_ffn_gate_w"], np.float32))
    uw = np.ascontiguousarray(np.asarray(inputs["L0_ffn_up_w"], np.float32))
    dw = np.ascontiguousarray(np.asarray(inputs["L0_ffn_down_w"], np.float32))
    s_in = float(np.asarray(inputs["L0_ffn_s_in"]).reshape(-1)[0])
    s_in_d = float(np.asarray(inputs["L0_ffn_s_in_down"]).reshape(-1)[0])

    cst = {}
    cst["ident"] = np.eye(128).astype(ml_dtypes.bfloat16)
    cst["ident_f"] = np.eye(128).astype(np.float32)
    tri = np.tril(np.ones((128, 128), np.float32))
    cst["trioneB"] = np.ascontiguousarray(tri.T).astype(ml_dtypes.bfloat16)
    cst["trioneF"] = np.ascontiguousarray(tri.T).astype(np.float32)
    jj, tt2 = np.meshgrid(np.arange(128), np.arange(128), indexing="ij")
    cst["maskMc"] = np.where(jj < tt2, 0.0, -1e30).astype(np.float32)
    cst["maskAc"] = np.where(jj <= tt2, 0.0, -1e30).astype(np.float32)
    cst["iotar"] = np.arange(1, 129, dtype=np.float32).reshape(1, 128)
    cst["ones1f"] = np.ones((1, 512), np.float32)
    cst["ones1b"] = np.ones((1, 512), ml_dtypes.bfloat16)
    cst["onescb"] = np.ones((128, 1), ml_dtypes.bfloat16)
    cst["i64c"] = np.eye(64).astype(np.float32)
    e127 = np.zeros((128, 1), np.float32); e127[127, 0] = 1.0
    cst["e127"] = e127
    e2 = np.zeros((128, 1), np.float32); e2[2, 0] = 1.0
    cst["e2"] = e2

    maps = []
    for i in range(8):
        b_, h_ = i % 4, i // 4
        xs = np.zeros((T + HALO, D), np.float32)
        if h_ == 0:
            xs[HALO:] = x[b_, 0:T]
        else:
            xs[:] = x[b_, h_ * T - HALO:h_ * T + T]
        sel = np.zeros((128, 8), np.float32)
        if h_ == 1:
            sel[:, i - 4] = 1.0
        m = dict(
            x_sh=xs, sel=sel,
            mask=np.full((128, 1), 1.0 if h_ == 1 else 0.0, np.float32),
            cond_in_W=cond_in_W, delta_W=delta_W, cnw=cnw, theta=theta,
            ctrl_W=ctrl_W, cbias=cb, rampq=rampt, beta_ramp=beta_ramp,
            convw=convw, onw2=onw2, Wo=Wo, bo=bo, gw=gw, uw=uw, dw=dw,
            s_in=np.full((128, 1), s_in, np.float32),
            s_in_d=np.full((128, 1), s_in_d, np.float32),
        )
        m.update(cst)
        maps.append(m)
    return maps


_DT_SIZE = {}
for _nm, _sz in (("float32", 4), ("int32", 4), ("uint32", 4), ("bfloat16", 2),
                 ("float16", 2), ("int16", 2), ("uint16", 2), ("int8", 1),
                 ("uint8", 1), ("float8_e4m3", 1), ("float8_e5m2", 1)):
    _d = getattr(mybir.dt, _nm, None)
    if _d is not None:
        _DT_SIZE[_d] = _sz

_READ, _WRITE = 0, 1
import os
SERIAL_FROM = int(os.environ.get('SERIAL_FROM', '100'))
_CAP = 48


class Ser:

    QW = 4

    def __init__(self, nc, E):
        self.nc = nc
        self.E = E
        self.eng = {"T": nc.tensor, "V": nc.vector, "S": nc.scalar,
                    "Q": nc.sync, "G": nc.gpsimd}
        self.sems = {k: E(nc.semaphore("sem" + k)) for k in ("T", "V", "S")}
        self.cnt = {k: 0 for k in ("T", "V", "S")}
        for base in ("Q", "G"):
            for j in range(self.QW):
                k = base + str(j)
                self.sems[k] = E(nc.semaphore("sem" + k))
                self.cnt[k] = 0
        self.qidx = {"Q": 0, "G": 0}
        self.wm = {k: {} for k in self.eng}
        self.writes = {}
        self.reads = {}
        self.cn = 0
        self.n = 0
        self.last = None
        self.phase = 0

    def _box(self, ap):
        t = ap.tensor
        name = t.name
        esz = _DT_SIZE[t.dtype]
        pat = list(ap.ap)
        tname = type(t).__name__
        if tname == "DRamTensorHandle":
            lo = ap.offset * esz
            span = esz
            for s, n_ in pat:
                span += (n_ - 1) * abs(s) * esz
            return (name, 0, 1, lo, lo + span)
        shp = list(t.shape)
        re_ = 1
        for d in shp[1:]:
            re_ *= d
        rowb = re_ * esz
        p0 = ap.offset // re_ + getattr(t, "base_partition", 0)
        c0 = (ap.offset % re_) * esz
        dp = 0
        cspan = esz
        ok = True
        for s, n_ in pat:
            if n_ <= 1 or s == 0:
                continue
            if s % re_ == 0:
                dp += (n_ - 1) * (s // re_)
            else:
                cspan += (n_ - 1) * abs(s) * esz
        if c0 + cspan > rowb:
            return (name, 0, 128, 0, rowb)
        return (name, p0, p0 + dp + 1, c0, c0 + cspan)

    @staticmethod
    def _overlap(box, r):
        return box[1] < r[1] and r[0] < box[2] and box[3] < r[3] and r[2] < box[4]

    def _deps_from(self, recs, box, deps):
        for r in recs:
            if self._overlap(box, r):
                e, s_ = r[4], r[5]
                if deps.get(e, 0) < s_:
                    deps[e] = s_

    def _record(self, store, box, eng, seq):
        recs = store.setdefault(box[0], [])
        nb = [box[1], box[2], box[3], box[4], eng, seq]
        keep = [r for r in recs
                if not (nb[0] <= r[0] and r[1] <= nb[1]
                        and nb[2] <= r[2] and r[3] <= nb[3])]
        keep.append(nb)
        if len(keep) > _CAP:
            byeng = {}
            for r in keep:
                byeng[r[4]] = max(byeng.get(r[4], 0), r[5])
            keep = [[0, 128, 0, 1 << 30, e, s_] for e, s_ in byeng.items()]
        store[box[0]] = keep

    def _emit_waits(self, E_, deps, self_raw):
        q = self.eng[E_]
        wmE = self.wm[E_]
        for F, v in deps.items():
            if F == E_ and not self_raw:
                continue
            if wmE.get(F, 0) < v:
                q.wait_ge(self.sems[F] if F in self.sems else F, v)
                wmE[F] = v

    def _issue(self, E_, rd_boxes, wr_boxes):
        raw_deps, war_deps = {}, {}
        for b in rd_boxes:
            self._deps_from(self.writes.get(b[0], ()), b, raw_deps)
        for b in wr_boxes:
            self._deps_from(self.writes.get(b[0], ()), b, war_deps)
            self._deps_from(self.reads.get(b[0], ()), b, war_deps)
        deps = dict(war_deps)
        for e, s_ in raw_deps.items():
            if deps.get(e, 0) < s_:
                deps[e] = s_
        if self.phase >= SERIAL_FROM:
            for F, c in self.cnt.items():
                if c > 0 and deps.get(F, 0) < c:
                    deps[F] = c
        self._emit_waits(E_, deps, True)

    def _finish(self, E_, rd_boxes, wr_boxes, seq):
        for b in rd_boxes:
            self._record(self.reads, b, E_, seq)
        for b in wr_boxes:
            self._record(self.writes, b, E_, seq)

    def _collect(self, a, k, wr_boxes, rd_boxes):
        wout = k.get("out", None)
        if wout is not None:
            wr_boxes.append(self._box(wout))
        for key, val in k.items():
            if key == "out" or val is None:
                continue
            if hasattr(val, "tensor") and hasattr(val, "ap"):
                rd_boxes.append(self._box(val))
        for val in a:
            if hasattr(val, "tensor") and hasattr(val, "ap"):
                rd_boxes.append(self._box(val))

    def _wrap(self, e, E_):
        ser = self

        class W:
            def __getattr__(_s, name):
                if name in ("nop", "engine_nop"):
                    return lambda *a, **k: None
                fn = getattr(e, name)
                if name in ("wait_ge", "then_inc"):
                    return fn

                def call(*a, **k):
                    wr, rd = [], []
                    if name == "memset":
                        wr.append(ser._box(a[0]))
                        ser._collect(a[1:], k, wr, rd)
                    else:
                        ser._collect(a, k, wr, rd)
                    if name == "matmul" and not k.get("start", True):
                        rd.extend(wr)
                    ser._issue(E_, rd, wr)
                    r = fn(*a, **k)
                    r.then_inc(ser.sems[E_], 1)
                    ser.cnt[E_] += 1
                    ser._finish(E_, rd, wr, ser.cnt[E_])
                    return r
                return call
        return W()

    @property
    def v(self):
        return self._wrap(self.eng["V"], "V")

    @property
    def s(self):
        return self._wrap(self.eng["S"], "S")

    @property
    def t(self):
        return self._wrap(self.eng["T"], "T")

    def dma(self, out, in_, cast=False):
        E_ = "G" if cast else "Q"
        e = self.eng[E_]
        key = E_ + str(self.qidx[E_] % self.QW)
        self.qidx[E_] += 1
        wr = [self._box(out)]
        rd = [self._box(in_)]
        self._issue(E_, rd, wr)
        if self.cnt[key] > 0 and self.wm[E_].get(key, 0) < self.cnt[key]:
            e.wait_ge(self.sems[key], self.cnt[key])
            self.wm[E_][key] = self.cnt[key]
        e.dma_start(out=out, in_=in_).then_inc(self.sems[key], 16)
        self.cnt[key] += 16
        self._finish(key, rd, wr, self.cnt[key])

    def cc(self, kind, ins, outs):
        e = self.eng["G"]
        csem = self.E(self.nc.semaphore("ccsem%d" % self.cn))
        self.cn += 1
        wr = [self._box(o) for o in outs]
        rd = [self._box(i) for i in ins]
        self._issue("G", rd, wr)
        e.collective_compute(kind, ALU.bypass, replica_groups=[list(range(8))],
                             ins=[i.opt() for i in ins],
                             outs=[o.opt() for o in outs]).then_inc(csem, 1)
        self.sems[csem] = csem
        self.cnt[csem] = 1
        self._finish(csem, rd, wr, 1)

    def _emit_waits_all(self, q):
        for F, c in self.cnt.items():
            if c > 0:
                q.wait_ge(self.sems[F], c)


def emit_quant(Sr, at, outq, N, q, gs6, gsi, bf16_s):
    v = Sr.v
    nb = N // 16

    def nop():
        v.engine_nop()

    av_all = at.rearrange("p (b k) -> p b k", k=16)
    v.tensor_reduce(out=q["b0"][:, :nb], in_=av_all, axis=AX, op=ALU.max,
                    apply_absolute_value=True)
    nop(); nop()
    v.tensor_scalar(out=q["b1"][:, :nb], in0=q["b0"][:, :nb], scalar1=gs6,
                    scalar2=448.0, op0=ALU.mult, op1=ALU.min)
    nop()
    v.memset(q["b2"][:, :nb], 2.0 ** -9)
    pp, cur = [q["b2"], q["b3"]], 0
    for i in range(-5, 9):
        v.tensor_scalar(out=q["b4"][:, :nb], in0=q["b1"][:, :nb],
                        scalar1=float(2.0 ** i), scalar2=1.0, op0=ALU.is_ge, op1=ALU.add)
        nop()
        v.tensor_tensor(out=pp[1 - cur][:, :nb], in0=pp[cur][:, :nb],
                        in1=q["b4"][:, :nb], op=ALU.mult)
        nop()
        cur = 1 - cur
    q2, spare = pp[cur], pp[1 - cur]
    v.reciprocal(out=q["b4"][:, :nb], in_=q2[:, :nb])
    nop()
    v.tensor_mul(out=spare[:, :nb], in0=q["b4"][:, :nb], in1=q["b1"][:, :nb])
    nop()
    v.memset(q["b1"][:, :nb], 0.0)
    pp2, cur2 = [q["b1"], q["b4"]], 0
    for j in range(16):
        v.scalar_tensor_tensor(out=pp2[1 - cur2][:, :nb], in0=spare[:, :nb],
                               scalar=j + 0.5, in1=pp2[cur2][:, :nb],
                               op0=ALU.is_ge, op1=ALU.add)
        nop()
        cur2 = 1 - cur2
    acc = pp2[cur2]
    v.tensor_tensor(out=spare[:, :nb], in0=acc[:, :nb], in1=q2[:, :nb], op=ALU.mult)
    nop()
    v.tensor_scalar(out=q["b0"][:, :nb], in0=spare[:, :nb], scalar1=gsi,
                    scalar2=None, op0=ALU.mult)
    nop()
    v.tensor_scalar(out=q["b4"][:, :nb], in0=q["b0"][:, :nb], scalar1=1e-30,
                    scalar2=None, op0=ALU.max)
    nop()
    v.reciprocal(out=q["b1"][:, :nb], in_=q["b4"][:, :nb])
    nop(); nop()
    for hh in range(N // 512):
        c0, nb0 = 512 * hh, 32 * hh
        av = at[:, c0:c0 + 512].rearrange("p (b k) -> p b k", k=16)
        uv = q["u"].rearrange("p (b k) -> p b k", k=16)
        v.tensor_tensor(out=uv, in0=av,
                        in1=q["b1"][:, nb0:nb0 + 32].unsqueeze(2).broadcast_to([128, 32, 16]),
                        op=ALU.mult)
        sdst = q["sb"] if bf16_s else q["s"]
        v.tensor_tensor(out=sdst, in0=q["u"], in1=q["u"], op=ALU.mult)
        v.tensor_scalar(out=q["qA"], in0=sdst, scalar1=THRQ[0], scalar2=None, op0=ALU.is_ge)
        pa_, cu = [q["qA"], q["qC"]], 0
        for thr in THRQ[1:4]:
            v.scalar_tensor_tensor(out=pa_[1 - cu], in0=sdst, scalar=thr,
                                   in1=pa_[cu], op0=ALU.is_ge, op1=ALU.add)
            cu = 1 - cu
        qa, sparea = pa_[cu], pa_[1 - cu]
        v.tensor_scalar(out=q["qB"], in0=sdst, scalar1=THRQ[4], scalar2=None, op0=ALU.is_ge)
        pb_, cu2 = [q["qB"], q["qD"]], 0
        for thr in (THRQ[5], THRQ[6], THRQ[6]):
            v.scalar_tensor_tensor(out=pb_[1 - cu2], in0=sdst, scalar=thr,
                                   in1=pb_[cu2], op0=ALU.is_ge, op1=ALU.add)
            cu2 = 1 - cu2
        qb = pb_[cu2]
        v.scalar_tensor_tensor(out=sparea, in0=qa, scalar=0.5, in1=qb,
                               op0=ALU.mult, op1=ALU.add)
        v.tensor_scalar(out=q["qB"], in0=q["u"], scalar1=0.0, scalar2=2.0,
                        op0=ALU.is_ge, op1=ALU.mult)
        qv = q["qd"].rearrange("p (b k) -> p b k", k=16)
        v.tensor_tensor(out=qv, in0=sparea.rearrange("p (b k) -> p b k", k=16),
                        in1=q["b0"][:, nb0:nb0 + 32].unsqueeze(2).broadcast_to([128, 32, 16]),
                        op=ALU.mult)
        v.tensor_scalar(out=q["qD"], in0=q["qB"], scalar1=1.0, scalar2=None, op0=ALU.subtract)
        nop()
        v.tensor_tensor(out=outq[:, c0:c0 + 512], in0=q["qd"], in1=q["qD"], op=ALU.mult)


def build_nc():
    nc = bass.Bass("TRN2", target_bir_lowering=False)
    stk = contextlib.ExitStack()
    E = stk.enter_context

    d = {}
    specs = dict(
        x_sh=([T + HALO, D], F32), sel=([128, 8], F32), mask=([128, 1], F32),
        cond_in_W=([D, BW], F32), delta_W=([BW, 3 * D + H], F32), cnw=([BW, 1], F32),
        theta=([1, 64], F32), ctrl_W=([D, 4128], F32), cbias=([128, 34], F32),
        rampq=([128, 24], F32), beta_ramp=([H, 1], F32), convw=([128, 96], F32),
        onw2=([128, 1], F32), Wo=([D, D], F32), bo=([1, D], F32),
        gw=([I_FFN, D], F32), uw=([I_FFN, D], F32), dw=([D, I_FFN], F32),
        s_in=([128, 1], F32), s_in_d=([128, 1], F32),
        ident=([128, 128], BF16), ident_f=([128, 128], F32),
        trioneB=([128, 128], BF16), trioneF=([128, 128], F32),
        maskMc=([128, 128], F32), maskAc=([128, 128], F32), iotar=([1, 128], F32),
        ones1f=([1, 512], F32), ones1b=([1, 512], BF16), onescb=([128, 1], BF16),
        i64c=([64, 64], F32), e127=([128, 1], F32), e2=([128, 1], F32),
    )
    for k, (shp, dt) in specs.items():
        d[k] = nc.dram_tensor(k, shp, dt, kind="ExternalInput").ap()
    y_out = nc.dram_tensor("y_sh", [T, D], F32, kind="ExternalOutput").ap()

    dr = dict(
        oloc=nc.dram_tensor("oloc_d", [T, D], BF16).ap(),
        gneg=nc.dram_tensor("gneg_d", [NCH * H * 64, 128], BF16).ap(),
        ptqc=nc.dram_tensor("ptqc_d", [NCH * H * 64, 128], BF16).ap(),
        gffn=nc.dram_tensor("gffn_d", [T, D], BF16).ap(),
        hq=nc.dram_tensor("hq_d", [T, I_FFN], BF16).ap(),
        gwq=nc.dram_tensor("gwq_d", [D, I_FFN], BF16).ap(),
        uwq=nc.dram_tensor("uwq_d", [D, I_FFN], BF16).ap(),
        dwq=nc.dram_tensor("dwq_d", [I_FFN, D], BF16).ap(),
    )
    dr["cc1i"] = nc.dram_tensor("cc1_in", [4, 128], F32)
    dr["cc1o"] = nc.dram_tensor("cc1_out", [32, 128], F32)
    dr["cc2i"] = nc.dram_tensor("cc2_in", [H * 64, 64], F32)
    dr["cc2o"] = nc.dram_tensor("cc2_out", [8 * H * 64, 64], F32)

    sb = {}

    def A(name, shape, dt):
        sb[name] = E(nc.sbuf_tensor("s_" + name, shape, dt))

    A("bigA", [128, 16384], BF16)
    A("bigB", [128, 16384], BF16)
    A("bigC", [128, 16384], BF16)
    A("bigD", [128, 16384], BF16)
    A("bigE", [128, 8192], BF16)
    A("cndf", [128, 2048], BF16)
    A("condW", [128, 1024], BF16)
    A("deltW", [128, 3088], BF16)
    A("convw", [128, 96], F32)
    A("rampq", [128, 24], F32)
    A("rampb", [16, 1], F32)
    A("cbias", [128, 34], F32)
    A("tabc", [128, 64], F32)
    A("tabs", [128, 64], F32)
    A("wslab", [128, 1024], BF16)
    A("xt", [128, 1024], F32)
    A("xrow", [128, 1024], F32)
    A("selw", [128, 8], F32)
    A("maskc", [128, 1], F32)
    A("gs6", [128, 1], F32); A("gsi", [128, 1], F32)
    A("gs6d", [128, 1], F32); A("gsid", [128, 1], F32)
    A("gs6w", [128, 1], F32); A("gsiw", [128, 1], F32)
    A("sc1", [128, 1], F32); A("sc2", [128, 1], F32); A("sc3", [128, 1], F32)
    A("ceps5", [128, 1], F32); A("ceps6", [128, 1], F32); A("cone", [128, 1], F32)
    A("onescf", [128, 1], F32); A("lamC", [1, 16], F32); A("e127", [128, 1], F32)
    A("e2", [128, 1], F32)
    A("sc4", [128, 1], F32)
    A("ln1", [128, 1], F32); A("ln2", [128, 1], F32)
    A("iotar", [1, 128], F32)
    A("trioneB", [128, 128], BF16)
    A("trioneF", [128, 128], F32)
    A("ident", [128, 128], BF16)
    A("ident_f", [128, 128], F32)
    A("maskM", [128, 128], F32)
    A("maskA", [128, 128], F32)
    A("ones1f", [1, 512], F32)
    A("ones1b", [1, 512], BF16)
    A("onescb", [128, 1], BF16)
    A("i64b", [64, 64], F32)
    A("onwc", [128, 1], F32)
    A("bo16", [1, 1024], BF16)
    A("thetar", [1, 64], F32)
    A("z0row", [1, 128], F32)
    A("tt1", [128, 128], F32); A("tt2", [128, 128], F32)
    A("tt3", [128, 128], F32); A("tt4", [128, 64], F32)
    A("upk", [128, 128], BF16)
    A("haloz", [4, 128], F32); A("halo2", [4, 128], F32)
    A("condh", [128, 3], BF16)
    A("xnhf", [128, 24], BF16)
    A("pre", [128, 520], F32)
    A("tails", [128, 72], F32)
    A("convo", [128, 512], F32)
    A("hid", [128, 512], F32)
    A("dbt", [128, 128], F32)
    A("lng", [128, 16], F32); A("lnb", [128, 16], F32); A("beta", [128, 16], F32)
    A("lam", [128, 16], F32); A("l1q", [128, 16], F32); A("l1k", [128, 16], F32)
    A("rowM", [128, 16], F32); A("rowA", [128, 16], F32); A("colv", [128, 16], F32)
    A("coefk", [128, 16], F32); A("coefp", [128, 16], F32)
    A("t16a", [128, 16], F32); A("t16b", [128, 16], F32)
    A("eAall", [128, 256], F32)
    A("gCc", [64, 16], F32)
    A("pk2a", [128, 2], F32); A("pk2b", [128, 2], F32); A("pk2c", [128, 2], F32)
    A("stk2", [2, 128], F32); A("stk2r", [2, 128], F32); A("stk2q", [2, 128], F32)
    A("sq16", [128, 128], BF16)
    A("AqS", [128, 512], BF16)
    A("Ua", [128, 512], BF16); A("Ub", [128, 512], BF16)
    A("Pa", [128, 512], BF16); A("Pb", [128, 512], BF16)
    A("PT", [128, 512], BF16)
    A("kpt", [128, 256], BF16)
    A("t64", [64, 64], F32)
    A("pk64", [64, 512], BF16)
    A("opre", [128, 64], F32); A("opre2", [128, 64], F32)
    A("ot64", [128, 64], BF16)

    ps = {}
    for name, dt in (("pa", F32), ("pb", F32), ("pc", F32), ("pd", F32),
                     ("pe", F32), ("pf", F32)):
        ps[name] = E(nc.psum_tensor(name, [128, 512], dt))
    ps["pt"] = E(nc.psum_tensor("pt", [128, 512], BF16))

    blk = E(nc.Block())

    @blk.sync
    def _(sync):
        Sr = Ser(nc, E)
        _emit(nc, Sr, d, y_out, sb, ps, dr)
        Sr._emit_waits_all(nc.sync)

    stk.close()
    return nc


def _emit(nc, Sr, d, y_out, sb, ps, dr):
    pa, pb, pc, pd, pe, pf = (ps[k][:, :] for k in ("pa", "pb", "pc", "pd", "pe", "pf"))
    pt = ps["pt"][:, :]
    bigA, bigB, bigC, bigD, bigE = (sb[k][:, :] for k in
                                    ("bigA", "bigB", "bigC", "bigD", "bigE"))
    bigC_f = bigC.bitcast(F32)
    bigD_f = bigD.bitcast(F32)
    xt2v = bigD_f[:, 0:1024]
    vw = {"qu": bigD_f[:, 1024:1536], "qqd": bigD_f[:, 1536:2048],
          "qs": bigD_f[:, 2048:2560], "qqA": bigD_f[:, 2560:3072],
          "qqB": bigD_f[:, 3072:3584], "qqC": bigD_f[:, 3584:4096],
          "qqD": bigD_f[:, 4096:4608],
          "qb0": bigD_f[:, 4608:4672], "qb1": bigD_f[:, 4672:4736],
          "qb2": bigD_f[:, 4736:4800], "qb3": bigD_f[:, 4800:4864],
          "qb4": bigD_f[:, 4864:4928],
          "convb": bigD[:, 9856:10368], "xb16": bigD[:, 10368:11392]}
    p_sb = bigC_f[:, 0:2048]
    z_sb = bigC_f[:, 2048:4096]
    ident = sb["ident"][:, :]
    identf = sb["ident_f"][:, :]

    def ap(n):
        return sb[n][:, :]

    qd = dict(b0=vw["qb0"], b1=vw["qb1"], b2=vw["qb2"], b3=vw["qb3"], b4=vw["qb4"],
              u=vw["qu"], qd=vw["qqd"], s=vw["qs"],
              sb=vw["qs"].bitcast(BF16)[:, 0:512],
              qA=vw["qqA"], qB=vw["qqB"], qC=vw["qqC"], qD=vw["qqD"])

    def vnop(k=2):
        v = Sr.v
        for _ in range(k):
            v.engine_nop()

    def snop():
        s_ = Sr.s
        s_.nop(); s_.nop()

    def idslice(base, n):
        return sb["ident"][base:base + n, base:base + n]

    def idslicef(base, n):
        return sb["ident_f"][base:base + n, base:base + n]

    v = Sr.v
    v.memset(sb["ceps5"][:, :], 1e-5)
    v.memset(sb["ceps6"][:, :], 1e-6)
    v.memset(sb["cone"][:, :], 1.0)
    v.memset(sb["onescf"][:, :], 1.0)
    v.memset(sb["pk2a"][:, :], 1.0)
    v.memset(sb["pk2b"][:, :], 1.0)
    v.memset(sb["pk2c"][:, :], 1.0)
    vnop()
    for nm in ("ident", "ident_f", "trioneB", "trioneF", "iotar", "ones1f",
               "ones1b", "onescb", "convw", "cbias", "rampq"):
        Sr.dma(ap(nm), d[nm])
    Sr.dma(ap("selw"), d["sel"])
    Sr.dma(ap("e127"), d["e127"])
    Sr.dma(ap("e2"), d["e2"])
    Sr.dma(ap("maskM"), d["maskMc"])
    Sr.dma(ap("maskA"), d["maskAc"])
    Sr.dma(ap("i64b"), d["i64c"])
    Sr.dma(ap("maskc"), d["mask"])
    Sr.dma(ap("thetar"), d["theta"])
    Sr.dma(sb["onwc"][:, :], d["onw2"])
    Sr.dma(sb["condW"][:, :].rearrange("p (dc w) -> p dc w", dc=8),
           d["cond_in_W"].rearrange("(dc p) w -> p dc w", p=128), cast=True)
    Sr.dma(ap("sc1"), d["cnw"])
    for c0, w in ((0, 1024), (1024, 1024), (2048, 1024), (3072, 16)):
        Sr.dma(sb["xt"][:, 0:w], d["delta_W"][:, c0:c0 + w])
        v = Sr.v
        v.tensor_scalar(out=sb["deltW"][:, c0:c0 + w], in0=sb["xt"][:, 0:w],
                        scalar1=sb["sc1"][:, 0:1], scalar2=None, op0=ALU.mult)
        vnop()
    s_ = Sr.s
    s_.activation(out=xt2v[:, 0:24], in_=sb["rampq"][:, 0:24], func=AF.Tanh, scale=0.5)
    snop()
    v = Sr.v
    v.tensor_scalar(out=ap("rampq"), in0=xt2v[:, 0:24], scalar1=0.5,
                    scalar2=0.5, op0=ALU.mult, op1=ALU.add)
    vnop()
    Sr.dma(sb["xt"][0:16, 0:1], d["beta_ramp"])
    s_ = Sr.s
    s_.activation(out=xt2v[0:16, 0:1], in_=sb["xt"][0:16, 0:1], func=AF.Tanh, scale=0.5)
    snop()
    v = Sr.v
    v.tensor_scalar(out=ap("rampb"), in0=xt2v[0:16, 0:1], scalar1=0.5,
                    scalar2=0.5, op0=ALU.mult, op1=ALU.add)
    vnop()
    Sr.dma(ap("sc1"), d["s_in"])
    Sr.dma(ap("sc2"), d["s_in_d"])
    v = Sr.v
    v.tensor_scalar(out=ap("gs6"), in0=ap("sc1"), scalar1=1.0 / 6.0, scalar2=None, op0=ALU.mult)
    v.tensor_scalar(out=ap("gs6d"), in0=ap("sc2"), scalar1=1.0 / 6.0, scalar2=None, op0=ALU.mult)
    vnop()
    v.reciprocal(out=ap("gsi"), in_=ap("sc1"))
    vnop()
    v.reciprocal(out=ap("gsid"), in_=ap("sc2"))
    vnop()
    Sr.dma(sb["bo16"][:, :], d["bo"], cast=True)
    t_ = Sr.t
    t_.matmul(out=pf[:, 0:64], lhsT=sb["iotar"][0:1, :], rhs=ap("thetar"),
              start=True, stop=True)
    _rot_table(Sr, sb, pf[:, 0:64], "tabs", 0.0)
    t_ = Sr.t
    t_.matmul(out=pf[:, 64:128], lhsT=sb["iotar"][0:1, :], rhs=ap("thetar"),
              start=True, stop=True)
    _rot_table(Sr, sb, pf[:, 64:128], "tabc", PI / 2.0)

    Sr.phase = 10
    for tc in range(NCH):
        Sr.dma(ap("xt"), d["x_sh"][HALO + 128 * tc:HALO + 128 * (tc + 1), :])
        _ln(Sr, sb, ap("xt"), xt2v, 128)
        v = Sr.v
        v.tensor_copy(out=vw["xb16"], in_=xt2v)
        _tr_to(Sr, sb, ps, vw["xb16"], bigA, tc, 8)
        emit_quant(Sr, xt2v, vw["xb16"], 1024, qd,
                   sb["gs6"][:, 0:1], sb["gsi"][:, 0:1], bf16_s=False)
        _tr_to(Sr, sb, ps, vw["xb16"], bigB, tc, 8)
    Sr.dma(sb["xt"][0:3, :], d["x_sh"][0:HALO, :])
    _ln(Sr, sb, sb["xt"][0:3, :], sb["xt"][0:3, :], 3)
    v = Sr.v
    v.tensor_copy(out=vw["xb16"][0:3, :], in_=sb["xt"][0:3, :])
    vnop()
    for dc in range(8):
        t_ = Sr.t
        t_.transpose(out=ps["pt"][:, 0:3], in_=vw["xb16"][0:3, 128 * dc:128 * (dc + 1)],
                     identity=idslice(0, 3))
        v = Sr.v
        v.tensor_copy(out=sb["xnhf"][:, 3 * dc:3 * dc + 3], in_=ps["pt"][:, 0:3])
        vnop()

    Sr.phase = 20
    for tc in range(NCH):
        t_ = Sr.t
        for dc in range(8):
            t_.matmul(out=pf[:, 0:128],
                      lhsT=bigA[:, 2048 * dc + 128 * tc:2048 * dc + 128 * (tc + 1)],
                      rhs=sb["condW"][:, 128 * dc:128 * (dc + 1)],
                      start=(dc == 0), stop=(dc == 7))
        v = Sr.v
        v.tensor_copy(out=p_sb[:, 128 * tc:128 * (tc + 1)], in_=pf[:, 0:128])
    v = Sr.v
    v.memset(sb["z0row"][:, :], 0.0)
    vnop()
    _rot_pass(Sr, sb, ps, p_sb, z_sb)
    Sr.dma(dr["cc1i"].ap()[0:3, :], z_sb[125:128, 128 * 15:128 * 16])
    Sr.dma(dr["cc1i"].ap()[3:4, :], z_sb[127:128, 128 * 15:128 * 16])
    Sr.cc("AllGather", [dr["cc1i"].ap()], [dr["cc1o"].ap()])
    Sr.dma(sb["xt"][0:4, 0:1024].rearrange("p (r c) -> p r c", c=128),
           dr["cc1o"].ap().rearrange("(r p) c -> p r c", p=4))
    v = Sr.v
    v.memset(sb["haloz"][0:3, :], 0.0)
    vnop()
    accp, cur = [sb["haloz"], sb["halo2"]], 0
    for r in range(8):
        v.scalar_tensor_tensor(out=accp[1 - cur][0:3, :],
                               in0=sb["xt"][0:3, 128 * r:128 * (r + 1)],
                               scalar=sb["selw"][0:3, r:r + 1], in1=accp[cur][0:3, :],
                               op0=ALU.mult, op1=ALU.add)
        vnop()
        cur = 1 - cur
    zrecv = accp[cur][0:3, :]
    zscr = accp[1 - cur]
    t_ = Sr.t
    t_.matmul(out=ps["pf"][0:1, 128:256], lhsT=sb["e2"][0:3, 0:1], rhs=zrecv,
              start=True, stop=True)
    v = Sr.v
    v.tensor_copy(out=sb["z0row"][:, :], in_=ps["pf"][0:1, 128:256])
    vnop()
    v.tensor_tensor(out=zscr[0:3, :], in0=zrecv, in1=zrecv, op=ALU.mult)
    vnop()
    v.tensor_reduce(out=sb["ln1"][0:3, :], in_=zscr[0:3, :], axis=AX, op=ALU.add)
    vnop()
    s_ = Sr.s
    s_.activation(out=sb["ln2"][0:3, :], in_=sb["ln1"][0:3, :], func=AF.Ln,
                  scale=1.0 / 128.0, bias=sb["ceps6"][0:3, 0:1])
    snop()
    s_ = Sr.s
    s_.activation(out=sb["ln1"][0:3, :], in_=sb["ln2"][0:3, :], func=AF.Exp, scale=-0.5)
    snop()
    v = Sr.v
    v.tensor_scalar(out=zscr[0:3, :], in0=zrecv, scalar1=sb["ln1"][0:3, 0:1],
                    scalar2=None, op0=ALU.mult)
    vnop()
    v.tensor_copy(out=vw["xb16"][0:3, 0:128], in_=zscr[0:3, :])
    vnop()
    t_ = Sr.t
    t_.transpose(out=ps["pt"][:, 0:3], in_=vw["xb16"][0:3, 0:128], identity=idslice(0, 3))
    v = Sr.v
    v.tensor_copy(out=ap("condh"), in_=ps["pt"][:, 0:3])
    vnop()
    _rot_pass(Sr, sb, ps, p_sb, z_sb)
    for tc in range(NCH):
        zc = z_sb[:, 128 * tc:128 * (tc + 1)]
        v = Sr.v
        v.tensor_tensor(out=ap("tt1"), in0=zc, in1=zc, op=ALU.mult)
        v.tensor_reduce(out=ap("ln1"), in_=ap("tt1"), axis=AX, op=ALU.add)
        vnop()
        s_ = Sr.s
        s_.activation(out=ap("ln2"), in_=ap("ln1"), func=AF.Ln, scale=1.0 / 128.0,
                      bias=sb["ceps6"][:, 0:1])
        snop()
        s_ = Sr.s
        s_.activation(out=ap("ln1"), in_=ap("ln2"), func=AF.Exp, scale=-0.5)
        snop()
        v = Sr.v
        v.tensor_scalar(out=ap("upk"), in0=zc, scalar1=sb["ln1"][:, 0:1],
                        scalar2=None, op0=ALU.mult)
        vnop()
        t_ = Sr.t
        t_.transpose(out=pt[:, 0:128], in_=ap("upk"), identity=ident)
        v = Sr.v
        v.tensor_copy(out=sb["cndf"][:, 128 * tc:128 * (tc + 1)], in_=pt[:, 0:128])
        vnop()

    Sr.phase = 30
    _ffn(Sr, sb, ps, d, dr, qd, vw)

    for sl in range(4):
        Sr.phase = 100 * (sl + 1)
        _p4_slab(Sr, sb, ps, d, dr, sl, vw)
        Sr.phase = 100 * (sl + 1) + 90
        _p5_slab(Sr, sb, ps, d, dr, sl, vw)

    Sr.phase = 600
    _p7(Sr, sb, ps, d, dr)

    Sr.phase = 610
    _p8(Sr, sb, ps, d, dr, y_out, vw)


def _rot_table(Sr, sb, psrc, dst, phase):
    v = Sr.v
    v.tensor_scalar(out=sb["tt2"][:, 0:64], in0=psrc, scalar1=phase, scalar2=None, op0=ALU.add)
    v.engine_nop(); v.engine_nop()
    v.tensor_scalar(out=sb["tt3"][:, 0:64], in0=sb["tt2"][:, 0:64],
                    scalar1=1.0 / TWO_PI, scalar2=None, op0=ALU.mult)
    v.engine_nop(); v.engine_nop()
    v.memset(sb["tt1"][:, 0:64], 0.0)
    accp, cur = [sb["tt1"], sb["tt4"]], 0
    for k in range(1, 12):
        v.scalar_tensor_tensor(out=accp[1 - cur][:, 0:64], in0=sb["tt3"][:, 0:64],
                               scalar=float(k), in1=accp[cur][:, 0:64],
                               op0=ALU.is_ge, op1=ALU.add)
        v.engine_nop()
        cur = 1 - cur
    fl = accp[cur]
    v.scalar_tensor_tensor(out=sb["tt3"][:, 0:64], in0=fl[:, 0:64], scalar=-TWO_PI,
                           in1=sb["tt2"][:, 0:64], op0=ALU.mult, op1=ALU.add)
    v.engine_nop(); v.engine_nop()
    v.tensor_scalar(out=sb["tt1"][:, 0:64], in0=sb["tt3"][:, 0:64], scalar1=PI,
                    scalar2=-TWO_PI, op0=ALU.is_ge, op1=ALU.mult)
    v.engine_nop(); v.engine_nop()
    v.tensor_tensor(out=sb["tt2"][:, 0:64], in0=sb["tt3"][:, 0:64],
                    in1=sb["tt1"][:, 0:64], op=ALU.add)
    v.engine_nop(); v.engine_nop()
    s_ = Sr.s
    s_.activation(out=sb[dst][:, :], in_=sb["tt2"][:, 0:64], func=AF.Sin)
    s_.nop(); s_.nop()


def _ln(Sr, sb, src, dst, P):
    v = Sr.v
    v.tensor_reduce(out=sb["ln1"][0:P, :], in_=src, axis=AX, op=ALU.add)
    v.engine_nop(); v.engine_nop()
    v.tensor_scalar(out=sb["ln2"][0:P, :], in0=sb["ln1"][0:P, :], scalar1=1.0 / 1024.0,
                    scalar2=None, op0=ALU.mult)
    v.engine_nop(); v.engine_nop()
    v.tensor_scalar(out=sb["xrow"][0:P, :], in0=src, scalar1=sb["ln2"][0:P, 0:1],
                    scalar2=None, op0=ALU.subtract)
    v.engine_nop(); v.engine_nop()
    v.tensor_tensor(out=sb["convo"][0:P, :], in0=sb["xrow"][0:P, 0:512],
                    in1=sb["xrow"][0:P, 0:512], op=ALU.mult)
    v.tensor_tensor(out=sb["hid"][0:P, :], in0=sb["xrow"][0:P, 512:1024],
                    in1=sb["xrow"][0:P, 512:1024], op=ALU.mult)
    v.tensor_reduce(out=sb["sc3"][0:P, :], in_=sb["convo"][0:P, :], axis=AX, op=ALU.add)
    v.tensor_reduce(out=sb["sc4"][0:P, :], in_=sb["hid"][0:P, :], axis=AX, op=ALU.add)
    v.engine_nop(); v.engine_nop()
    v.tensor_tensor(out=sb["ln1"][0:P, :], in0=sb["sc3"][0:P, :], in1=sb["sc4"][0:P, :],
                    op=ALU.add)
    v.engine_nop(); v.engine_nop()
    s_ = Sr.s
    s_.activation(out=sb["ln2"][0:P, :], in_=sb["ln1"][0:P, :], func=AF.Ln,
                  scale=1.0 / 1024.0, bias=sb["ceps5"][0:P, 0:1])
    s_.nop(); s_.nop()
    s_ = Sr.s
    s_.activation(out=sb["ln1"][0:P, :], in_=sb["ln2"][0:P, :], func=AF.Exp, scale=-0.5)
    s_.nop(); s_.nop()
    v = Sr.v
    v.tensor_scalar(out=dst, in0=sb["xrow"][0:P, :], scalar1=sb["ln1"][0:P, 0:1],
                    scalar2=None, op0=ALU.mult)
    v.engine_nop(); v.engine_nop()


def _tr_to(Sr, sb, ps, src_bf, dst, tc, ndc):
    for dc in range(ndc):
        t_ = Sr.t
        t_.transpose(out=ps["pt"][:, 0:128], in_=src_bf[:, 128 * dc:128 * (dc + 1)],
                     identity=sb["ident"][:, :])
        v = Sr.v
        v.tensor_copy(out=dst[:, 2048 * dc + 128 * tc:2048 * dc + 128 * (tc + 1)],
                      in_=ps["pt"][:, 0:128])


def _rot_pass(Sr, sb, ps, p_sb, z_sb):
    pf = ps["pf"][:, :]
    for tc in range(NCH):
        pcv = p_sb[:, 128 * tc:128 * (tc + 1)]
        re, im = pcv[:, 0:64], pcv[:, 64:128]
        v = Sr.v
        v.tensor_tensor(out=sb["tt1"][:, 0:64], in0=sb["tabc"][:, :], in1=re, op=ALU.mult)
        v.tensor_tensor(out=sb["tt2"][:, 0:64], in0=sb["tabs"][:, :], in1=im, op=ALU.mult)
        v.tensor_tensor(out=sb["upk"][:, 0:64], in0=sb["tt1"][:, 0:64],
                        in1=sb["tt2"][:, 0:64], op=ALU.add)
        v.tensor_tensor(out=sb["tt1"][:, 0:64], in0=sb["tabc"][:, :], in1=im, op=ALU.mult)
        v.tensor_tensor(out=sb["tt2"][:, 0:64], in0=sb["tabs"][:, :], in1=re, op=ALU.mult)
        v.tensor_tensor(out=sb["upk"][:, 64:128], in0=sb["tt1"][:, 0:64],
                        in1=sb["tt2"][:, 0:64], op=ALU.subtract)
        t_ = Sr.t
        t_.matmul(out=pf[:, 0:128], lhsT=sb["trioneB"][:, :], rhs=sb["upk"][:, :],
                  start=True, stop=False)
        t_.matmul(out=pf[:, 0:128], lhsT=sb["ones1f"][0:1, 0:128], rhs=sb["z0row"][:, :],
                  start=False, stop=True)
        v = Sr.v
        v.tensor_tensor(out=sb["tt1"][:, 0:64], in0=sb["tabc"][:, :], in1=pf[:, 0:64], op=ALU.mult)
        v.tensor_tensor(out=sb["tt2"][:, 0:64], in0=sb["tabs"][:, :], in1=pf[:, 64:128], op=ALU.mult)
        v.tensor_tensor(out=z_sb[:, 128 * tc:128 * tc + 64], in0=sb["tt1"][:, 0:64],
                        in1=sb["tt2"][:, 0:64], op=ALU.subtract)
        v.tensor_tensor(out=sb["tt1"][:, 0:64], in0=sb["tabs"][:, :], in1=pf[:, 0:64], op=ALU.mult)
        v.tensor_tensor(out=sb["tt2"][:, 0:64], in0=sb["tabc"][:, :], in1=pf[:, 64:128], op=ALU.mult)
        v.tensor_tensor(out=z_sb[:, 128 * tc + 64:128 * (tc + 1)], in0=sb["tt1"][:, 0:64],
                        in1=sb["tt2"][:, 0:64], op=ALU.add)
        t_ = Sr.t
        t_.matmul(out=pf[0:1, 128:256], lhsT=sb["e127"][:, 0:1],
                  rhs=z_sb[:, 128 * tc:128 * (tc + 1)], start=True, stop=True)
        v = Sr.v
        v.tensor_copy(out=sb["z0row"][:, :], in_=pf[0:1, 128:256])
        v.engine_nop(); v.engine_nop()


def _ffn(Sr, sb, ps, d, dr, qdict, vw):
    pa, pb, pf = ps["pa"][:, :], ps["pb"][:, :], ps["pf"][:, :]
    bigB, bigC, bigE = sb["bigB"][:, :], sb["bigC"][:, :], sb["bigE"][:, :]

    def vnop(k=2):
        v = Sr.v
        for _ in range(k):
            v.engine_nop()

    for (src, dst, R, Cw) in ((d["gw"], dr["gwq"], I_FFN, D),
                              (d["uw"], dr["uwq"], I_FFN, D),
                              (d["dw"], dr["dwq"], D, I_FFN)):
        v = Sr.v
        v.memset(sb["sc1"][:, :], 0.0)
        vnop()
        accp, cur = [sb["sc1"], sb["sc2"]], 0
        for r in range(R // 128):
            for p_ in range(Cw // 1024):
                Sr.dma(sb["xrow"][:, :], src[128 * r:128 * (r + 1), 1024 * p_:1024 * (p_ + 1)])
                v = Sr.v
                v.tensor_reduce(out=sb["sc3"][:, :], in_=sb["xrow"][:, :], axis=AX,
                                op=ALU.max, apply_absolute_value=True)
                vnop()
                v.tensor_tensor(out=accp[1 - cur][:, :], in0=accp[cur][:, :],
                                in1=sb["sc3"][:, :], op=ALU.max)
                vnop()
                cur = 1 - cur
        t_ = Sr.t
        t_.transpose(out=ps["pe"][0:1, 0:128], in_=accp[cur][:, 0:1], identity=sb["ident_f"][:, :])
        v = Sr.v
        v.tensor_reduce(out=sb["sc3"][0:1, :], in_=ps["pe"][0:1, 0:128], axis=AX, op=ALU.max)
        vnop()
        v.reciprocal(out=sb["sc4"][0:1, :], in_=sb["sc3"][0:1, :])
        vnop()
        v.tensor_scalar(out=sb["sc3"][0:1, :], in0=sb["sc4"][0:1, :], scalar1=2688.0,
                        scalar2=None, op0=ALU.mult)
        vnop()
        t_ = Sr.t
        t_.matmul(out=ps["pe"][:, 0:1], lhsT=sb["ones1f"][0:1, 0:128],
                  rhs=sb["sc3"][0:1, 0:1], start=True, stop=True)
        v = Sr.v
        v.tensor_scalar(out=sb["gs6w"][:, :], in0=ps["pe"][:, 0:1], scalar1=1.0 / 6.0,
                        scalar2=None, op0=ALU.mult)
        vnop()
        v.tensor_copy(out=sb["sc4"][:, :], in_=ps["pe"][:, 0:1])
        vnop()
        v.reciprocal(out=sb["gsiw"][:, :], in_=sb["sc4"][:, :])
        vnop()
        for r in range(R // 128):
            for p_ in range(Cw // 1024):
                Sr.dma(sb["xrow"][:, :], src[128 * r:128 * (r + 1), 1024 * p_:1024 * (p_ + 1)])
                emit_quant(Sr, sb["xrow"][:, :], vw["xb16"][:, :], 1024, qdict,
                           sb["gs6w"][:, 0:1], sb["gsiw"][:, 0:1], bf16_s=False)
                for sub in range(8):
                    t_ = Sr.t
                    t_.transpose(out=ps["pt"][:, 0:128],
                                 in_=vw["xb16"][:, 128 * sub:128 * (sub + 1)],
                                 identity=sb["ident"][:, :])
                    v = Sr.v
                    v.tensor_copy(out=sb["wslab"][:, 128 * sub:128 * (sub + 1)],
                                  in_=ps["pt"][:, 0:128])
                if Cw == D:
                    outap = dst.rearrange("(dc p) i -> p dc i", p=128)[:, :, 128 * r:128 * (r + 1)]
                else:
                    outap = dst.rearrange("(ic p) o -> p ic o", p=128)[:, 8 * p_:8 * (p_ + 1),
                                                                      128 * r:128 * (r + 1)]
                Sr.dma(outap, sb["wslab"][:, :].rearrange("p (s c) -> p s c", c=128))
    for islab in range(8):
        for dc in range(8):
            Sr.dma(bigE[:, 512 * dc:512 * (dc + 1)],
                   dr["gwq"][128 * dc:128 * (dc + 1), 512 * islab:512 * (islab + 1)])
            Sr.dma(bigE[:, 4096 + 512 * dc:4096 + 512 * (dc + 1)],
                   dr["uwq"][128 * dc:128 * (dc + 1), 512 * islab:512 * (islab + 1)])
        for tc in range(NCH):
            t_ = Sr.t
            for dc in range(8):
                t_.matmul(out=pa, lhsT=bigB[:, 2048 * dc + 128 * tc:2048 * dc + 128 * (tc + 1)],
                          rhs=bigE[:, 512 * dc:512 * (dc + 1)],
                          start=(dc == 0), stop=(dc == 7))
            for dc in range(8):
                t_.matmul(out=pb, lhsT=bigB[:, 2048 * dc + 128 * tc:2048 * dc + 128 * (tc + 1)],
                          rhs=bigE[:, 4096 + 512 * dc:4096 + 512 * (dc + 1)],
                          start=(dc == 0), stop=(dc == 7))
            s_ = Sr.s
            s_.activation(out=sb["hid"][:, :], in_=pa, func=AF.Silu)
            v = Sr.v
            v.tensor_tensor(out=sb["convo"][:, :], in0=sb["hid"][:, :], in1=pb, op=ALU.mult)
            emit_quant(Sr, sb["convo"][:, :], vw["convb"][:, :], 512, qdict,
                       sb["gs6d"][:, 0:1], sb["gsid"][:, 0:1], bf16_s=True)
            Sr.dma(dr["hq"][128 * tc:128 * (tc + 1), 512 * islab:512 * (islab + 1)],
                   vw["convb"][:, :])
    bigD = sb["bigD"][:, :]
    for ic in range(32):
        Sr.dma(bigC[:, 512 * ic:512 * (ic + 1)],
               dr["dwq"][128 * ic:128 * (ic + 1), 0:512])
        Sr.dma(bigD[:, 512 * ic:512 * (ic + 1)],
               dr["dwq"][128 * ic:128 * (ic + 1), 512:1024])
    for tc in range(NCH):
        Sr.dma(bigE[:, 0:4096], dr["hq"][128 * tc:128 * (tc + 1), :])
        for sub in range(32):
            t_ = Sr.t
            t_.transpose(out=ps["pt"][:, 0:128], in_=bigE[:, 128 * sub:128 * (sub + 1)],
                         identity=sb["ident"][:, :])
            v = Sr.v
            v.tensor_copy(out=bigE[:, 4096 + 128 * sub:4096 + 128 * (sub + 1)],
                          in_=ps["pt"][:, 0:128])
        for js in range(2):
            t_ = Sr.t
            wsrc = bigC if js == 0 else bigD
            for ic in range(32):
                t_.matmul(out=pa, lhsT=bigE[:, 4096 + 128 * ic:4096 + 128 * (ic + 1)],
                          rhs=wsrc[:, 512 * ic:512 * (ic + 1)],
                          start=(ic == 0), stop=(ic == 31))
            s_ = Sr.s
            s_.activation(out=sb["wslab"][:, 0:512], in_=pa, func=AF.Silu)
            Sr.dma(dr["gffn"][128 * tc:128 * (tc + 1), 512 * js:512 * (js + 1)],
                   sb["wslab"][:, 0:512])


CCDEF = [(cc, 128 * cc, 128) for cc in range(24)] + [(24, 3072, 32)] + \
        [(25 + g, 3104 + 128 * g, 128) for g in range(8)]


def _p4_slab(Sr, sb, ps, d, dr, sl, vw):
    pa, pb, pc, pd = ps["pa"][:, :], ps["pb"][:, :], ps["pc"][:, :], ps["pd"][:, :]
    bigA, bigB, bigC, bigD = (sb[k][:, :] for k in ("bigA", "bigB", "bigC", "bigD"))

    def vnop(k=2):
        v = Sr.v
        for _ in range(k):
            v.engine_nop()

    for (cc, c0, csz) in CCDEF:
        Sr.phase = 100 * (sl + 1) + 2 * cc
        Sr.dma(sb["wslab"][:, 0:8 * csz].rearrange("p (dc c) -> p dc c", c=csz),
               d["ctrl_W"].rearrange("(dc p) c -> p dc c", p=128)[:, :, c0:c0 + csz],
               cast=True)
        t_ = Sr.t
        for dc in range(8):
            t_.matmul(out=pa[0:csz, 0:512], lhsT=sb["wslab"][:, csz * dc:csz * (dc + 1)],
                      rhs=bigA[:, 2048 * dc + 512 * sl:2048 * dc + 512 * (sl + 1)],
                      start=(dc == 0), stop=(dc == 7))
        if cc < 24:
            kind, fc = cc // 8, cc % 8
            t_ = Sr.t
            t_.matmul(out=pb[0:128, 0:512], lhsT=sb["deltW"][:, 128 * cc:128 * (cc + 1)],
                      rhs=sb["cndf"][:, 512 * sl:512 * (sl + 1)], start=True, stop=True)
            v = Sr.v
            v.tensor_scalar(out=sb["convo"][:, :], in0=pa[0:128, 0:512],
                            scalar1=sb["cbias"][:, cc:cc + 1], scalar2=None, op0=ALU.add)
            v.scalar_tensor_tensor(out=sb["pre"][:, 3:515], in0=pb[0:128, 0:512],
                                   scalar=sb["rampq"][:, cc:cc + 1], in1=sb["convo"][:, :],
                                   op0=ALU.mult, op1=ALU.add)
            if sl == 0:
                t_ = Sr.t
                for dc in range(8):
                    t_.matmul(out=pc[0:csz, 0:3], lhsT=sb["wslab"][:, csz * dc:csz * (dc + 1)],
                              rhs=sb["xnhf"][:, 3 * dc:3 * dc + 3],
                              start=(dc == 0), stop=(dc == 7))
                t_.matmul(out=pd[0:128, 0:3], lhsT=sb["deltW"][:, 128 * cc:128 * (cc + 1)],
                          rhs=sb["condh"][:, :], start=True, stop=True)
                v = Sr.v
                v.tensor_scalar(out=sb["tt1"][:, 0:3], in0=pc[0:128, 0:3],
                                scalar1=sb["cbias"][:, cc:cc + 1], scalar2=None, op0=ALU.add)
                vnop()
                v.scalar_tensor_tensor(out=sb["tt2"][:, 0:3], in0=pd[0:128, 0:3],
                                       scalar=sb["rampq"][:, cc:cc + 1], in1=sb["tt1"][:, 0:3],
                                       op0=ALU.mult, op1=ALU.add)
                vnop()
                v.tensor_scalar(out=sb["tails"][:, 3 * cc:3 * cc + 3], in0=sb["tt2"][:, 0:3],
                                scalar1=sb["maskc"][:, 0:1], scalar2=None, op0=ALU.mult)
                vnop()
            v = Sr.v
            v.tensor_copy(out=sb["pre"][:, 0:3], in_=sb["tails"][:, 3 * cc:3 * cc + 3])
            vnop()
            base = (kind * 8 + fc) * 4
            v.tensor_scalar(out=sb["hid"][:, :], in0=sb["pre"][:, 0:512],
                            scalar1=sb["convw"][:, base:base + 1], scalar2=None, op0=ALU.mult)
            v.scalar_tensor_tensor(out=sb["convo"][:, :], in0=sb["pre"][:, 1:513],
                                   scalar=sb["convw"][:, base + 1:base + 2],
                                   in1=sb["hid"][:, :], op0=ALU.mult, op1=ALU.add)
            v.scalar_tensor_tensor(out=sb["hid"][:, :], in0=sb["pre"][:, 2:514],
                                   scalar=sb["convw"][:, base + 2:base + 3],
                                   in1=sb["convo"][:, :], op0=ALU.mult, op1=ALU.add)
            v.scalar_tensor_tensor(out=sb["convo"][:, :], in0=sb["pre"][:, 3:515],
                                   scalar=sb["convw"][:, base + 3:base + 4],
                                   in1=sb["hid"][:, :], op0=ALU.mult, op1=ALU.add)
            v.tensor_copy(out=sb["tails"][:, 3 * cc:3 * cc + 3], in_=sb["pre"][:, 512:515])
            vnop()
            s_ = Sr.s
            s_.activation(out=vw["convb"][:, :], in_=sb["convo"][:, :], func=AF.Silu)
            v = Sr.v
            if kind == 0:
                dst = bigB[:, 2048 * fc + 512 * sl:2048 * fc + 512 * (sl + 1)]
            elif kind == 1:
                dst = bigD[:, 512 * fc:512 * (fc + 1)]
            else:
                dst = bigD[:, 4096 + 512 * fc:4096 + 512 * (fc + 1)]
            v.tensor_copy(out=dst, in_=vw["convb"][:, :])
        elif cc == 24:
            t_ = Sr.t
            for dc in range(8):
                t_.matmul(out=pc[0:16, 0:512], lhsT=sb["wslab"][:, 32 * dc:32 * dc + 16],
                          rhs=bigA[:, 2048 * dc + 512 * sl:2048 * dc + 512 * (sl + 1)],
                          start=(dc == 0), stop=(dc == 7))
            for dc in range(8):
                t_.matmul(out=pd[0:16, 0:512], lhsT=sb["wslab"][:, 32 * dc + 16:32 * dc + 32],
                          rhs=bigA[:, 2048 * dc + 512 * sl:2048 * dc + 512 * (sl + 1)],
                          start=(dc == 0), stop=(dc == 7))
            t_.matmul(out=pb[0:16, 0:512], lhsT=sb["deltW"][:, 3072:3088],
                      rhs=sb["cndf"][:, 512 * sl:512 * (sl + 1)], start=True, stop=True)
            v = Sr.v
            v.tensor_scalar(out=sb["convo"][0:16, :], in0=pc[0:16, 0:512],
                            scalar1=sb["cbias"][0:16, 24:25], scalar2=None, op0=ALU.add)
            v.tensor_scalar(out=sb["hid"][0:16, :], in0=pd[0:16, 0:512],
                            scalar1=sb["cbias"][0:16, 25:26], scalar2=None, op0=ALU.add)
            vnop()
            v.scalar_tensor_tensor(out=sb["pre"][0:16, 0:512], in0=pb[0:16, 0:512],
                                   scalar=sb["rampb"][:, 0:1], in1=sb["hid"][0:16, :],
                                   op0=ALU.mult, op1=ALU.add)
            vnop()
            for tcs in range(4):
                t_ = Sr.t
                t_.transpose(out=ps["pe"][:, 0:16],
                             in_=sb["convo"][0:16, 128 * tcs:128 * (tcs + 1)],
                             identity=sb["ident_f"][0:16, 0:16])
                v = Sr.v
                v.tensor_copy(out=sb["dbt"][:, 32 * tcs:32 * tcs + 16], in_=ps["pe"][:, 0:16])
                vnop()
                t_ = Sr.t
                t_.transpose(out=ps["pe"][:, 16:32],
                             in_=sb["pre"][0:16, 128 * tcs:128 * (tcs + 1)],
                             identity=sb["ident_f"][0:16, 0:16])
                v = Sr.v
                v.tensor_copy(out=sb["dbt"][:, 32 * tcs + 16:32 * tcs + 32],
                              in_=ps["pe"][:, 16:32])
                vnop()
        else:
            gc = cc - 25
            v = Sr.v
            v.tensor_scalar(out=sb["convo"][:, :], in0=pa[0:128, 0:512],
                            scalar1=sb["cbias"][:, cc + 1:cc + 2], scalar2=None, op0=ALU.add)
            s_ = Sr.s
            s_.activation(out=vw["convb"][:, :], in_=sb["convo"][:, :], func=AF.Silu)
            for tcs in range(4):
                t_ = Sr.t
                t_.transpose(out=ps["pt"][:, 0:128],
                             in_=vw["convb"][:, 128 * tcs:128 * (tcs + 1)],
                             identity=sb["ident"][:, :])
                v = Sr.v
                v.tensor_copy(out=bigC[:, 1024 * (4 * sl + tcs) + 128 * gc:
                                       1024 * (4 * sl + tcs) + 128 * (gc + 1)],
                              in_=ps["pt"][:, 0:128])


def _p5_slab(Sr, sb, ps, d, dr, sl, vw):
    pa, pb, pc, pd, pe, pf = (ps[k][:, :] for k in ("pa", "pb", "pc", "pd", "pe", "pf"))
    pt = ps["pt"][:, :]
    bigB, bigD = sb["bigB"][:, :], sb["bigD"][:, :]

    def vnop(k=2):
        v = Sr.v
        for _ in range(k):
            v.engine_nop()

    def snop():
        s_ = Sr.s
        s_.nop(); s_.nop()

    for tcs in range(4):
        tcg = 4 * sl + tcs
        dec = sb["dbt"][:, 32 * tcs:32 * tcs + 16]
        bet = sb["dbt"][:, 32 * tcs + 16:32 * tcs + 32]
        s_ = Sr.s
        s_.activation(out=sb["t16a"][:, :], in_=dec, func=AF.Exp, scale=-1.0)
        snop()
        s_ = Sr.s
        s_.activation(out=sb["t16b"][:, :], in_=sb["t16a"][:, :], func=AF.Ln,
                      bias=sb["cone"][:, 0:1])
        snop()
        v = Sr.v
        v.tensor_scalar(out=sb["lng"][:, :], in0=sb["t16b"][:, :], scalar1=-1.0,
                        scalar2=None, op0=ALU.mult)
        vnop()
        s_ = Sr.s
        s_.activation(out=sb["t16a"][:, :], in_=bet, func=AF.Exp, scale=-1.0)
        snop()
        s_ = Sr.s
        s_.activation(out=sb["t16b"][:, :], in_=sb["t16a"][:, :], func=AF.Ln,
                      bias=sb["cone"][:, 0:1])
        snop()
        v = Sr.v
        v.tensor_scalar(out=sb["lnb"][:, :], in0=sb["t16b"][:, :], scalar1=-1.0,
                        scalar2=None, op0=ALU.mult)
        vnop()
        s_ = Sr.s
        s_.activation(out=sb["beta"][:, :], in_=sb["lnb"][:, :], func=AF.Exp)
        snop()
        t_ = Sr.t
        t_.matmul(out=pf[:, 0:16], lhsT=sb["trioneF"][:, :], rhs=sb["lng"][:, :],
                  start=True, stop=True)
        v = Sr.v
        v.tensor_copy(out=sb["lam"][:, :], in_=pf[:, 0:16])
        vnop()
        for fc in range(8):
            qt = bigB[:, 2048 * fc + 128 * tcg:2048 * fc + 128 * (tcg + 1)]
            kt = bigD[:, 512 * fc + 128 * tcs:512 * fc + 128 * (tcs + 1)]
            v = Sr.v
            v.tensor_tensor(out=sb["sq16"][:, :], in0=qt, in1=qt, op=ALU.mult)
            t_ = Sr.t
            t_.matmul(out=pf[:, 16 + 2 * fc:17 + 2 * fc], lhsT=sb["sq16"][0:64, :],
                      rhs=sb["onescb"][0:64, 0:1], start=True, stop=True)
            t_.matmul(out=pf[:, 17 + 2 * fc:18 + 2 * fc], lhsT=sb["sq16"][64:128, :],
                      rhs=sb["onescb"][64:128, 0:1], start=True, stop=True,
                      tile_position=(64, 0))
            v = Sr.v
            v.tensor_tensor(out=sb["sq16"][:, :], in0=kt, in1=kt, op=ALU.mult)
            t_ = Sr.t
            t_.matmul(out=pf[:, 32 + 2 * fc:33 + 2 * fc], lhsT=sb["sq16"][0:64, :],
                      rhs=sb["onescb"][0:64, 0:1], start=True, stop=True)
            t_.matmul(out=pf[:, 33 + 2 * fc:34 + 2 * fc], lhsT=sb["sq16"][64:128, :],
                      rhs=sb["onescb"][64:128, 0:1], start=True, stop=True,
                      tile_position=(64, 0))
        s_ = Sr.s
        s_.activation(out=sb["l1q"][:, :], in_=pf[:, 16:32], func=AF.Ln,
                      bias=sb["ceps6"][:, 0:1])
        snop()
        s_ = Sr.s
        s_.activation(out=sb["l1k"][:, :], in_=pf[:, 32:48], func=AF.Ln,
                      bias=sb["ceps6"][:, 0:1])
        snop()
        v = Sr.v
        v.scalar_tensor_tensor(out=sb["t16a"][:, :], in0=sb["l1k"][:, :], scalar=-0.5,
                               in1=sb["lam"][:, :], op0=ALU.mult, op1=ALU.add)
        vnop()
        v.tensor_tensor(out=sb["rowM"][:, :], in0=sb["t16a"][:, :], in1=sb["lnb"][:, :],
                        op=ALU.add)
        vnop()
        v.scalar_tensor_tensor(out=sb["rowA"][:, :], in0=sb["l1q"][:, :], scalar=-0.5,
                               in1=sb["lam"][:, :], op0=ALU.mult, op1=ALU.add)
        vnop()
        v.scalar_tensor_tensor(out=sb["t16b"][:, :], in0=sb["l1k"][:, :], scalar=0.5,
                               in1=sb["lam"][:, :], op0=ALU.mult, op1=ALU.add)
        vnop()
        v.tensor_scalar(out=sb["colv"][:, :], in0=sb["t16b"][:, :], scalar1=-1.0,
                        scalar2=None, op0=ALU.mult)
        vnop()
        s_ = Sr.s
        s_.activation(out=sb["coefk"][:, :], in_=sb["rowM"][:, :], func=AF.Exp)
        snop()
        s_ = Sr.s
        s_.activation(out=sb["eAall"][:, 16 * tcg:16 * (tcg + 1)], in_=sb["rowA"][:, :],
                      func=AF.Exp)
        snop()
        t_ = Sr.t
        t_.matmul(out=pe[0:1, 0:16], lhsT=sb["onescf"][:, 0:1], rhs=sb["lng"][:, :],
                  start=True, stop=True)
        v = Sr.v
        v.tensor_copy(out=sb["lamC"][:, :], in_=pe[0:1, 0:16])
        vnop()
        t_ = Sr.t
        t_.matmul(out=pf[:, 48:64], lhsT=sb["ones1f"][0:1, 0:128], rhs=sb["lamC"][:, :],
                  start=True, stop=True)
        v = Sr.v
        v.tensor_tensor(out=sb["t16a"][:, :], in0=pf[:, 48:64], in1=sb["colv"][:, :],
                        op=ALU.add)
        vnop()
        s_ = Sr.s
        s_.activation(out=sb["coefp"][:, :], in_=sb["t16a"][:, :], func=AF.Exp)
        snop()
        s_ = Sr.s
        s_.activation(out=sb["gCc"][:, :], in_=ps["pf"][0:64, 48:64], func=AF.Exp)
        snop()
        for g4 in range(4):
            for hs in range(4):
                h = 4 * g4 + hs
                half, fc = h % 2, h // 2
                qf_t = bigB[64 * half:64 * half + 64,
                            2048 * fc + 128 * tcg:2048 * fc + 128 * (tcg + 1)]
                kf_t = bigD[64 * half:64 * half + 64,
                            512 * fc + 128 * tcs:512 * fc + 128 * (tcs + 1)]
                vf_t = bigD[64 * half:64 * half + 64,
                            4096 + 512 * fc + 128 * tcs:4096 + 512 * fc + 128 * (tcs + 1)]
                tp = (64 * half, 0)
                t_ = Sr.t
                t_.matmul(out=pc[:, 0:128], lhsT=kf_t, rhs=kf_t, start=True, stop=True,
                          tile_position=tp)
                t_.matmul(out=pc[:, 128:256], lhsT=kf_t, rhs=qf_t, start=True, stop=True,
                          tile_position=tp)
                v = Sr.v
                v.tensor_copy(out=sb["pk2a"][:, 1:2], in_=sb["colv"][:, h:h + 1])
                v.tensor_copy(out=sb["pk2b"][:, 0:1], in_=sb["rowM"][:, h:h + 1])
                v.tensor_copy(out=sb["pk2c"][:, 0:1], in_=sb["rowA"][:, h:h + 1])
                vnop()
                t_ = Sr.t
                t_.transpose(out=pb[0:2, 0:128], in_=sb["pk2a"][:, :],
                             identity=sb["ident_f"][:, :])
                t_.transpose(out=pb[0:2, 128:256], in_=sb["pk2b"][:, :],
                             identity=sb["ident_f"][:, :])
                t_.transpose(out=pb[0:2, 256:384], in_=sb["pk2c"][:, :],
                             identity=sb["ident_f"][:, :])
                v = Sr.v
                v.tensor_copy(out=sb["stk2"][:, :], in_=pb[0:2, 0:128])
                v.tensor_copy(out=sb["stk2r"][:, :], in_=pb[0:2, 128:256])
                v.tensor_copy(out=sb["stk2q"][:, :], in_=pb[0:2, 256:384])
                vnop()
                t_ = Sr.t
                t_.matmul(out=pe[:, 0:128], lhsT=sb["stk2"][0:2, :], rhs=sb["stk2r"][0:2, :],
                          start=True, stop=True)
                t_.matmul(out=pe[:, 128:256], lhsT=sb["stk2"][0:2, :], rhs=sb["stk2q"][0:2, :],
                          start=True, stop=True)
                v = Sr.v
                v.scalar_tensor_tensor(out=sb["tt1"][:, :], in0=pe[:, 0:128], scalar=1.0,
                                       in1=sb["maskM"][:, :], op0=ALU.mult, op1=ALU.add)
                s_ = Sr.s
                s_.activation(out=sb["tt2"][:, :], in_=sb["tt1"][:, :], func=AF.Exp)
                v = Sr.v
                v.tensor_scalar(out=sb["tt3"][:, :], in0=sb["tt2"][:, :], scalar1=-1.0,
                                scalar2=None, op0=ALU.mult)
                v.tensor_tensor(out=sb["Pa"][:, 128 * hs:128 * (hs + 1)],
                                in0=sb["tt3"][:, :], in1=pc[:, 0:128], op=ALU.mult)
                v.scalar_tensor_tensor(out=sb["tt1"][:, :], in0=pe[:, 128:256], scalar=1.0,
                                       in1=sb["maskA"][:, :], op0=ALU.mult, op1=ALU.add)
                s_ = Sr.s
                s_.activation(out=sb["tt2"][:, :], in_=sb["tt1"][:, :], func=AF.Exp)
                v = Sr.v
                v.tensor_tensor(out=sb["AqS"][:, 128 * hs:128 * (hs + 1)],
                                in0=sb["tt2"][:, :], in1=pc[:, 128:256], op=ALU.mult)
                t_ = Sr.t
                t_.transpose(out=pt[:, 0:64], in_=vf_t,
                             identity=sb["ident"][64 * half:64 * half + 64,
                                                  64 * half:64 * half + 64],
                             tile_position=tp)
                v = Sr.v
                v.tensor_scalar(out=sb["Ua"][:, 128 * hs:128 * hs + 64], in0=pt[:, 0:64],
                                scalar1=sb["beta"][:, h:h + 1], scalar2=None, op0=ALU.mult)
                t_ = Sr.t
                t_.transpose(out=pt[:, 64:128], in_=kf_t,
                             identity=sb["ident"][64 * half:64 * half + 64,
                                                  64 * half:64 * half + 64],
                             tile_position=tp)
                v = Sr.v
                v.tensor_scalar(out=sb["Ua"][:, 128 * hs + 64:128 * (hs + 1)],
                                in0=pt[:, 64:128], scalar1=sb["coefk"][:, h:h + 1],
                                scalar2=None, op0=ALU.mult)
                v.tensor_scalar(out=sb["kpt"][:, 64 * hs:64 * (hs + 1)], in0=pt[:, 64:128],
                                scalar1=sb["coefp"][:, h:h + 1], scalar2=None, op0=ALU.mult)
            Pcur, Pnxt = sb["Pa"][:, :], sb["Pb"][:, :]
            Ucur, Unxt = sb["Ua"][:, :], sb["Ub"][:, :]
            for lvl in range(7):
                t_ = Sr.t
                for hs in range(4):
                    t_.matmul(out=pd[:, 128 * hs:128 * (hs + 1)],
                              lhsT=Pcur[:, 128 * hs:128 * (hs + 1)],
                              rhs=Ucur[:, 128 * hs:128 * (hs + 1)], start=True, stop=True)
                v = Sr.v
                v.tensor_tensor(out=Unxt, in0=pd, in1=Ucur, op=ALU.add)
                Ucur, Unxt = Unxt, Ucur
                if lvl < 6:
                    t_ = Sr.t
                    for hs in range(4):
                        t_.transpose(out=pt[:, 128 * hs:128 * (hs + 1)],
                                     in_=Pcur[:, 128 * hs:128 * (hs + 1)],
                                     identity=sb["ident"][:, :])
                    v = Sr.v
                    v.tensor_copy(out=sb["PT"][:, :], in_=pt)
                    t_ = Sr.t
                    for hs in range(4):
                        t_.matmul(out=pe[:, 128 * hs:128 * (hs + 1)],
                                  lhsT=sb["PT"][:, 128 * hs:128 * (hs + 1)],
                                  rhs=Pcur[:, 128 * hs:128 * (hs + 1)], start=True, stop=True)
                    v = Sr.v
                    v.tensor_copy(out=Pnxt, in_=pe)
                    Pcur, Pnxt = Pnxt, Pcur
            Ufin = Ucur
            t_ = Sr.t
            for hs in range(4):
                t_.matmul(out=pd[:, 64 * hs:64 * (hs + 1)],
                          lhsT=sb["AqS"][:, 128 * hs:128 * (hs + 1)],
                          rhs=Ufin[:, 128 * hs:128 * hs + 64], start=True, stop=True)
            for hs in range(4):
                t_.matmul(out=pe[0:64, 128 * hs:128 * (hs + 1)],
                          lhsT=Ufin[:, 128 * hs + 64:128 * (hs + 1)],
                          rhs=sb["AqS"][:, 128 * hs:128 * (hs + 1)], start=True, stop=True)
            for hs in range(4):
                t_.matmul(out=pf[0:64, 64 * hs:64 * (hs + 1)],
                          lhsT=sb["kpt"][:, 64 * hs:64 * (hs + 1)],
                          rhs=Ufin[:, 128 * hs:128 * hs + 64], start=True, stop=True)
            for hs in range(4):
                t_.matmul(out=pc[0:64, 64 * hs:64 * (hs + 1)],
                          lhsT=Ufin[:, 128 * hs + 64:128 * (hs + 1)],
                          rhs=sb["kpt"][:, 64 * hs:64 * (hs + 1)], start=True, stop=True)
            v = Sr.v
            v.tensor_copy(out=vw["convb"][:, 0:256], in_=pd[:, 0:256])
            Sr.dma(dr["oloc"][128 * tcg:128 * (tcg + 1), 256 * g4:256 * (g4 + 1)],
                   vw["convb"][:, 0:256])
            v = Sr.v
            v.tensor_scalar(out=sb["pk64"][:, :], in0=pe[0:64, 0:512], scalar1=-1.0,
                            scalar2=None, op0=ALU.mult)
            Sr.dma(dr["gneg"][64 * (16 * tcg + 4 * g4):64 * (16 * tcg + 4 * g4) + 256, :]
                   .rearrange("(hh r) c -> r hh c", hh=4),
                   sb["pk64"][:, :].rearrange("p (hh c) -> p hh c", hh=4))
            for hs in range(4):
                h = 4 * g4 + hs
                v = Sr.v
                v.tensor_scalar(out=sb["t64"][:, :], in0=sb["i64b"][:, :],
                                scalar1=sb["gCc"][:, h:h + 1], scalar2=None, op0=ALU.mult)
                vnop()
                v.scalar_tensor_tensor(out=sb["pk64"][:, 128 * hs:128 * hs + 64],
                                       in0=pc[0:64, 64 * hs:64 * (hs + 1)], scalar=-1.0,
                                       in1=sb["t64"][:, :], op0=ALU.mult, op1=ALU.add)
                vnop()
                v.tensor_copy(out=sb["pk64"][:, 128 * hs + 64:128 * (hs + 1)],
                              in_=pf[0:64, 64 * hs:64 * (hs + 1)])
                vnop()
            Sr.dma(dr["ptqc"][64 * (16 * tcg + 4 * g4):64 * (16 * tcg + 4 * g4) + 256, :]
                   .rearrange("(hh r) c -> r hh c", hh=4),
                   sb["pk64"][:, :].rearrange("p (hh c) -> p hh c", hh=4))


def _p7(Sr, sb, ps, d, dr):
    pd, pe, pf = ps["pd"][:, :], ps["pe"][:, :], ps["pf"][:, :]
    pt = ps["pt"][:, :]
    bigA, bigB, bigC, bigE = (sb[k][:, :] for k in ("bigA", "bigB", "bigC", "bigE"))
    ovl = {"ptq": sb["cndf"][:, 0:1024], "gng": sb["cndf"][:, 1024:2048],
           "Sc1": sb["Ua"][:, :], "Sc2": sb["Ub"][:, :],
           "Tc1": sb["Pa"][:, :], "Tc2": sb["Pb"][:, :],
           "Strue": sb["PT"][:, :], "Srec": sb["AqS"][:, :],
           "S16p": sb["pre"][:, 0:512]}

    def vnop(k=2):
        v = Sr.v
        for _ in range(k):
            v.engine_nop()

    def snop():
        s_ = Sr.s
        s_.nop(); s_.nop()

    def load_ptq(ch):
        Sr.dma(ovl["ptq"].rearrange("p (j c) -> p j c", j=8),
               dr["ptqc"][64 * 16 * ch:64 * 16 * (ch + 1), :]
               .rearrange("(j two r) c -> (two r) j c", j=8, two=2))

    v = Sr.v
    v.memset(ovl["Sc1"], 0.0)
    vnop()
    Scur, Snxt = ovl["Sc1"], ovl["Sc2"]
    for ch in range(NCH):
        load_ptq(ch)
        t_ = Sr.t
        for h in range(H):
            two, j = h % 2, h // 2
            t_.matmul(out=pd[64 * two:64 * two + 64, 64 * j:64 * (j + 1)],
                      lhsT=ovl["ptq"][64 * two:64 * two + 64, 128 * j:128 * j + 64],
                      rhs=Scur[64 * two:64 * two + 64, 64 * j:64 * (j + 1)],
                      start=True, stop=True, tile_position=(64 * two, 64 * two))
        for h in range(H):
            two, j = h % 2, h // 2
            v = Sr.v
            v.tensor_tensor(out=Snxt[64 * two:64 * two + 64, 64 * j:64 * (j + 1)],
                            in0=pd[64 * two:64 * two + 64, 64 * j:64 * (j + 1)],
                            in1=ovl["ptq"][64 * two:64 * two + 64, 128 * j + 64:128 * (j + 1)],
                            op=ALU.add)
            if ch == NCH - 1:
                v.tensor_tensor(out=ovl["S16p"][64 * two:64 * two + 64, 64 * j:64 * (j + 1)],
                                in0=pd[64 * two:64 * two + 64, 64 * j:64 * (j + 1)],
                                in1=ovl["ptq"][64 * two:64 * two + 64, 128 * j + 64:128 * (j + 1)],
                                op=ALU.add)
        v = Sr.v
        v.tensor_copy(out=bigE[:, 512 * ch:512 * (ch + 1)], in_=Snxt)
        Scur, Snxt = Snxt, Scur
    for two in range(2):
        Sr.dma(dr["cc2i"].ap().rearrange("(j two r) v -> r j two v", j=8, two=2)
               [:, :, two:two + 1, :],
               ovl["S16p"][64 * two:64 * two + 64, :]
               .rearrange("r (j v) -> r j v", j=8).unsqueeze(2))
    Sr.cc("AllGather", [dr["cc2i"].ap()], [dr["cc2o"].ap()])
    v = Sr.v
    v.memset(sb["convo"][:, :], 0.0)
    vnop()
    accp, cur = [sb["convo"], sb["hid"]], 0
    for r8 in range(8):
        Sr.dma(ovl["S16p"].rearrange("p (j v) -> p j v", j=8),
               dr["cc2o"].ap()[1024 * r8:1024 * (r8 + 1), :]
               .rearrange("(j two r) v -> (two r) j v", j=8, two=2))
        v = Sr.v
        v.scalar_tensor_tensor(out=accp[1 - cur][:, :], in0=ovl["S16p"],
                               scalar=sb["selw"][:, r8:r8 + 1], in1=accp[cur][:, :],
                               op0=ALU.mult, op1=ALU.add)
        cur = 1 - cur
    v = Sr.v
    v.tensor_copy(out=ovl["Srec"], in_=accp[cur][:, :])
    vnop()
    Tcur, Tnxt = ovl["Tc1"], ovl["Tc2"]
    v = Sr.v
    v.tensor_copy(out=Tcur, in_=ovl["Srec"])
    vnop()
    for ch in range(NCH):
        if ch == 0:
            v = Sr.v
            v.tensor_copy(out=ovl["Strue"], in_=ovl["Srec"])
            vnop()
        else:
            v = Sr.v
            v.tensor_tensor(out=ovl["Strue"], in0=bigE[:, 512 * (ch - 1):512 * ch],
                            in1=Tcur, op=ALU.add)
            vnop()
        Sr.dma(ovl["gng"].rearrange("p (j c) -> p j c", j=8),
               dr["gneg"][64 * 16 * ch:64 * 16 * (ch + 1), :]
               .rearrange("(j two r) c -> (two r) j c", j=8, two=2))
        Sr.dma(sb["wslab"][:, :], dr["oloc"][128 * ch:128 * (ch + 1), :])
        for wave in range(2):
            t_ = Sr.t
            for hh in range(8):
                h = 8 * wave + hh
                two, j = h % 2, h // 2
                St = ovl["Strue"][64 * two:64 * two + 64, 64 * j:64 * (j + 1)]
                qf_t = bigB[64 * two:64 * two + 64,
                            2048 * j + 128 * ch:2048 * j + 128 * (ch + 1)]
                t_.matmul(out=pd[:, 64 * hh:64 * (hh + 1)], lhsT=qf_t, rhs=St,
                          start=True, stop=True, tile_position=(64 * two, 0))
                t_.matmul(out=pe[:, 64 * hh:64 * (hh + 1)],
                          lhsT=ovl["gng"][64 * two:64 * two + 64, 128 * j:128 * (j + 1)],
                          rhs=St, start=True, stop=True, tile_position=(64 * two, 0))
            for hh in range(8):
                h = 8 * wave + hh
                v = Sr.v
                v.tensor_scalar(out=sb["opre"][:, :], in0=pd[:, 64 * hh:64 * (hh + 1)],
                                scalar1=sb["eAall"][:, 16 * ch + h:16 * ch + h + 1],
                                scalar2=None, op0=ALU.mult)
                v.tensor_tensor(out=sb["opre2"][:, :], in0=sb["opre"][:, :],
                                in1=pe[:, 64 * hh:64 * (hh + 1)], op=ALU.add)
                v.tensor_tensor(out=sb["opre"][:, :], in0=sb["opre2"][:, :],
                                in1=sb["wslab"][:, 64 * h:64 * (h + 1)], op=ALU.add)
                v.tensor_tensor(out=sb["tt1"][:, 0:64], in0=sb["opre"][:, :],
                                in1=sb["opre"][:, :], op=ALU.mult)
                v.tensor_reduce(out=sb["ln1"][:, :], in_=sb["tt1"][:, 0:64], axis=AX,
                                op=ALU.add)
                vnop()
                s_ = Sr.s
                s_.activation(out=sb["ln2"][:, :], in_=sb["ln1"][:, :], func=AF.Ln,
                              scale=1.0 / 64.0, bias=sb["ceps6"][:, 0:1])
                snop()
                s_ = Sr.s
                s_.activation(out=sb["ln1"][:, :], in_=sb["ln2"][:, :], func=AF.Exp,
                              scale=-0.5)
                snop()
                v = Sr.v
                v.tensor_scalar(out=sb["opre2"][:, :], in0=sb["opre"][:, :],
                                scalar1=sb["ln1"][:, 0:1], scalar2=None, op0=ALU.mult)
                vnop()
                v.tensor_tensor(out=sb["ot64"][:, :], in0=sb["opre2"][:, :],
                                in1=bigC[:, 1024 * ch + 64 * h:1024 * ch + 64 * (h + 1)],
                                op=ALU.mult)
                vnop()
                two = h % 2
                t_ = Sr.t
                t_.transpose(out=pt[64 * two:64 * two + 64, 0:128], in_=sb["ot64"][:, :],
                             identity=sb["ident"][:, :], tile_position=(0, 64 * two))
                v = Sr.v
                v.tensor_scalar(out=bigA[64 * two:64 * two + 64,
                                         2048 * (h // 2) + 128 * ch:
                                         2048 * (h // 2) + 128 * (ch + 1)],
                                in0=pt[64 * two:64 * two + 64, 0:128],
                                scalar1=sb["onwc"][64 * two:64 * two + 64, 0:1],
                                scalar2=None, op0=ALU.mult)
        load_ptq(ch)
        t_ = Sr.t
        for h in range(H):
            two, j = h % 2, h // 2
            t_.matmul(out=pf[64 * two:64 * two + 64, 64 * j:64 * (j + 1)],
                      lhsT=ovl["ptq"][64 * two:64 * two + 64, 128 * j:128 * j + 64],
                      rhs=Tcur[64 * two:64 * two + 64, 64 * j:64 * (j + 1)],
                      start=True, stop=True, tile_position=(64 * two, 64 * two))
        v = Sr.v
        v.tensor_copy(out=Tnxt, in_=pf)
        Tcur, Tnxt = Tnxt, Tcur


def _p8(Sr, sb, ps, d, dr, y_out, vw):
    pa = ps["pa"][:, :]
    bigA, bigD = sb["bigA"][:, :], sb["bigD"][:, :]

    def vnop(k=2):
        v = Sr.v
        for _ in range(k):
            v.engine_nop()

    for jc in range(8):
        Sr.dma(sb["xrow"][:, :], d["Wo"][128 * jc:128 * (jc + 1), :])
        v = Sr.v
        v.tensor_copy(out=vw["xb16"][:, :], in_=sb["xrow"][:, :])
        for dc in range(8):
            t_ = Sr.t
            t_.transpose(out=ps["pt"][:, 0:128], in_=vw["xb16"][:, 128 * dc:128 * (dc + 1)],
                         identity=sb["ident"][:, :])
            v = Sr.v
            v.tensor_copy(out=bigD[:, 1024 * dc + 128 * jc:1024 * dc + 128 * (jc + 1)],
                          in_=ps["pt"][:, 0:128])
    for tc in range(NCH):
        Sr.dma(sb["xrow"][:, :], d["x_sh"][HALO + 128 * tc:HALO + 128 * (tc + 1), :])
        Sr.dma(sb["wslab"][:, :], dr["gffn"][128 * tc:128 * (tc + 1), :])
        for js in range(2):
            t_ = Sr.t
            for dc in range(8):
                t_.matmul(out=pa, lhsT=bigA[:, 2048 * dc + 128 * tc:2048 * dc + 128 * (tc + 1)],
                          rhs=bigD[:, 1024 * dc + 512 * js:1024 * dc + 512 * (js + 1)],
                          start=(dc == 0), stop=False)
            t_.matmul(out=pa, lhsT=sb["ones1b"][0:1, 0:128], rhs=sb["bo16"][0:1, 512 * js:512 * (js + 1)],
                      start=False, stop=True)
            v = Sr.v
            v.tensor_tensor(out=sb["convo"][:, :], in0=pa, in1=sb["wslab"][:, 512 * js:512 * (js + 1)],
                            op=ALU.mult)
            v.tensor_tensor(out=sb["xt"][:, 512 * js:512 * (js + 1)], in0=sb["convo"][:, :],
                            in1=sb["xrow"][:, 512 * js:512 * (js + 1)], op=ALU.add)
        Sr.dma(y_out[128 * tc:128 * (tc + 1), :], sb["xt"][:, :])


def kernel(**inputs):
    if "nc" not in _CACHE:
        _CACHE["nc"] = build_nc()
    nc = _CACHE["nc"]
    maps = _host_maps(inputs)
    res = run_bass_kernel_spmd(nc, maps, core_ids=list(range(8)))
    out = np.zeros((B, S, D), np.float32)
    for i in range(8):
        b_, h_ = i % 4, i // 4
        out[b_, h_ * T:(h_ + 1) * T] = res.results[i]["y_sh"]
    return out


# version 11
# speedup vs baseline: 1.3166x; 1.1727x over previous
import contextlib
import numpy as np
import ml_dtypes
import concourse.bass as bass
import concourse.mybir as mybir
from concourse.bass_utils import run_bass_kernel_spmd

F32 = mybir.dt.float32
BF16 = mybir.dt.bfloat16
ALU = mybir.AluOpType
AF = mybir.ActivationFunctionType
AX = mybir.AxisListType.X

B, S, D, H, DH, BW = 4, 4096, 1024, 16, 64, 128
T, HALO, C = 2048, 3, 128
NCH = T // C
I_FFN = 4096
PI = float(np.pi)
TWO_PI = float(2 * np.pi)
THRQ = [0.25 ** 2, 0.75 ** 2, 1.25 ** 2, 1.75 ** 2, 2.5 ** 2, 3.5 ** 2, 5.0 ** 2]

_CACHE = {}


def _host_maps(inputs):
    x = np.ascontiguousarray(np.asarray(inputs["L0_attn_hidden"], np.float32))
    perm = np.concatenate([np.arange(0, BW, 2), np.arange(1, BW, 2)])
    cond_in_W = np.ascontiguousarray(np.asarray(inputs["L0_attn_cond_in_W"], np.float32)[:, perm])
    delta_W = np.ascontiguousarray(np.asarray(inputs["L0_attn_delta_W"], np.float32)[perm, :])
    cnw = np.ascontiguousarray(
        np.asarray(inputs["L0_attn_cond_norm_w"], np.float32)[perm].reshape(BW, 1))
    theta = np.asarray(inputs["L0_attn_rot_theta"], np.float32).reshape(1, 64)
    ctrl_W = np.ascontiguousarray(np.asarray(inputs["L0_attn_ctrl_W"], np.float32))
    ctrl_b = np.asarray(inputs["L0_attn_ctrl_b"], np.float32).reshape(-1)
    cb = np.zeros((128, 34), np.float32)
    for cc in range(24):
        cb[:, cc] = ctrl_b[128 * cc:128 * cc + 128]
    cb[0:16, 24] = ctrl_b[3072:3088]
    cb[0:16, 25] = ctrl_b[3088:3104]
    for g in range(8):
        cb[:, 26 + g] = ctrl_b[3104 + 128 * g:3104 + 128 * g + 128]
    ramp = np.asarray(inputs["L0_attn_qkv_ramp_logit"], np.float32)
    rampt = np.zeros((128, 24), np.float32)
    for kind in range(3):
        for fc in range(8):
            rampt[:, kind * 8 + fc] = ramp[kind, 128 * fc:128 * fc + 128]
    beta_ramp = np.asarray(inputs["L0_attn_beta_ramp_logit"], np.float32).reshape(H, 1)
    convw = np.zeros((128, 96), np.float32)
    for kind, key in enumerate(("q", "k", "v")):
        w = np.asarray(inputs[f"L0_attn_{key}_conv_w"], np.float32)
        for fc in range(8):
            convw[:, (kind * 8 + fc) * 4:(kind * 8 + fc) * 4 + 4] = \
                w[128 * fc:128 * fc + 128]
    onw2 = np.concatenate([np.asarray(inputs["L0_attn_out_norm_w"], np.float32).reshape(DH, 1)] * 2)
    Wo = np.ascontiguousarray(np.asarray(inputs["L0_attn_out_proj_W"], np.float32))
    bo = np.asarray(inputs["L0_attn_out_proj_b"], np.float32).reshape(1, D)
    gw = np.ascontiguousarray(np.asarray(inputs["L0_ffn_gate_w"], np.float32))
    uw = np.ascontiguousarray(np.asarray(inputs["L0_ffn_up_w"], np.float32))
    dw = np.ascontiguousarray(np.asarray(inputs["L0_ffn_down_w"], np.float32))
    s_in = float(np.asarray(inputs["L0_ffn_s_in"]).reshape(-1)[0])
    s_in_d = float(np.asarray(inputs["L0_ffn_s_in_down"]).reshape(-1)[0])

    cst = {}
    cst["ident"] = np.eye(128).astype(ml_dtypes.bfloat16)
    cst["ident_f"] = np.eye(128).astype(np.float32)
    tri = np.tril(np.ones((128, 128), np.float32))
    cst["trioneB"] = np.ascontiguousarray(tri.T).astype(ml_dtypes.bfloat16)
    cst["trioneF"] = np.ascontiguousarray(tri.T).astype(np.float32)
    jj, tt2 = np.meshgrid(np.arange(128), np.arange(128), indexing="ij")
    cst["maskMc"] = np.where(jj < tt2, 0.0, -1e30).astype(np.float32)
    cst["maskAc"] = np.where(jj <= tt2, 0.0, -1e30).astype(np.float32)
    cst["iotar"] = np.arange(1, 129, dtype=np.float32).reshape(1, 128)
    cst["ones1f"] = np.ones((1, 512), np.float32)
    cst["ones1b"] = np.ones((1, 512), ml_dtypes.bfloat16)
    cst["onescb"] = np.ones((128, 1), ml_dtypes.bfloat16)
    cst["i64c"] = np.eye(64).astype(np.float32)
    e127 = np.zeros((128, 1), np.float32); e127[127, 0] = 1.0
    cst["e127"] = e127
    e2 = np.zeros((128, 1), np.float32); e2[2, 0] = 1.0
    cst["e2"] = e2

    maps = []
    for i in range(8):
        b_, h_ = i % 4, i // 4
        xs = np.zeros((T + HALO, D), np.float32)
        if h_ == 0:
            xs[HALO:] = x[b_, 0:T]
        else:
            xs[:] = x[b_, h_ * T - HALO:h_ * T + T]
        sel = np.zeros((128, 8), np.float32)
        if h_ == 1:
            sel[:, i - 4] = 1.0
        m = dict(
            x_sh=xs, sel=sel,
            mask=np.full((128, 1), 1.0 if h_ == 1 else 0.0, np.float32),
            cond_in_W=cond_in_W, delta_W=delta_W, cnw=cnw, theta=theta,
            ctrl_W=ctrl_W, cbias=cb, rampq=rampt, beta_ramp=beta_ramp,
            convw=convw, onw2=onw2, Wo=Wo, bo=bo, gw=gw, uw=uw, dw=dw,
            s_in=np.full((128, 1), s_in, np.float32),
            s_in_d=np.full((128, 1), s_in_d, np.float32),
        )
        m.update(cst)
        maps.append(m)
    return maps


_DT_SIZE = {}
for _nm, _sz in (("float32", 4), ("int32", 4), ("uint32", 4), ("bfloat16", 2),
                 ("float16", 2), ("int16", 2), ("uint16", 2), ("int8", 1),
                 ("uint8", 1), ("float8_e4m3", 1), ("float8_e5m2", 1)):
    _d = getattr(mybir.dt, _nm, None)
    if _d is not None:
        _DT_SIZE[_d] = _sz

_READ, _WRITE = 0, 1
import os
SERIAL_FROM = int(os.environ.get('SERIAL_FROM', '9999'))
_CAP = 48


class Ser:

    QW = 4

    def __init__(self, nc, E):
        self.nc = nc
        self.E = E
        self.eng = {"T": nc.tensor, "V": nc.vector, "S": nc.scalar,
                    "Q": nc.sync, "G": nc.gpsimd}
        self.sems = {k: E(nc.semaphore("sem" + k)) for k in ("T", "V", "S")}
        self.cnt = {k: 0 for k in ("T", "V", "S")}
        for base in ("Q", "G"):
            for j in range(self.QW):
                k = base + str(j)
                self.sems[k] = E(nc.semaphore("sem" + k))
                self.cnt[k] = 0
        self.qidx = {"Q": 0, "G": 0}
        self.wm = {k: {} for k in self.eng}
        self.writes = {}
        self.reads = {}
        self.cn = 0
        self.n = 0
        self.last = None
        self.phase = 0

    def _box(self, ap):
        t = ap.tensor
        name = t.name
        esz = _DT_SIZE[t.dtype]
        pat = list(ap.ap)
        tname = type(t).__name__
        if tname == "DRamTensorHandle":
            lo = ap.offset * esz
            span = esz
            for s, n_ in pat:
                span += (n_ - 1) * abs(s) * esz
            return (name, 0, 1, lo, lo + span)
        shp = list(t.shape)
        re_ = 1
        for d in shp[1:]:
            re_ *= d
        rowb = re_ * esz
        if tname == "PSumTensorHandle":
            return (name, 0, 128, 0, rowb)
        p0 = ap.offset // re_ + getattr(t, "base_partition", 0)
        c0 = (ap.offset % re_) * esz
        dp = 0
        cspan = esz
        ok = True
        for s, n_ in pat:
            if n_ <= 1 or s == 0:
                continue
            if s % re_ == 0:
                dp += (n_ - 1) * (s // re_)
            else:
                cspan += (n_ - 1) * abs(s) * esz
        if c0 + cspan > rowb:
            return (name, 0, 128, 0, rowb)
        return (name, p0, p0 + dp + 1, c0, c0 + cspan)

    @staticmethod
    def _overlap(box, r):
        return box[1] < r[1] and r[0] < box[2] and box[3] < r[3] and r[2] < box[4]

    def _deps_from(self, recs, box, deps):
        for r in recs:
            if self._overlap(box, r):
                e, s_ = r[4], r[5]
                if deps.get(e, 0) < s_:
                    deps[e] = s_

    def _record(self, store, box, eng, seq):
        recs = store.setdefault(box[0], [])
        nb = [box[1], box[2], box[3], box[4], eng, seq]
        keep = [r for r in recs
                if not (nb[0] <= r[0] and r[1] <= nb[1]
                        and nb[2] <= r[2] and r[3] <= nb[3])]
        keep.append(nb)
        if len(keep) > _CAP:
            byeng = {}
            for r in keep:
                byeng[r[4]] = max(byeng.get(r[4], 0), r[5])
            keep = [[0, 128, 0, 1 << 30, e, s_] for e, s_ in byeng.items()]
        store[box[0]] = keep

    def _emit_waits(self, E_, deps, self_raw):
        q = self.eng[E_]
        wmE = self.wm[E_]
        for F, v in deps.items():
            if F == E_ and not self_raw:
                continue
            if wmE.get(F, 0) < v:
                q.wait_ge(self.sems[F] if F in self.sems else F, v)
                wmE[F] = v

    def _issue(self, E_, rd_boxes, wr_boxes):
        raw_deps, war_deps = {}, {}
        for b in rd_boxes:
            self._deps_from(self.writes.get(b[0], ()), b, raw_deps)
        for b in wr_boxes:
            self._deps_from(self.writes.get(b[0], ()), b, war_deps)
            self._deps_from(self.reads.get(b[0], ()), b, war_deps)
        deps = dict(war_deps)
        for e, s_ in raw_deps.items():
            if deps.get(e, 0) < s_:
                deps[e] = s_
        if self.phase >= SERIAL_FROM:
            for F, c in self.cnt.items():
                if c > 0 and deps.get(F, 0) < c:
                    deps[F] = c
        self._emit_waits(E_, deps, True)

    def _finish(self, E_, rd_boxes, wr_boxes, seq):
        for b in rd_boxes:
            self._record(self.reads, b, E_, seq)
        for b in wr_boxes:
            self._record(self.writes, b, E_, seq)

    def _collect(self, a, k, wr_boxes, rd_boxes):
        wout = k.get("out", None)
        if wout is not None:
            wr_boxes.append(self._box(wout))
        for key, val in k.items():
            if key == "out" or val is None:
                continue
            if hasattr(val, "tensor") and hasattr(val, "ap"):
                rd_boxes.append(self._box(val))
        for val in a:
            if hasattr(val, "tensor") and hasattr(val, "ap"):
                rd_boxes.append(self._box(val))

    def _wrap(self, e, E_):
        ser = self

        class W:
            def __getattr__(_s, name):
                if name in ("nop", "engine_nop"):
                    return lambda *a, **k: None
                fn = getattr(e, name)
                if name in ("wait_ge", "then_inc"):
                    return fn

                def call(*a, **k):
                    wr, rd = [], []
                    if name == "memset":
                        wr.append(ser._box(a[0]))
                        ser._collect(a[1:], k, wr, rd)
                    else:
                        ser._collect(a, k, wr, rd)
                    if name == "matmul" and not k.get("start", True):
                        rd.extend(wr)
                    ser._issue(E_, rd, wr)
                    r = fn(*a, **k)
                    r.then_inc(ser.sems[E_], 1)
                    ser.cnt[E_] += 1
                    ser._finish(E_, rd, wr, ser.cnt[E_])
                    return r
                return call
        return W()

    @property
    def v(self):
        return self._wrap(self.eng["V"], "V")

    @property
    def s(self):
        return self._wrap(self.eng["S"], "S")

    @property
    def t(self):
        return self._wrap(self.eng["T"], "T")

    def dma(self, out, in_, cast=False):
        E_ = "G" if cast else "Q"
        e = self.eng[E_]
        key = E_ + str(self.qidx[E_] % self.QW)
        self.qidx[E_] += 1
        wr = [self._box(out)]
        rd = [self._box(in_)]
        self._issue(E_, rd, wr)
        if self.cnt[key] > 0 and self.wm[E_].get(key, 0) < self.cnt[key]:
            e.wait_ge(self.sems[key], self.cnt[key])
            self.wm[E_][key] = self.cnt[key]
        e.dma_start(out=out, in_=in_).then_inc(self.sems[key], 16)
        self.cnt[key] += 16
        self._finish(key, rd, wr, self.cnt[key])

    def cc(self, kind, ins, outs):
        e = self.eng["G"]
        csem = self.E(self.nc.semaphore("ccsem%d" % self.cn))
        self.cn += 1
        wr = [self._box(o) for o in outs]
        rd = [self._box(i) for i in ins]
        self._issue("G", rd, wr)
        e.collective_compute(kind, ALU.bypass, replica_groups=[list(range(8))],
                             ins=[i.opt() for i in ins],
                             outs=[o.opt() for o in outs]).then_inc(csem, 1)
        self.sems[csem] = csem
        self.cnt[csem] = 1
        self._finish(csem, rd, wr, 1)

    def _emit_waits_all(self, q):
        for F, c in self.cnt.items():
            if c > 0:
                q.wait_ge(self.sems[F], c)


def emit_quant(Sr, at, outq, N, q, gs6, gsi, bf16_s):
    v = Sr.v
    nb = N // 16

    def nop():
        v.engine_nop()

    av_all = at.rearrange("p (b k) -> p b k", k=16)
    v.tensor_reduce(out=q["b0"][:, :nb], in_=av_all, axis=AX, op=ALU.max,
                    apply_absolute_value=True)
    nop(); nop()
    v.tensor_scalar(out=q["b1"][:, :nb], in0=q["b0"][:, :nb], scalar1=gs6,
                    scalar2=448.0, op0=ALU.mult, op1=ALU.min)
    nop()
    v.memset(q["b2"][:, :nb], 2.0 ** -9)
    pp, cur = [q["b2"], q["b3"]], 0
    for i in range(-5, 9):
        v.tensor_scalar(out=q["b4"][:, :nb], in0=q["b1"][:, :nb],
                        scalar1=float(2.0 ** i), scalar2=1.0, op0=ALU.is_ge, op1=ALU.add)
        nop()
        v.tensor_tensor(out=pp[1 - cur][:, :nb], in0=pp[cur][:, :nb],
                        in1=q["b4"][:, :nb], op=ALU.mult)
        nop()
        cur = 1 - cur
    q2, spare = pp[cur], pp[1 - cur]
    v.reciprocal(out=q["b4"][:, :nb], in_=q2[:, :nb])
    nop()
    v.tensor_mul(out=spare[:, :nb], in0=q["b4"][:, :nb], in1=q["b1"][:, :nb])
    nop()
    v.memset(q["b1"][:, :nb], 0.0)
    pp2, cur2 = [q["b1"], q["b4"]], 0
    for j in range(16):
        v.scalar_tensor_tensor(out=pp2[1 - cur2][:, :nb], in0=spare[:, :nb],
                               scalar=j + 0.5, in1=pp2[cur2][:, :nb],
                               op0=ALU.is_ge, op1=ALU.add)
        nop()
        cur2 = 1 - cur2
    acc = pp2[cur2]
    v.tensor_tensor(out=spare[:, :nb], in0=acc[:, :nb], in1=q2[:, :nb], op=ALU.mult)
    nop()
    v.tensor_scalar(out=q["b0"][:, :nb], in0=spare[:, :nb], scalar1=gsi,
                    scalar2=None, op0=ALU.mult)
    nop()
    v.tensor_scalar(out=q["b4"][:, :nb], in0=q["b0"][:, :nb], scalar1=1e-30,
                    scalar2=None, op0=ALU.max)
    nop()
    v.reciprocal(out=q["b1"][:, :nb], in_=q["b4"][:, :nb])
    nop(); nop()
    for hh in range(N // 512):
        c0, nb0 = 512 * hh, 32 * hh
        av = at[:, c0:c0 + 512].rearrange("p (b k) -> p b k", k=16)
        uv = q["u"].rearrange("p (b k) -> p b k", k=16)
        v.tensor_tensor(out=uv, in0=av,
                        in1=q["b1"][:, nb0:nb0 + 32].unsqueeze(2).broadcast_to([128, 32, 16]),
                        op=ALU.mult)
        sdst = q["sb"] if bf16_s else q["s"]
        v.tensor_tensor(out=sdst, in0=q["u"], in1=q["u"], op=ALU.mult)
        v.tensor_scalar(out=q["qA"], in0=sdst, scalar1=THRQ[0], scalar2=None, op0=ALU.is_ge)
        pa_, cu = [q["qA"], q["qC"]], 0
        for thr in THRQ[1:4]:
            v.scalar_tensor_tensor(out=pa_[1 - cu], in0=sdst, scalar=thr,
                                   in1=pa_[cu], op0=ALU.is_ge, op1=ALU.add)
            cu = 1 - cu
        qa, sparea = pa_[cu], pa_[1 - cu]
        v.tensor_scalar(out=q["qB"], in0=sdst, scalar1=THRQ[4], scalar2=None, op0=ALU.is_ge)
        pb_, cu2 = [q["qB"], q["qD"]], 0
        for thr in (THRQ[5], THRQ[6], THRQ[6]):
            v.scalar_tensor_tensor(out=pb_[1 - cu2], in0=sdst, scalar=thr,
                                   in1=pb_[cu2], op0=ALU.is_ge, op1=ALU.add)
            cu2 = 1 - cu2
        qb = pb_[cu2]
        v.scalar_tensor_tensor(out=sparea, in0=qa, scalar=0.5, in1=qb,
                               op0=ALU.mult, op1=ALU.add)
        v.tensor_scalar(out=q["qB"], in0=q["u"], scalar1=0.0, scalar2=2.0,
                        op0=ALU.is_ge, op1=ALU.mult)
        qv = q["qd"].rearrange("p (b k) -> p b k", k=16)
        v.tensor_tensor(out=qv, in0=sparea.rearrange("p (b k) -> p b k", k=16),
                        in1=q["b0"][:, nb0:nb0 + 32].unsqueeze(2).broadcast_to([128, 32, 16]),
                        op=ALU.mult)
        v.tensor_scalar(out=q["qD"], in0=q["qB"], scalar1=1.0, scalar2=None, op0=ALU.subtract)
        nop()
        v.tensor_tensor(out=outq[:, c0:c0 + 512], in0=q["qd"], in1=q["qD"], op=ALU.mult)


def build_nc():
    nc = bass.Bass("TRN2", target_bir_lowering=False)
    stk = contextlib.ExitStack()
    E = stk.enter_context

    d = {}
    specs = dict(
        x_sh=([T + HALO, D], F32), sel=([128, 8], F32), mask=([128, 1], F32),
        cond_in_W=([D, BW], F32), delta_W=([BW, 3 * D + H], F32), cnw=([BW, 1], F32),
        theta=([1, 64], F32), ctrl_W=([D, 4128], F32), cbias=([128, 34], F32),
        rampq=([128, 24], F32), beta_ramp=([H, 1], F32), convw=([128, 96], F32),
        onw2=([128, 1], F32), Wo=([D, D], F32), bo=([1, D], F32),
        gw=([I_FFN, D], F32), uw=([I_FFN, D], F32), dw=([D, I_FFN], F32),
        s_in=([128, 1], F32), s_in_d=([128, 1], F32),
        ident=([128, 128], BF16), ident_f=([128, 128], F32),
        trioneB=([128, 128], BF16), trioneF=([128, 128], F32),
        maskMc=([128, 128], F32), maskAc=([128, 128], F32), iotar=([1, 128], F32),
        ones1f=([1, 512], F32), ones1b=([1, 512], BF16), onescb=([128, 1], BF16),
        i64c=([64, 64], F32), e127=([128, 1], F32), e2=([128, 1], F32),
    )
    for k, (shp, dt) in specs.items():
        d[k] = nc.dram_tensor(k, shp, dt, kind="ExternalInput").ap()
    y_out = nc.dram_tensor("y_sh", [T, D], F32, kind="ExternalOutput").ap()

    dr = dict(
        oloc=nc.dram_tensor("oloc_d", [T, D], BF16).ap(),
        gneg=nc.dram_tensor("gneg_d", [NCH * H * 64, 128], BF16).ap(),
        ptqc=nc.dram_tensor("ptqc_d", [NCH * H * 64, 128], BF16).ap(),
        gffn=nc.dram_tensor("gffn_d", [T, D], BF16).ap(),
        hq=nc.dram_tensor("hq_d", [T, I_FFN], BF16).ap(),
        gwq=nc.dram_tensor("gwq_d", [D, I_FFN], BF16).ap(),
        uwq=nc.dram_tensor("uwq_d", [D, I_FFN], BF16).ap(),
        dwq=nc.dram_tensor("dwq_d", [I_FFN, D], BF16).ap(),
    )
    dr["cc1i"] = nc.dram_tensor("cc1_in", [4, 128], F32)
    dr["cc1o"] = nc.dram_tensor("cc1_out", [32, 128], F32)
    dr["cc2i"] = nc.dram_tensor("cc2_in", [H * 64, 64], F32)
    dr["cc2o"] = nc.dram_tensor("cc2_out", [8 * H * 64, 64], F32)

    sb = {}

    def A(name, shape, dt):
        sb[name] = E(nc.sbuf_tensor("s_" + name, shape, dt))

    A("bigA", [128, 16384], BF16)
    A("bigB", [128, 16384], BF16)
    A("bigC", [128, 16384], BF16)
    A("bigD", [128, 16384], BF16)
    A("bigE", [128, 8192], BF16)
    A("cndf", [128, 2048], BF16)
    A("condW", [128, 1024], BF16)
    A("deltW", [128, 3088], BF16)
    A("convw", [128, 96], F32)
    A("rampq", [128, 24], F32)
    A("rampb", [16, 1], F32)
    A("cbias", [128, 34], F32)
    A("tabc", [128, 64], F32)
    A("tabs", [128, 64], F32)
    A("wslab", [128, 1024], BF16)
    A("xt", [128, 1024], F32)
    A("xrow", [128, 1024], F32)
    A("selw", [128, 8], F32)
    A("maskc", [128, 1], F32)
    A("gs6", [128, 1], F32); A("gsi", [128, 1], F32)
    A("gs6d", [128, 1], F32); A("gsid", [128, 1], F32)
    A("gs6w", [128, 1], F32); A("gsiw", [128, 1], F32)
    A("sc1", [128, 1], F32); A("sc2", [128, 1], F32); A("sc3", [128, 1], F32)
    A("ceps5", [128, 1], F32); A("ceps6", [128, 1], F32); A("cone", [128, 1], F32)
    A("onescf", [128, 1], F32); A("lamC", [1, 16], F32); A("e127", [128, 1], F32)
    A("e2", [128, 1], F32)
    A("sc4", [128, 1], F32)
    A("ln1", [128, 1], F32); A("ln2", [128, 1], F32)
    A("iotar", [1, 128], F32)
    A("trioneB", [128, 128], BF16)
    A("trioneF", [128, 128], F32)
    A("ident", [128, 128], BF16)
    A("ident_f", [128, 128], F32)
    A("maskM", [128, 128], F32)
    A("maskA", [128, 128], F32)
    A("ones1f", [1, 512], F32)
    A("ones1b", [1, 512], BF16)
    A("onescb", [128, 1], BF16)
    A("i64b", [64, 64], F32)
    A("onwc", [128, 1], F32)
    A("bo16", [1, 1024], BF16)
    A("thetar", [1, 64], F32)
    A("z0row", [1, 128], F32)
    A("tt1", [128, 128], F32); A("tt2", [128, 128], F32)
    A("tt3", [128, 128], F32); A("tt4", [128, 64], F32)
    A("upk", [128, 128], BF16)
    A("haloz", [4, 128], F32); A("halo2", [4, 128], F32)
    A("condh", [128, 3], BF16)
    A("xnhf", [128, 24], BF16)
    A("pre", [128, 520], F32)
    A("tails", [128, 72], F32)
    A("convo", [128, 512], F32)
    A("hid", [128, 512], F32)
    A("dbt", [128, 128], F32)
    A("lng", [128, 16], F32); A("lnb", [128, 16], F32); A("beta", [128, 16], F32)
    A("lam", [128, 16], F32); A("l1q", [128, 16], F32); A("l1k", [128, 16], F32)
    A("rowM", [128, 16], F32); A("rowA", [128, 16], F32); A("colv", [128, 16], F32)
    A("coefk", [128, 16], F32); A("coefp", [128, 16], F32)
    A("t16a", [128, 16], F32); A("t16b", [128, 16], F32)
    A("eAall", [128, 256], F32)
    A("gCc", [64, 16], F32)
    A("pk2a", [128, 2], F32); A("pk2b", [128, 2], F32); A("pk2c", [128, 2], F32)
    A("stk2", [2, 128], F32); A("stk2r", [2, 128], F32); A("stk2q", [2, 128], F32)
    A("sq16", [128, 128], BF16)
    A("AqS", [128, 512], BF16)
    A("Ua", [128, 512], BF16); A("Ub", [128, 512], BF16)
    A("Pa", [128, 512], BF16); A("Pb", [128, 512], BF16)
    A("PT", [128, 512], BF16)
    A("kpt", [128, 256], BF16)
    A("t64", [64, 64], F32)
    A("pk64", [64, 512], BF16)
    A("opre", [128, 64], F32); A("opre2", [128, 64], F32)
    A("ot64", [128, 64], BF16)

    ps = {}
    for name, dt in (("pa", F32), ("pb", F32), ("pc", F32), ("pd", F32),
                     ("pe", F32), ("pf", F32)):
        ps[name] = E(nc.psum_tensor(name, [128, 512], dt))
    ps["pt"] = E(nc.psum_tensor("pt", [128, 512], BF16))

    blk = E(nc.Block())

    @blk.sync
    def _(sync):
        Sr = Ser(nc, E)
        _emit(nc, Sr, d, y_out, sb, ps, dr)
        Sr._emit_waits_all(nc.sync)

    stk.close()
    return nc


def _emit(nc, Sr, d, y_out, sb, ps, dr):
    pa, pb, pc, pd, pe, pf = (ps[k][:, :] for k in ("pa", "pb", "pc", "pd", "pe", "pf"))
    pt = ps["pt"][:, :]
    bigA, bigB, bigC, bigD, bigE = (sb[k][:, :] for k in
                                    ("bigA", "bigB", "bigC", "bigD", "bigE"))
    bigC_f = bigC.bitcast(F32)
    bigD_f = bigD.bitcast(F32)
    xt2v = bigD_f[:, 0:1024]
    vw = {"qu": bigD_f[:, 1024:1536], "qqd": bigD_f[:, 1536:2048],
          "qs": bigD_f[:, 2048:2560], "qqA": bigD_f[:, 2560:3072],
          "qqB": bigD_f[:, 3072:3584], "qqC": bigD_f[:, 3584:4096],
          "qqD": bigD_f[:, 4096:4608],
          "qb0": bigD_f[:, 4608:4672], "qb1": bigD_f[:, 4672:4736],
          "qb2": bigD_f[:, 4736:4800], "qb3": bigD_f[:, 4800:4864],
          "qb4": bigD_f[:, 4864:4928],
          "convb": bigD[:, 9856:10368], "xb16": bigD[:, 10368:11392]}
    p_sb = bigC_f[:, 0:2048]
    z_sb = bigC_f[:, 2048:4096]
    ident = sb["ident"][:, :]
    identf = sb["ident_f"][:, :]

    def ap(n):
        return sb[n][:, :]

    qd = dict(b0=vw["qb0"], b1=vw["qb1"], b2=vw["qb2"], b3=vw["qb3"], b4=vw["qb4"],
              u=vw["qu"], qd=vw["qqd"], s=vw["qs"],
              sb=vw["qs"].bitcast(BF16)[:, 0:512],
              qA=vw["qqA"], qB=vw["qqB"], qC=vw["qqC"], qD=vw["qqD"])

    def vnop(k=2):
        v = Sr.v
        for _ in range(k):
            v.engine_nop()

    def snop():
        s_ = Sr.s
        s_.nop(); s_.nop()

    def idslice(base, n):
        return sb["ident"][base:base + n, base:base + n]

    def idslicef(base, n):
        return sb["ident_f"][base:base + n, base:base + n]

    v = Sr.v
    v.memset(sb["ceps5"][:, :], 1e-5)
    v.memset(sb["ceps6"][:, :], 1e-6)
    v.memset(sb["cone"][:, :], 1.0)
    v.memset(sb["onescf"][:, :], 1.0)
    v.memset(sb["pk2a"][:, :], 1.0)
    v.memset(sb["pk2b"][:, :], 1.0)
    v.memset(sb["pk2c"][:, :], 1.0)
    vnop()
    for nm in ("ident", "ident_f", "trioneB", "trioneF", "iotar", "ones1f",
               "ones1b", "onescb", "convw", "cbias", "rampq"):
        Sr.dma(ap(nm), d[nm])
    Sr.dma(ap("selw"), d["sel"])
    Sr.dma(ap("e127"), d["e127"])
    Sr.dma(ap("e2"), d["e2"])
    Sr.dma(ap("maskM"), d["maskMc"])
    Sr.dma(ap("maskA"), d["maskAc"])
    Sr.dma(ap("i64b"), d["i64c"])
    Sr.dma(ap("maskc"), d["mask"])
    Sr.dma(ap("thetar"), d["theta"])
    Sr.dma(sb["onwc"][:, :], d["onw2"])
    Sr.dma(sb["condW"][:, :].rearrange("p (dc w) -> p dc w", dc=8),
           d["cond_in_W"].rearrange("(dc p) w -> p dc w", p=128), cast=True)
    Sr.dma(ap("sc1"), d["cnw"])
    for c0, w in ((0, 1024), (1024, 1024), (2048, 1024), (3072, 16)):
        Sr.dma(sb["xt"][:, 0:w], d["delta_W"][:, c0:c0 + w])
        v = Sr.v
        v.tensor_scalar(out=sb["deltW"][:, c0:c0 + w], in0=sb["xt"][:, 0:w],
                        scalar1=sb["sc1"][:, 0:1], scalar2=None, op0=ALU.mult)
        vnop()
    s_ = Sr.s
    s_.activation(out=xt2v[:, 0:24], in_=sb["rampq"][:, 0:24], func=AF.Tanh, scale=0.5)
    snop()
    v = Sr.v
    v.tensor_scalar(out=ap("rampq"), in0=xt2v[:, 0:24], scalar1=0.5,
                    scalar2=0.5, op0=ALU.mult, op1=ALU.add)
    vnop()
    Sr.dma(sb["xt"][0:16, 0:1], d["beta_ramp"])
    s_ = Sr.s
    s_.activation(out=xt2v[0:16, 0:1], in_=sb["xt"][0:16, 0:1], func=AF.Tanh, scale=0.5)
    snop()
    v = Sr.v
    v.tensor_scalar(out=ap("rampb"), in0=xt2v[0:16, 0:1], scalar1=0.5,
                    scalar2=0.5, op0=ALU.mult, op1=ALU.add)
    vnop()
    Sr.dma(ap("sc1"), d["s_in"])
    Sr.dma(ap("sc2"), d["s_in_d"])
    v = Sr.v
    v.tensor_scalar(out=ap("gs6"), in0=ap("sc1"), scalar1=1.0 / 6.0, scalar2=None, op0=ALU.mult)
    v.tensor_scalar(out=ap("gs6d"), in0=ap("sc2"), scalar1=1.0 / 6.0, scalar2=None, op0=ALU.mult)
    vnop()
    v.reciprocal(out=ap("gsi"), in_=ap("sc1"))
    vnop()
    v.reciprocal(out=ap("gsid"), in_=ap("sc2"))
    vnop()
    Sr.dma(sb["bo16"][:, :], d["bo"], cast=True)
    t_ = Sr.t
    t_.matmul(out=pf[:, 0:64], lhsT=sb["iotar"][0:1, :], rhs=ap("thetar"),
              start=True, stop=True)
    _rot_table(Sr, sb, pf[:, 0:64], "tabs", 0.0)
    t_ = Sr.t
    t_.matmul(out=pf[:, 64:128], lhsT=sb["iotar"][0:1, :], rhs=ap("thetar"),
              start=True, stop=True)
    _rot_table(Sr, sb, pf[:, 64:128], "tabc", PI / 2.0)

    Sr.phase = 10
    for tc in range(NCH):
        Sr.dma(ap("xt"), d["x_sh"][HALO + 128 * tc:HALO + 128 * (tc + 1), :])
        _ln(Sr, sb, ap("xt"), xt2v, 128)
        v = Sr.v
        v.tensor_copy(out=vw["xb16"], in_=xt2v)
        _tr_to(Sr, sb, ps, vw["xb16"], bigA, tc, 8)
        emit_quant(Sr, xt2v, vw["xb16"], 1024, qd,
                   sb["gs6"][:, 0:1], sb["gsi"][:, 0:1], bf16_s=False)
        _tr_to(Sr, sb, ps, vw["xb16"], bigB, tc, 8)
    Sr.dma(sb["xt"][0:3, :], d["x_sh"][0:HALO, :])
    _ln(Sr, sb, sb["xt"][0:3, :], sb["xt"][0:3, :], 3)
    v = Sr.v
    v.tensor_copy(out=vw["xb16"][0:3, :], in_=sb["xt"][0:3, :])
    vnop()
    for dc in range(8):
        t_ = Sr.t
        t_.transpose(out=ps["pt"][:, 0:3], in_=vw["xb16"][0:3, 128 * dc:128 * (dc + 1)],
                     identity=idslice(0, 3))
        v = Sr.v
        v.tensor_copy(out=sb["xnhf"][:, 3 * dc:3 * dc + 3], in_=ps["pt"][:, 0:3])
        vnop()

    Sr.phase = 20
    for tc in range(NCH):
        t_ = Sr.t
        for dc in range(8):
            t_.matmul(out=pf[:, 0:128],
                      lhsT=bigA[:, 2048 * dc + 128 * tc:2048 * dc + 128 * (tc + 1)],
                      rhs=sb["condW"][:, 128 * dc:128 * (dc + 1)],
                      start=(dc == 0), stop=(dc == 7))
        v = Sr.v
        v.tensor_copy(out=p_sb[:, 128 * tc:128 * (tc + 1)], in_=pf[:, 0:128])
    v = Sr.v
    v.memset(sb["z0row"][:, :], 0.0)
    vnop()
    _rot_pass(Sr, sb, ps, p_sb, z_sb)
    Sr.dma(dr["cc1i"].ap()[0:3, :], z_sb[125:128, 128 * 15:128 * 16])
    Sr.dma(dr["cc1i"].ap()[3:4, :], z_sb[127:128, 128 * 15:128 * 16])
    Sr.cc("AllGather", [dr["cc1i"].ap()], [dr["cc1o"].ap()])
    Sr.dma(sb["xt"][0:4, 0:1024].rearrange("p (r c) -> p r c", c=128),
           dr["cc1o"].ap().rearrange("(r p) c -> p r c", p=4))
    v = Sr.v
    v.memset(sb["haloz"][0:3, :], 0.0)
    vnop()
    accp, cur = [sb["haloz"], sb["halo2"]], 0
    for r in range(8):
        v.scalar_tensor_tensor(out=accp[1 - cur][0:3, :],
                               in0=sb["xt"][0:3, 128 * r:128 * (r + 1)],
                               scalar=sb["selw"][0:3, r:r + 1], in1=accp[cur][0:3, :],
                               op0=ALU.mult, op1=ALU.add)
        vnop()
        cur = 1 - cur
    zrecv = accp[cur][0:3, :]
    zscr = accp[1 - cur]
    t_ = Sr.t
    t_.matmul(out=ps["pf"][0:1, 128:256], lhsT=sb["e2"][0:3, 0:1], rhs=zrecv,
              start=True, stop=True)
    v = Sr.v
    v.tensor_copy(out=sb["z0row"][:, :], in_=ps["pf"][0:1, 128:256])
    vnop()
    v.tensor_tensor(out=zscr[0:3, :], in0=zrecv, in1=zrecv, op=ALU.mult)
    vnop()
    v.tensor_reduce(out=sb["ln1"][0:3, :], in_=zscr[0:3, :], axis=AX, op=ALU.add)
    vnop()
    s_ = Sr.s
    s_.activation(out=sb["ln2"][0:3, :], in_=sb["ln1"][0:3, :], func=AF.Ln,
                  scale=1.0 / 128.0, bias=sb["ceps6"][0:3, 0:1])
    snop()
    s_ = Sr.s
    s_.activation(out=sb["ln1"][0:3, :], in_=sb["ln2"][0:3, :], func=AF.Exp, scale=-0.5)
    snop()
    v = Sr.v
    v.tensor_scalar(out=zscr[0:3, :], in0=zrecv, scalar1=sb["ln1"][0:3, 0:1],
                    scalar2=None, op0=ALU.mult)
    vnop()
    v.tensor_copy(out=vw["xb16"][0:3, 0:128], in_=zscr[0:3, :])
    vnop()
    t_ = Sr.t
    t_.transpose(out=ps["pt"][:, 0:3], in_=vw["xb16"][0:3, 0:128], identity=idslice(0, 3))
    v = Sr.v
    v.tensor_copy(out=ap("condh"), in_=ps["pt"][:, 0:3])
    vnop()
    _rot_pass(Sr, sb, ps, p_sb, z_sb)
    for tc in range(NCH):
        zc = z_sb[:, 128 * tc:128 * (tc + 1)]
        v = Sr.v
        v.tensor_tensor(out=ap("tt1"), in0=zc, in1=zc, op=ALU.mult)
        v.tensor_reduce(out=ap("ln1"), in_=ap("tt1"), axis=AX, op=ALU.add)
        vnop()
        s_ = Sr.s
        s_.activation(out=ap("ln2"), in_=ap("ln1"), func=AF.Ln, scale=1.0 / 128.0,
                      bias=sb["ceps6"][:, 0:1])
        snop()
        s_ = Sr.s
        s_.activation(out=ap("ln1"), in_=ap("ln2"), func=AF.Exp, scale=-0.5)
        snop()
        v = Sr.v
        v.tensor_scalar(out=ap("upk"), in0=zc, scalar1=sb["ln1"][:, 0:1],
                        scalar2=None, op0=ALU.mult)
        vnop()
        t_ = Sr.t
        t_.transpose(out=pt[:, 0:128], in_=ap("upk"), identity=ident)
        v = Sr.v
        v.tensor_copy(out=sb["cndf"][:, 128 * tc:128 * (tc + 1)], in_=pt[:, 0:128])
        vnop()

    Sr.phase = 30
    _ffn(Sr, sb, ps, d, dr, qd, vw)

    for sl in range(4):
        Sr.phase = 100 * (sl + 1)
        _p4_slab(Sr, sb, ps, d, dr, sl, vw)
        Sr.phase = 100 * (sl + 1) + 90
        _p5_slab(Sr, sb, ps, d, dr, sl, vw)

    Sr.phase = 600
    _p7(Sr, sb, ps, d, dr)

    Sr.phase = 610
    _p8(Sr, sb, ps, d, dr, y_out, vw)


def _rot_table(Sr, sb, psrc, dst, phase):
    v = Sr.v
    v.tensor_scalar(out=sb["tt2"][:, 0:64], in0=psrc, scalar1=phase, scalar2=None, op0=ALU.add)
    v.engine_nop(); v.engine_nop()
    v.tensor_scalar(out=sb["tt3"][:, 0:64], in0=sb["tt2"][:, 0:64],
                    scalar1=1.0 / TWO_PI, scalar2=None, op0=ALU.mult)
    v.engine_nop(); v.engine_nop()
    v.memset(sb["tt1"][:, 0:64], 0.0)
    accp, cur = [sb["tt1"], sb["tt4"]], 0
    for k in range(1, 12):
        v.scalar_tensor_tensor(out=accp[1 - cur][:, 0:64], in0=sb["tt3"][:, 0:64],
                               scalar=float(k), in1=accp[cur][:, 0:64],
                               op0=ALU.is_ge, op1=ALU.add)
        v.engine_nop()
        cur = 1 - cur
    fl = accp[cur]
    v.scalar_tensor_tensor(out=sb["tt3"][:, 0:64], in0=fl[:, 0:64], scalar=-TWO_PI,
                           in1=sb["tt2"][:, 0:64], op0=ALU.mult, op1=ALU.add)
    v.engine_nop(); v.engine_nop()
    v.tensor_scalar(out=sb["tt1"][:, 0:64], in0=sb["tt3"][:, 0:64], scalar1=PI,
                    scalar2=-TWO_PI, op0=ALU.is_ge, op1=ALU.mult)
    v.engine_nop(); v.engine_nop()
    v.tensor_tensor(out=sb["tt2"][:, 0:64], in0=sb["tt3"][:, 0:64],
                    in1=sb["tt1"][:, 0:64], op=ALU.add)
    v.engine_nop(); v.engine_nop()
    s_ = Sr.s
    s_.activation(out=sb[dst][:, :], in_=sb["tt2"][:, 0:64], func=AF.Sin)
    s_.nop(); s_.nop()


def _ln(Sr, sb, src, dst, P):
    v = Sr.v
    v.tensor_reduce(out=sb["ln1"][0:P, :], in_=src, axis=AX, op=ALU.add)
    v.engine_nop(); v.engine_nop()
    v.tensor_scalar(out=sb["ln2"][0:P, :], in0=sb["ln1"][0:P, :], scalar1=1.0 / 1024.0,
                    scalar2=None, op0=ALU.mult)
    v.engine_nop(); v.engine_nop()
    v.tensor_scalar(out=sb["xrow"][0:P, :], in0=src, scalar1=sb["ln2"][0:P, 0:1],
                    scalar2=None, op0=ALU.subtract)
    v.engine_nop(); v.engine_nop()
    v.tensor_tensor(out=sb["convo"][0:P, :], in0=sb["xrow"][0:P, 0:512],
                    in1=sb["xrow"][0:P, 0:512], op=ALU.mult)
    v.tensor_tensor(out=sb["hid"][0:P, :], in0=sb["xrow"][0:P, 512:1024],
                    in1=sb["xrow"][0:P, 512:1024], op=ALU.mult)
    v.tensor_reduce(out=sb["sc3"][0:P, :], in_=sb["convo"][0:P, :], axis=AX, op=ALU.add)
    v.tensor_reduce(out=sb["sc4"][0:P, :], in_=sb["hid"][0:P, :], axis=AX, op=ALU.add)
    v.engine_nop(); v.engine_nop()
    v.tensor_tensor(out=sb["ln1"][0:P, :], in0=sb["sc3"][0:P, :], in1=sb["sc4"][0:P, :],
                    op=ALU.add)
    v.engine_nop(); v.engine_nop()
    s_ = Sr.s
    s_.activation(out=sb["ln2"][0:P, :], in_=sb["ln1"][0:P, :], func=AF.Ln,
                  scale=1.0 / 1024.0, bias=sb["ceps5"][0:P, 0:1])
    s_.nop(); s_.nop()
    s_ = Sr.s
    s_.activation(out=sb["ln1"][0:P, :], in_=sb["ln2"][0:P, :], func=AF.Exp, scale=-0.5)
    s_.nop(); s_.nop()
    v = Sr.v
    v.tensor_scalar(out=dst, in0=sb["xrow"][0:P, :], scalar1=sb["ln1"][0:P, 0:1],
                    scalar2=None, op0=ALU.mult)
    v.engine_nop(); v.engine_nop()


def _tr_to(Sr, sb, ps, src_bf, dst, tc, ndc):
    for dc in range(ndc):
        t_ = Sr.t
        t_.transpose(out=ps["pt"][:, 0:128], in_=src_bf[:, 128 * dc:128 * (dc + 1)],
                     identity=sb["ident"][:, :])
        v = Sr.v
        v.tensor_copy(out=dst[:, 2048 * dc + 128 * tc:2048 * dc + 128 * (tc + 1)],
                      in_=ps["pt"][:, 0:128])


def _rot_pass(Sr, sb, ps, p_sb, z_sb):
    pf = ps["pf"][:, :]
    for tc in range(NCH):
        pcv = p_sb[:, 128 * tc:128 * (tc + 1)]
        re, im = pcv[:, 0:64], pcv[:, 64:128]
        v = Sr.v
        v.tensor_tensor(out=sb["tt1"][:, 0:64], in0=sb["tabc"][:, :], in1=re, op=ALU.mult)
        v.tensor_tensor(out=sb["tt2"][:, 0:64], in0=sb["tabs"][:, :], in1=im, op=ALU.mult)
        v.tensor_tensor(out=sb["upk"][:, 0:64], in0=sb["tt1"][:, 0:64],
                        in1=sb["tt2"][:, 0:64], op=ALU.add)
        v.tensor_tensor(out=sb["tt1"][:, 0:64], in0=sb["tabc"][:, :], in1=im, op=ALU.mult)
        v.tensor_tensor(out=sb["tt2"][:, 0:64], in0=sb["tabs"][:, :], in1=re, op=ALU.mult)
        v.tensor_tensor(out=sb["upk"][:, 64:128], in0=sb["tt1"][:, 0:64],
                        in1=sb["tt2"][:, 0:64], op=ALU.subtract)
        t_ = Sr.t
        t_.matmul(out=pf[:, 0:128], lhsT=sb["trioneB"][:, :], rhs=sb["upk"][:, :],
                  start=True, stop=False)
        t_.matmul(out=pf[:, 0:128], lhsT=sb["ones1f"][0:1, 0:128], rhs=sb["z0row"][:, :],
                  start=False, stop=True)
        v = Sr.v
        v.tensor_tensor(out=sb["tt1"][:, 0:64], in0=sb["tabc"][:, :], in1=pf[:, 0:64], op=ALU.mult)
        v.tensor_tensor(out=sb["tt2"][:, 0:64], in0=sb["tabs"][:, :], in1=pf[:, 64:128], op=ALU.mult)
        v.tensor_tensor(out=z_sb[:, 128 * tc:128 * tc + 64], in0=sb["tt1"][:, 0:64],
                        in1=sb["tt2"][:, 0:64], op=ALU.subtract)
        v.tensor_tensor(out=sb["tt1"][:, 0:64], in0=sb["tabs"][:, :], in1=pf[:, 0:64], op=ALU.mult)
        v.tensor_tensor(out=sb["tt2"][:, 0:64], in0=sb["tabc"][:, :], in1=pf[:, 64:128], op=ALU.mult)
        v.tensor_tensor(out=z_sb[:, 128 * tc + 64:128 * (tc + 1)], in0=sb["tt1"][:, 0:64],
                        in1=sb["tt2"][:, 0:64], op=ALU.add)
        t_ = Sr.t
        t_.matmul(out=pf[0:1, 128:256], lhsT=sb["e127"][:, 0:1],
                  rhs=z_sb[:, 128 * tc:128 * (tc + 1)], start=True, stop=True)
        v = Sr.v
        v.tensor_copy(out=sb["z0row"][:, :], in_=pf[0:1, 128:256])
        v.engine_nop(); v.engine_nop()


def _ffn(Sr, sb, ps, d, dr, qdict, vw):
    pa, pb, pf = ps["pa"][:, :], ps["pb"][:, :], ps["pf"][:, :]
    bigB, bigC, bigE = sb["bigB"][:, :], sb["bigC"][:, :], sb["bigE"][:, :]

    def vnop(k=2):
        v = Sr.v
        for _ in range(k):
            v.engine_nop()

    for (src, dst, R, Cw) in ((d["gw"], dr["gwq"], I_FFN, D),
                              (d["uw"], dr["uwq"], I_FFN, D),
                              (d["dw"], dr["dwq"], D, I_FFN)):
        v = Sr.v
        v.memset(sb["sc1"][:, :], 0.0)
        vnop()
        accp, cur = [sb["sc1"], sb["sc2"]], 0
        for r in range(R // 128):
            for p_ in range(Cw // 1024):
                Sr.dma(sb["xrow"][:, :], src[128 * r:128 * (r + 1), 1024 * p_:1024 * (p_ + 1)])
                v = Sr.v
                v.tensor_reduce(out=sb["sc3"][:, :], in_=sb["xrow"][:, :], axis=AX,
                                op=ALU.max, apply_absolute_value=True)
                vnop()
                v.tensor_tensor(out=accp[1 - cur][:, :], in0=accp[cur][:, :],
                                in1=sb["sc3"][:, :], op=ALU.max)
                vnop()
                cur = 1 - cur
        t_ = Sr.t
        t_.transpose(out=ps["pe"][0:1, 0:128], in_=accp[cur][:, 0:1], identity=sb["ident_f"][:, :])
        v = Sr.v
        v.tensor_reduce(out=sb["sc3"][0:1, :], in_=ps["pe"][0:1, 0:128], axis=AX, op=ALU.max)
        vnop()
        v.reciprocal(out=sb["sc4"][0:1, :], in_=sb["sc3"][0:1, :])
        vnop()
        v.tensor_scalar(out=sb["sc3"][0:1, :], in0=sb["sc4"][0:1, :], scalar1=2688.0,
                        scalar2=None, op0=ALU.mult)
        vnop()
        t_ = Sr.t
        t_.matmul(out=ps["pe"][:, 0:1], lhsT=sb["ones1f"][0:1, 0:128],
                  rhs=sb["sc3"][0:1, 0:1], start=True, stop=True)
        v = Sr.v
        v.tensor_scalar(out=sb["gs6w"][:, :], in0=ps["pe"][:, 0:1], scalar1=1.0 / 6.0,
                        scalar2=None, op0=ALU.mult)
        vnop()
        v.tensor_copy(out=sb["sc4"][:, :], in_=ps["pe"][:, 0:1])
        vnop()
        v.reciprocal(out=sb["gsiw"][:, :], in_=sb["sc4"][:, :])
        vnop()
        for r in range(R // 128):
            for p_ in range(Cw // 1024):
                Sr.dma(sb["xrow"][:, :], src[128 * r:128 * (r + 1), 1024 * p_:1024 * (p_ + 1)])
                emit_quant(Sr, sb["xrow"][:, :], vw["xb16"][:, :], 1024, qdict,
                           sb["gs6w"][:, 0:1], sb["gsiw"][:, 0:1], bf16_s=False)
                for sub in range(8):
                    t_ = Sr.t
                    t_.transpose(out=ps["pt"][:, 0:128],
                                 in_=vw["xb16"][:, 128 * sub:128 * (sub + 1)],
                                 identity=sb["ident"][:, :])
                    v = Sr.v
                    v.tensor_copy(out=sb["wslab"][:, 128 * sub:128 * (sub + 1)],
                                  in_=ps["pt"][:, 0:128])
                if Cw == D:
                    outap = dst.rearrange("(dc p) i -> p dc i", p=128)[:, :, 128 * r:128 * (r + 1)]
                else:
                    outap = dst.rearrange("(ic p) o -> p ic o", p=128)[:, 8 * p_:8 * (p_ + 1),
                                                                      128 * r:128 * (r + 1)]
                Sr.dma(outap, sb["wslab"][:, :].rearrange("p (s c) -> p s c", c=128))
    for islab in range(8):
        for dc in range(8):
            Sr.dma(bigE[:, 512 * dc:512 * (dc + 1)],
                   dr["gwq"][128 * dc:128 * (dc + 1), 512 * islab:512 * (islab + 1)])
            Sr.dma(bigE[:, 4096 + 512 * dc:4096 + 512 * (dc + 1)],
                   dr["uwq"][128 * dc:128 * (dc + 1), 512 * islab:512 * (islab + 1)])
        for tc in range(NCH):
            t_ = Sr.t
            for dc in range(8):
                t_.matmul(out=pa, lhsT=bigB[:, 2048 * dc + 128 * tc:2048 * dc + 128 * (tc + 1)],
                          rhs=bigE[:, 512 * dc:512 * (dc + 1)],
                          start=(dc == 0), stop=(dc == 7))
            for dc in range(8):
                t_.matmul(out=pb, lhsT=bigB[:, 2048 * dc + 128 * tc:2048 * dc + 128 * (tc + 1)],
                          rhs=bigE[:, 4096 + 512 * dc:4096 + 512 * (dc + 1)],
                          start=(dc == 0), stop=(dc == 7))
            s_ = Sr.s
            s_.activation(out=sb["hid"][:, :], in_=pa, func=AF.Silu)
            v = Sr.v
            v.tensor_tensor(out=sb["convo"][:, :], in0=sb["hid"][:, :], in1=pb, op=ALU.mult)
            emit_quant(Sr, sb["convo"][:, :], vw["convb"][:, :], 512, qdict,
                       sb["gs6d"][:, 0:1], sb["gsid"][:, 0:1], bf16_s=True)
            Sr.dma(dr["hq"][128 * tc:128 * (tc + 1), 512 * islab:512 * (islab + 1)],
                   vw["convb"][:, :])
    bigD = sb["bigD"][:, :]
    for ic in range(32):
        Sr.dma(bigC[:, 512 * ic:512 * (ic + 1)],
               dr["dwq"][128 * ic:128 * (ic + 1), 0:512])
        Sr.dma(bigD[:, 512 * ic:512 * (ic + 1)],
               dr["dwq"][128 * ic:128 * (ic + 1), 512:1024])
    for tc in range(NCH):
        Sr.dma(bigE[:, 0:4096], dr["hq"][128 * tc:128 * (tc + 1), :])
        for sub in range(32):
            t_ = Sr.t
            t_.transpose(out=ps["pt"][:, 0:128], in_=bigE[:, 128 * sub:128 * (sub + 1)],
                         identity=sb["ident"][:, :])
            v = Sr.v
            v.tensor_copy(out=bigE[:, 4096 + 128 * sub:4096 + 128 * (sub + 1)],
                          in_=ps["pt"][:, 0:128])
        for js in range(2):
            t_ = Sr.t
            wsrc = bigC if js == 0 else bigD
            for ic in range(32):
                t_.matmul(out=pa, lhsT=bigE[:, 4096 + 128 * ic:4096 + 128 * (ic + 1)],
                          rhs=wsrc[:, 512 * ic:512 * (ic + 1)],
                          start=(ic == 0), stop=(ic == 31))
            s_ = Sr.s
            s_.activation(out=sb["wslab"][:, 0:512], in_=pa, func=AF.Silu)
            Sr.dma(dr["gffn"][128 * tc:128 * (tc + 1), 512 * js:512 * (js + 1)],
                   sb["wslab"][:, 0:512])


CCDEF = [(cc, 128 * cc, 128) for cc in range(24)] + [(24, 3072, 32)] + \
        [(25 + g, 3104 + 128 * g, 128) for g in range(8)]


def _p4_slab(Sr, sb, ps, d, dr, sl, vw):
    pa, pb, pc, pd = ps["pa"][:, :], ps["pb"][:, :], ps["pc"][:, :], ps["pd"][:, :]
    bigA, bigB, bigC, bigD = (sb[k][:, :] for k in ("bigA", "bigB", "bigC", "bigD"))

    def vnop(k=2):
        v = Sr.v
        for _ in range(k):
            v.engine_nop()

    for (cc, c0, csz) in CCDEF:
        Sr.phase = 100 * (sl + 1) + 2 * cc
        Sr.dma(sb["wslab"][:, 0:8 * csz].rearrange("p (dc c) -> p dc c", c=csz),
               d["ctrl_W"].rearrange("(dc p) c -> p dc c", p=128)[:, :, c0:c0 + csz],
               cast=True)
        t_ = Sr.t
        for dc in range(8):
            t_.matmul(out=pa[0:csz, 0:512], lhsT=sb["wslab"][:, csz * dc:csz * (dc + 1)],
                      rhs=bigA[:, 2048 * dc + 512 * sl:2048 * dc + 512 * (sl + 1)],
                      start=(dc == 0), stop=(dc == 7))
        if cc < 24:
            kind, fc = cc // 8, cc % 8
            t_ = Sr.t
            t_.matmul(out=pb[0:128, 0:512], lhsT=sb["deltW"][:, 128 * cc:128 * (cc + 1)],
                      rhs=sb["cndf"][:, 512 * sl:512 * (sl + 1)], start=True, stop=True)
            v = Sr.v
            v.tensor_scalar(out=sb["convo"][:, :], in0=pa[0:128, 0:512],
                            scalar1=sb["cbias"][:, cc:cc + 1], scalar2=None, op0=ALU.add)
            v.scalar_tensor_tensor(out=sb["pre"][:, 3:515], in0=pb[0:128, 0:512],
                                   scalar=sb["rampq"][:, cc:cc + 1], in1=sb["convo"][:, :],
                                   op0=ALU.mult, op1=ALU.add)
            if sl == 0:
                t_ = Sr.t
                for dc in range(8):
                    t_.matmul(out=pc[0:csz, 0:3], lhsT=sb["wslab"][:, csz * dc:csz * (dc + 1)],
                              rhs=sb["xnhf"][:, 3 * dc:3 * dc + 3],
                              start=(dc == 0), stop=(dc == 7))
                t_.matmul(out=pd[0:128, 0:3], lhsT=sb["deltW"][:, 128 * cc:128 * (cc + 1)],
                          rhs=sb["condh"][:, :], start=True, stop=True)
                v = Sr.v
                v.tensor_scalar(out=sb["tt1"][:, 0:3], in0=pc[0:128, 0:3],
                                scalar1=sb["cbias"][:, cc:cc + 1], scalar2=None, op0=ALU.add)
                vnop()
                v.scalar_tensor_tensor(out=sb["tt2"][:, 0:3], in0=pd[0:128, 0:3],
                                       scalar=sb["rampq"][:, cc:cc + 1], in1=sb["tt1"][:, 0:3],
                                       op0=ALU.mult, op1=ALU.add)
                vnop()
                v.tensor_scalar(out=sb["tails"][:, 3 * cc:3 * cc + 3], in0=sb["tt2"][:, 0:3],
                                scalar1=sb["maskc"][:, 0:1], scalar2=None, op0=ALU.mult)
                vnop()
            v = Sr.v
            v.tensor_copy(out=sb["pre"][:, 0:3], in_=sb["tails"][:, 3 * cc:3 * cc + 3])
            vnop()
            base = (kind * 8 + fc) * 4
            v.tensor_scalar(out=sb["hid"][:, :], in0=sb["pre"][:, 0:512],
                            scalar1=sb["convw"][:, base:base + 1], scalar2=None, op0=ALU.mult)
            v.scalar_tensor_tensor(out=sb["convo"][:, :], in0=sb["pre"][:, 1:513],
                                   scalar=sb["convw"][:, base + 1:base + 2],
                                   in1=sb["hid"][:, :], op0=ALU.mult, op1=ALU.add)
            v.scalar_tensor_tensor(out=sb["hid"][:, :], in0=sb["pre"][:, 2:514],
                                   scalar=sb["convw"][:, base + 2:base + 3],
                                   in1=sb["convo"][:, :], op0=ALU.mult, op1=ALU.add)
            v.scalar_tensor_tensor(out=sb["convo"][:, :], in0=sb["pre"][:, 3:515],
                                   scalar=sb["convw"][:, base + 3:base + 4],
                                   in1=sb["hid"][:, :], op0=ALU.mult, op1=ALU.add)
            v.tensor_copy(out=sb["tails"][:, 3 * cc:3 * cc + 3], in_=sb["pre"][:, 512:515])
            vnop()
            s_ = Sr.s
            s_.activation(out=vw["convb"][:, :], in_=sb["convo"][:, :], func=AF.Silu)
            v = Sr.v
            if kind == 0:
                dst = bigB[:, 2048 * fc + 512 * sl:2048 * fc + 512 * (sl + 1)]
            elif kind == 1:
                dst = bigD[:, 512 * fc:512 * (fc + 1)]
            else:
                dst = bigD[:, 4096 + 512 * fc:4096 + 512 * (fc + 1)]
            v.tensor_copy(out=dst, in_=vw["convb"][:, :])
        elif cc == 24:
            t_ = Sr.t
            for dc in range(8):
                t_.matmul(out=pc[0:16, 0:512], lhsT=sb["wslab"][:, 32 * dc:32 * dc + 16],
                          rhs=bigA[:, 2048 * dc + 512 * sl:2048 * dc + 512 * (sl + 1)],
                          start=(dc == 0), stop=(dc == 7))
            for dc in range(8):
                t_.matmul(out=pd[0:16, 0:512], lhsT=sb["wslab"][:, 32 * dc + 16:32 * dc + 32],
                          rhs=bigA[:, 2048 * dc + 512 * sl:2048 * dc + 512 * (sl + 1)],
                          start=(dc == 0), stop=(dc == 7))
            t_.matmul(out=pb[0:16, 0:512], lhsT=sb["deltW"][:, 3072:3088],
                      rhs=sb["cndf"][:, 512 * sl:512 * (sl + 1)], start=True, stop=True)
            v = Sr.v
            v.tensor_scalar(out=sb["convo"][0:16, :], in0=pc[0:16, 0:512],
                            scalar1=sb["cbias"][0:16, 24:25], scalar2=None, op0=ALU.add)
            v.tensor_scalar(out=sb["hid"][0:16, :], in0=pd[0:16, 0:512],
                            scalar1=sb["cbias"][0:16, 25:26], scalar2=None, op0=ALU.add)
            vnop()
            v.scalar_tensor_tensor(out=sb["pre"][0:16, 0:512], in0=pb[0:16, 0:512],
                                   scalar=sb["rampb"][:, 0:1], in1=sb["hid"][0:16, :],
                                   op0=ALU.mult, op1=ALU.add)
            vnop()
            for tcs in range(4):
                t_ = Sr.t
                t_.transpose(out=ps["pe"][:, 0:16],
                             in_=sb["convo"][0:16, 128 * tcs:128 * (tcs + 1)],
                             identity=sb["ident_f"][0:16, 0:16])
                v = Sr.v
                v.tensor_copy(out=sb["dbt"][:, 32 * tcs:32 * tcs + 16], in_=ps["pe"][:, 0:16])
                vnop()
                t_ = Sr.t
                t_.transpose(out=ps["pe"][:, 16:32],
                             in_=sb["pre"][0:16, 128 * tcs:128 * (tcs + 1)],
                             identity=sb["ident_f"][0:16, 0:16])
                v = Sr.v
                v.tensor_copy(out=sb["dbt"][:, 32 * tcs + 16:32 * tcs + 32],
                              in_=ps["pe"][:, 16:32])
                vnop()
        else:
            gc = cc - 25
            v = Sr.v
            v.tensor_scalar(out=sb["convo"][:, :], in0=pa[0:128, 0:512],
                            scalar1=sb["cbias"][:, cc + 1:cc + 2], scalar2=None, op0=ALU.add)
            s_ = Sr.s
            s_.activation(out=vw["convb"][:, :], in_=sb["convo"][:, :], func=AF.Silu)
            for tcs in range(4):
                t_ = Sr.t
                t_.transpose(out=ps["pt"][:, 0:128],
                             in_=vw["convb"][:, 128 * tcs:128 * (tcs + 1)],
                             identity=sb["ident"][:, :])
                v = Sr.v
                v.tensor_copy(out=bigC[:, 1024 * (4 * sl + tcs) + 128 * gc:
                                       1024 * (4 * sl + tcs) + 128 * (gc + 1)],
                              in_=ps["pt"][:, 0:128])


def _p5_slab(Sr, sb, ps, d, dr, sl, vw):
    pa, pb, pc, pd, pe, pf = (ps[k][:, :] for k in ("pa", "pb", "pc", "pd", "pe", "pf"))
    pt = ps["pt"][:, :]
    bigB, bigD = sb["bigB"][:, :], sb["bigD"][:, :]

    def vnop(k=2):
        v = Sr.v
        for _ in range(k):
            v.engine_nop()

    def snop():
        s_ = Sr.s
        s_.nop(); s_.nop()

    for tcs in range(4):
        tcg = 4 * sl + tcs
        dec = sb["dbt"][:, 32 * tcs:32 * tcs + 16]
        bet = sb["dbt"][:, 32 * tcs + 16:32 * tcs + 32]
        s_ = Sr.s
        s_.activation(out=sb["t16a"][:, :], in_=dec, func=AF.Exp, scale=-1.0)
        snop()
        s_ = Sr.s
        s_.activation(out=sb["t16b"][:, :], in_=sb["t16a"][:, :], func=AF.Ln,
                      bias=sb["cone"][:, 0:1])
        snop()
        v = Sr.v
        v.tensor_scalar(out=sb["lng"][:, :], in0=sb["t16b"][:, :], scalar1=-1.0,
                        scalar2=None, op0=ALU.mult)
        vnop()
        s_ = Sr.s
        s_.activation(out=sb["t16a"][:, :], in_=bet, func=AF.Exp, scale=-1.0)
        snop()
        s_ = Sr.s
        s_.activation(out=sb["t16b"][:, :], in_=sb["t16a"][:, :], func=AF.Ln,
                      bias=sb["cone"][:, 0:1])
        snop()
        v = Sr.v
        v.tensor_scalar(out=sb["lnb"][:, :], in0=sb["t16b"][:, :], scalar1=-1.0,
                        scalar2=None, op0=ALU.mult)
        vnop()
        s_ = Sr.s
        s_.activation(out=sb["beta"][:, :], in_=sb["lnb"][:, :], func=AF.Exp)
        snop()
        t_ = Sr.t
        t_.matmul(out=pf[:, 0:16], lhsT=sb["trioneF"][:, :], rhs=sb["lng"][:, :],
                  start=True, stop=True)
        v = Sr.v
        v.tensor_copy(out=sb["lam"][:, :], in_=pf[:, 0:16])
        vnop()
        for fc in range(8):
            qt = bigB[:, 2048 * fc + 128 * tcg:2048 * fc + 128 * (tcg + 1)]
            kt = bigD[:, 512 * fc + 128 * tcs:512 * fc + 128 * (tcs + 1)]
            v = Sr.v
            v.tensor_tensor(out=sb["sq16"][:, :], in0=qt, in1=qt, op=ALU.mult)
            t_ = Sr.t
            t_.matmul(out=pf[:, 16 + 2 * fc:17 + 2 * fc], lhsT=sb["sq16"][0:64, :],
                      rhs=sb["onescb"][0:64, 0:1], start=True, stop=True)
            t_.matmul(out=pf[:, 17 + 2 * fc:18 + 2 * fc], lhsT=sb["sq16"][64:128, :],
                      rhs=sb["onescb"][64:128, 0:1], start=True, stop=True,
                      tile_position=(64, 0))
            v = Sr.v
            v.tensor_tensor(out=sb["sq16"][:, :], in0=kt, in1=kt, op=ALU.mult)
            t_ = Sr.t
            t_.matmul(out=pf[:, 32 + 2 * fc:33 + 2 * fc], lhsT=sb["sq16"][0:64, :],
                      rhs=sb["onescb"][0:64, 0:1], start=True, stop=True)
            t_.matmul(out=pf[:, 33 + 2 * fc:34 + 2 * fc], lhsT=sb["sq16"][64:128, :],
                      rhs=sb["onescb"][64:128, 0:1], start=True, stop=True,
                      tile_position=(64, 0))
        s_ = Sr.s
        s_.activation(out=sb["l1q"][:, :], in_=pf[:, 16:32], func=AF.Ln,
                      bias=sb["ceps6"][:, 0:1])
        snop()
        s_ = Sr.s
        s_.activation(out=sb["l1k"][:, :], in_=pf[:, 32:48], func=AF.Ln,
                      bias=sb["ceps6"][:, 0:1])
        snop()
        v = Sr.v
        v.scalar_tensor_tensor(out=sb["t16a"][:, :], in0=sb["l1k"][:, :], scalar=-0.5,
                               in1=sb["lam"][:, :], op0=ALU.mult, op1=ALU.add)
        vnop()
        v.tensor_tensor(out=sb["rowM"][:, :], in0=sb["t16a"][:, :], in1=sb["lnb"][:, :],
                        op=ALU.add)
        vnop()
        v.scalar_tensor_tensor(out=sb["rowA"][:, :], in0=sb["l1q"][:, :], scalar=-0.5,
                               in1=sb["lam"][:, :], op0=ALU.mult, op1=ALU.add)
        vnop()
        v.scalar_tensor_tensor(out=sb["t16b"][:, :], in0=sb["l1k"][:, :], scalar=0.5,
                               in1=sb["lam"][:, :], op0=ALU.mult, op1=ALU.add)
        vnop()
        v.tensor_scalar(out=sb["colv"][:, :], in0=sb["t16b"][:, :], scalar1=-1.0,
                        scalar2=None, op0=ALU.mult)
        vnop()
        s_ = Sr.s
        s_.activation(out=sb["coefk"][:, :], in_=sb["rowM"][:, :], func=AF.Exp)
        snop()
        s_ = Sr.s
        s_.activation(out=sb["eAall"][:, 16 * tcg:16 * (tcg + 1)], in_=sb["rowA"][:, :],
                      func=AF.Exp)
        snop()
        t_ = Sr.t
        t_.matmul(out=pe[0:1, 0:16], lhsT=sb["onescf"][:, 0:1], rhs=sb["lng"][:, :],
                  start=True, stop=True)
        v = Sr.v
        v.tensor_copy(out=sb["lamC"][:, :], in_=pe[0:1, 0:16])
        vnop()
        t_ = Sr.t
        t_.matmul(out=pf[:, 48:64], lhsT=sb["ones1f"][0:1, 0:128], rhs=sb["lamC"][:, :],
                  start=True, stop=True)
        v = Sr.v
        v.tensor_tensor(out=sb["t16a"][:, :], in0=pf[:, 48:64], in1=sb["colv"][:, :],
                        op=ALU.add)
        vnop()
        s_ = Sr.s
        s_.activation(out=sb["coefp"][:, :], in_=sb["t16a"][:, :], func=AF.Exp)
        snop()
        s_ = Sr.s
        s_.activation(out=sb["gCc"][:, :], in_=ps["pf"][0:64, 48:64], func=AF.Exp)
        snop()
        for g4 in range(4):
            for hs in range(4):
                h = 4 * g4 + hs
                half, fc = h % 2, h // 2
                qf_t = bigB[64 * half:64 * half + 64,
                            2048 * fc + 128 * tcg:2048 * fc + 128 * (tcg + 1)]
                kf_t = bigD[64 * half:64 * half + 64,
                            512 * fc + 128 * tcs:512 * fc + 128 * (tcs + 1)]
                vf_t = bigD[64 * half:64 * half + 64,
                            4096 + 512 * fc + 128 * tcs:4096 + 512 * fc + 128 * (tcs + 1)]
                tp = (64 * half, 0)
                t_ = Sr.t
                t_.matmul(out=pc[:, 0:128], lhsT=kf_t, rhs=kf_t, start=True, stop=True,
                          tile_position=tp)
                t_.matmul(out=pc[:, 128:256], lhsT=kf_t, rhs=qf_t, start=True, stop=True,
                          tile_position=tp)
                v = Sr.v
                v.tensor_copy(out=sb["pk2a"][:, 1:2], in_=sb["colv"][:, h:h + 1])
                v.tensor_copy(out=sb["pk2b"][:, 0:1], in_=sb["rowM"][:, h:h + 1])
                v.tensor_copy(out=sb["pk2c"][:, 0:1], in_=sb["rowA"][:, h:h + 1])
                vnop()
                t_ = Sr.t
                t_.transpose(out=pb[0:2, 0:128], in_=sb["pk2a"][:, :],
                             identity=sb["ident_f"][:, :])
                t_.transpose(out=pb[0:2, 128:256], in_=sb["pk2b"][:, :],
                             identity=sb["ident_f"][:, :])
                t_.transpose(out=pb[0:2, 256:384], in_=sb["pk2c"][:, :],
                             identity=sb["ident_f"][:, :])
                v = Sr.v
                v.tensor_copy(out=sb["stk2"][:, :], in_=pb[0:2, 0:128])
                v.tensor_copy(out=sb["stk2r"][:, :], in_=pb[0:2, 128:256])
                v.tensor_copy(out=sb["stk2q"][:, :], in_=pb[0:2, 256:384])
                vnop()
                t_ = Sr.t
                t_.matmul(out=pe[:, 0:128], lhsT=sb["stk2"][0:2, :], rhs=sb["stk2r"][0:2, :],
                          start=True, stop=True)
                t_.matmul(out=pe[:, 128:256], lhsT=sb["stk2"][0:2, :], rhs=sb["stk2q"][0:2, :],
                          start=True, stop=True)
                v = Sr.v
                v.scalar_tensor_tensor(out=sb["tt1"][:, :], in0=pe[:, 0:128], scalar=1.0,
                                       in1=sb["maskM"][:, :], op0=ALU.mult, op1=ALU.add)
                s_ = Sr.s
                s_.activation(out=sb["tt2"][:, :], in_=sb["tt1"][:, :], func=AF.Exp)
                v = Sr.v
                v.tensor_scalar(out=sb["tt3"][:, :], in0=sb["tt2"][:, :], scalar1=-1.0,
                                scalar2=None, op0=ALU.mult)
                v.tensor_tensor(out=sb["Pa"][:, 128 * hs:128 * (hs + 1)],
                                in0=sb["tt3"][:, :], in1=pc[:, 0:128], op=ALU.mult)
                v.scalar_tensor_tensor(out=sb["tt1"][:, :], in0=pe[:, 128:256], scalar=1.0,
                                       in1=sb["maskA"][:, :], op0=ALU.mult, op1=ALU.add)
                s_ = Sr.s
                s_.activation(out=sb["tt2"][:, :], in_=sb["tt1"][:, :], func=AF.Exp)
                v = Sr.v
                v.tensor_tensor(out=sb["AqS"][:, 128 * hs:128 * (hs + 1)],
                                in0=sb["tt2"][:, :], in1=pc[:, 128:256], op=ALU.mult)
                t_ = Sr.t
                t_.transpose(out=pt[:, 0:64], in_=vf_t,
                             identity=sb["ident"][64 * half:64 * half + 64,
                                                  64 * half:64 * half + 64],
                             tile_position=tp)
                v = Sr.v
                v.tensor_scalar(out=sb["Ua"][:, 128 * hs:128 * hs + 64], in0=pt[:, 0:64],
                                scalar1=sb["beta"][:, h:h + 1], scalar2=None, op0=ALU.mult)
                t_ = Sr.t
                t_.transpose(out=pt[:, 64:128], in_=kf_t,
                             identity=sb["ident"][64 * half:64 * half + 64,
                                                  64 * half:64 * half + 64],
                             tile_position=tp)
                v = Sr.v
                v.tensor_scalar(out=sb["Ua"][:, 128 * hs + 64:128 * (hs + 1)],
                                in0=pt[:, 64:128], scalar1=sb["coefk"][:, h:h + 1],
                                scalar2=None, op0=ALU.mult)
                v.tensor_scalar(out=sb["kpt"][:, 64 * hs:64 * (hs + 1)], in0=pt[:, 64:128],
                                scalar1=sb["coefp"][:, h:h + 1], scalar2=None, op0=ALU.mult)
            Pcur, Pnxt = sb["Pa"][:, :], sb["Pb"][:, :]
            Ucur, Unxt = sb["Ua"][:, :], sb["Ub"][:, :]
            for lvl in range(7):
                t_ = Sr.t
                for hs in range(4):
                    t_.matmul(out=pd[:, 128 * hs:128 * (hs + 1)],
                              lhsT=Pcur[:, 128 * hs:128 * (hs + 1)],
                              rhs=Ucur[:, 128 * hs:128 * (hs + 1)], start=True, stop=True)
                v = Sr.v
                v.tensor_tensor(out=Unxt, in0=pd, in1=Ucur, op=ALU.add)
                Ucur, Unxt = Unxt, Ucur
                if lvl < 6:
                    t_ = Sr.t
                    for hs in range(4):
                        t_.transpose(out=pt[:, 128 * hs:128 * (hs + 1)],
                                     in_=Pcur[:, 128 * hs:128 * (hs + 1)],
                                     identity=sb["ident"][:, :])
                    v = Sr.v
                    v.tensor_copy(out=sb["PT"][:, :], in_=pt)
                    t_ = Sr.t
                    for hs in range(4):
                        t_.matmul(out=pe[:, 128 * hs:128 * (hs + 1)],
                                  lhsT=sb["PT"][:, 128 * hs:128 * (hs + 1)],
                                  rhs=Pcur[:, 128 * hs:128 * (hs + 1)], start=True, stop=True)
                    v = Sr.v
                    v.tensor_copy(out=Pnxt, in_=pe)
                    Pcur, Pnxt = Pnxt, Pcur
            Ufin = Ucur
            t_ = Sr.t
            for hs in range(4):
                t_.matmul(out=pd[:, 64 * hs:64 * (hs + 1)],
                          lhsT=sb["AqS"][:, 128 * hs:128 * (hs + 1)],
                          rhs=Ufin[:, 128 * hs:128 * hs + 64], start=True, stop=True)
            for hs in range(4):
                t_.matmul(out=pe[0:64, 128 * hs:128 * (hs + 1)],
                          lhsT=Ufin[:, 128 * hs + 64:128 * (hs + 1)],
                          rhs=sb["AqS"][:, 128 * hs:128 * (hs + 1)], start=True, stop=True)
            for hs in range(4):
                t_.matmul(out=pf[0:64, 64 * hs:64 * (hs + 1)],
                          lhsT=sb["kpt"][:, 64 * hs:64 * (hs + 1)],
                          rhs=Ufin[:, 128 * hs:128 * hs + 64], start=True, stop=True)
            for hs in range(4):
                t_.matmul(out=pc[0:64, 64 * hs:64 * (hs + 1)],
                          lhsT=Ufin[:, 128 * hs + 64:128 * (hs + 1)],
                          rhs=sb["kpt"][:, 64 * hs:64 * (hs + 1)], start=True, stop=True)
            v = Sr.v
            v.tensor_copy(out=vw["convb"][:, 0:256], in_=pd[:, 0:256])
            Sr.dma(dr["oloc"][128 * tcg:128 * (tcg + 1), 256 * g4:256 * (g4 + 1)],
                   vw["convb"][:, 0:256])
            v = Sr.v
            v.tensor_scalar(out=sb["pk64"][:, :], in0=pe[0:64, 0:512], scalar1=-1.0,
                            scalar2=None, op0=ALU.mult)
            Sr.dma(dr["gneg"][64 * (16 * tcg + 4 * g4):64 * (16 * tcg + 4 * g4) + 256, :]
                   .rearrange("(hh r) c -> r hh c", hh=4),
                   sb["pk64"][:, :].rearrange("p (hh c) -> p hh c", hh=4))
            for hs in range(4):
                h = 4 * g4 + hs
                v = Sr.v
                v.tensor_scalar(out=sb["t64"][:, :], in0=sb["i64b"][:, :],
                                scalar1=sb["gCc"][:, h:h + 1], scalar2=None, op0=ALU.mult)
                vnop()
                v.scalar_tensor_tensor(out=sb["pk64"][:, 128 * hs:128 * hs + 64],
                                       in0=pc[0:64, 64 * hs:64 * (hs + 1)], scalar=-1.0,
                                       in1=sb["t64"][:, :], op0=ALU.mult, op1=ALU.add)
                vnop()
                v.tensor_copy(out=sb["pk64"][:, 128 * hs + 64:128 * (hs + 1)],
                              in_=pf[0:64, 64 * hs:64 * (hs + 1)])
                vnop()
            Sr.dma(dr["ptqc"][64 * (16 * tcg + 4 * g4):64 * (16 * tcg + 4 * g4) + 256, :]
                   .rearrange("(hh r) c -> r hh c", hh=4),
                   sb["pk64"][:, :].rearrange("p (hh c) -> p hh c", hh=4))


def _p7(Sr, sb, ps, d, dr):
    pd, pe, pf = ps["pd"][:, :], ps["pe"][:, :], ps["pf"][:, :]
    pt = ps["pt"][:, :]
    bigA, bigB, bigC, bigE = (sb[k][:, :] for k in ("bigA", "bigB", "bigC", "bigE"))
    ovl = {"ptq": sb["cndf"][:, 0:1024], "gng": sb["cndf"][:, 1024:2048],
           "Sc1": sb["Ua"][:, :], "Sc2": sb["Ub"][:, :],
           "Tc1": sb["Pa"][:, :], "Tc2": sb["Pb"][:, :],
           "Strue": sb["PT"][:, :], "Srec": sb["AqS"][:, :],
           "S16p": sb["pre"][:, 0:512]}

    def vnop(k=2):
        v = Sr.v
        for _ in range(k):
            v.engine_nop()

    def snop():
        s_ = Sr.s
        s_.nop(); s_.nop()

    def load_ptq(ch):
        Sr.dma(ovl["ptq"].rearrange("p (j c) -> p j c", j=8),
               dr["ptqc"][64 * 16 * ch:64 * 16 * (ch + 1), :]
               .rearrange("(j two r) c -> (two r) j c", j=8, two=2))

    v = Sr.v
    v.memset(ovl["Sc1"], 0.0)
    vnop()
    Scur, Snxt = ovl["Sc1"], ovl["Sc2"]
    for ch in range(NCH):
        load_ptq(ch)
        t_ = Sr.t
        for h in range(H):
            two, j = h % 2, h // 2
            t_.matmul(out=pd[64 * two:64 * two + 64, 64 * j:64 * (j + 1)],
                      lhsT=ovl["ptq"][64 * two:64 * two + 64, 128 * j:128 * j + 64],
                      rhs=Scur[64 * two:64 * two + 64, 64 * j:64 * (j + 1)],
                      start=True, stop=True, tile_position=(64 * two, 64 * two))
        for h in range(H):
            two, j = h % 2, h // 2
            v = Sr.v
            v.tensor_tensor(out=Snxt[64 * two:64 * two + 64, 64 * j:64 * (j + 1)],
                            in0=pd[64 * two:64 * two + 64, 64 * j:64 * (j + 1)],
                            in1=ovl["ptq"][64 * two:64 * two + 64, 128 * j + 64:128 * (j + 1)],
                            op=ALU.add)
            if ch == NCH - 1:
                v.tensor_tensor(out=ovl["S16p"][64 * two:64 * two + 64, 64 * j:64 * (j + 1)],
                                in0=pd[64 * two:64 * two + 64, 64 * j:64 * (j + 1)],
                                in1=ovl["ptq"][64 * two:64 * two + 64, 128 * j + 64:128 * (j + 1)],
                                op=ALU.add)
        v = Sr.v
        v.tensor_copy(out=bigE[:, 512 * ch:512 * (ch + 1)], in_=Snxt)
        Scur, Snxt = Snxt, Scur
    for two in range(2):
        Sr.dma(dr["cc2i"].ap().rearrange("(j two r) v -> r j two v", j=8, two=2)
               [:, :, two:two + 1, :],
               ovl["S16p"][64 * two:64 * two + 64, :]
               .rearrange("r (j v) -> r j v", j=8).unsqueeze(2))
    Sr.cc("AllGather", [dr["cc2i"].ap()], [dr["cc2o"].ap()])
    v = Sr.v
    v.memset(sb["convo"][:, :], 0.0)
    vnop()
    accp, cur = [sb["convo"], sb["hid"]], 0
    for r8 in range(8):
        Sr.dma(ovl["S16p"].rearrange("p (j v) -> p j v", j=8),
               dr["cc2o"].ap()[1024 * r8:1024 * (r8 + 1), :]
               .rearrange("(j two r) v -> (two r) j v", j=8, two=2))
        v = Sr.v
        v.scalar_tensor_tensor(out=accp[1 - cur][:, :], in0=ovl["S16p"],
                               scalar=sb["selw"][:, r8:r8 + 1], in1=accp[cur][:, :],
                               op0=ALU.mult, op1=ALU.add)
        cur = 1 - cur
    v = Sr.v
    v.tensor_copy(out=ovl["Srec"], in_=accp[cur][:, :])
    vnop()
    Tcur, Tnxt = ovl["Tc1"], ovl["Tc2"]
    v = Sr.v
    v.tensor_copy(out=Tcur, in_=ovl["Srec"])
    vnop()
    for ch in range(NCH):
        if ch == 0:
            v = Sr.v
            v.tensor_copy(out=ovl["Strue"], in_=ovl["Srec"])
            vnop()
        else:
            v = Sr.v
            v.tensor_tensor(out=ovl["Strue"], in0=bigE[:, 512 * (ch - 1):512 * ch],
                            in1=Tcur, op=ALU.add)
            vnop()
        Sr.dma(ovl["gng"].rearrange("p (j c) -> p j c", j=8),
               dr["gneg"][64 * 16 * ch:64 * 16 * (ch + 1), :]
               .rearrange("(j two r) c -> (two r) j c", j=8, two=2))
        Sr.dma(sb["wslab"][:, :], dr["oloc"][128 * ch:128 * (ch + 1), :])
        for wave in range(2):
            t_ = Sr.t
            for hh in range(8):
                h = 8 * wave + hh
                two, j = h % 2, h // 2
                St = ovl["Strue"][64 * two:64 * two + 64, 64 * j:64 * (j + 1)]
                qf_t = bigB[64 * two:64 * two + 64,
                            2048 * j + 128 * ch:2048 * j + 128 * (ch + 1)]
                t_.matmul(out=pd[:, 64 * hh:64 * (hh + 1)], lhsT=qf_t, rhs=St,
                          start=True, stop=True, tile_position=(64 * two, 0))
                t_.matmul(out=pe[:, 64 * hh:64 * (hh + 1)],
                          lhsT=ovl["gng"][64 * two:64 * two + 64, 128 * j:128 * (j + 1)],
                          rhs=St, start=True, stop=True, tile_position=(64 * two, 0))
            for hh in range(8):
                h = 8 * wave + hh
                v = Sr.v
                v.tensor_scalar(out=sb["opre"][:, :], in0=pd[:, 64 * hh:64 * (hh + 1)],
                                scalar1=sb["eAall"][:, 16 * ch + h:16 * ch + h + 1],
                                scalar2=None, op0=ALU.mult)
                v.tensor_tensor(out=sb["opre2"][:, :], in0=sb["opre"][:, :],
                                in1=pe[:, 64 * hh:64 * (hh + 1)], op=ALU.add)
                v.tensor_tensor(out=sb["opre"][:, :], in0=sb["opre2"][:, :],
                                in1=sb["wslab"][:, 64 * h:64 * (h + 1)], op=ALU.add)
                v.tensor_tensor(out=sb["tt1"][:, 0:64], in0=sb["opre"][:, :],
                                in1=sb["opre"][:, :], op=ALU.mult)
                v.tensor_reduce(out=sb["ln1"][:, :], in_=sb["tt1"][:, 0:64], axis=AX,
                                op=ALU.add)
                vnop()
                s_ = Sr.s
                s_.activation(out=sb["ln2"][:, :], in_=sb["ln1"][:, :], func=AF.Ln,
                              scale=1.0 / 64.0, bias=sb["ceps6"][:, 0:1])
                snop()
                s_ = Sr.s
                s_.activation(out=sb["ln1"][:, :], in_=sb["ln2"][:, :], func=AF.Exp,
                              scale=-0.5)
                snop()
                v = Sr.v
                v.tensor_scalar(out=sb["opre2"][:, :], in0=sb["opre"][:, :],
                                scalar1=sb["ln1"][:, 0:1], scalar2=None, op0=ALU.mult)
                vnop()
                v.tensor_tensor(out=sb["ot64"][:, :], in0=sb["opre2"][:, :],
                                in1=bigC[:, 1024 * ch + 64 * h:1024 * ch + 64 * (h + 1)],
                                op=ALU.mult)
                vnop()
                two = h % 2
                t_ = Sr.t
                t_.transpose(out=pt[64 * two:64 * two + 64, 0:128], in_=sb["ot64"][:, :],
                             identity=sb["ident"][:, :], tile_position=(0, 64 * two))
                v = Sr.v
                v.tensor_scalar(out=bigA[64 * two:64 * two + 64,
                                         2048 * (h // 2) + 128 * ch:
                                         2048 * (h // 2) + 128 * (ch + 1)],
                                in0=pt[64 * two:64 * two + 64, 0:128],
                                scalar1=sb["onwc"][64 * two:64 * two + 64, 0:1],
                                scalar2=None, op0=ALU.mult)
        load_ptq(ch)
        t_ = Sr.t
        for h in range(H):
            two, j = h % 2, h // 2
            t_.matmul(out=pf[64 * two:64 * two + 64, 64 * j:64 * (j + 1)],
                      lhsT=ovl["ptq"][64 * two:64 * two + 64, 128 * j:128 * j + 64],
                      rhs=Tcur[64 * two:64 * two + 64, 64 * j:64 * (j + 1)],
                      start=True, stop=True, tile_position=(64 * two, 64 * two))
        v = Sr.v
        v.tensor_copy(out=Tnxt, in_=pf)
        Tcur, Tnxt = Tnxt, Tcur


def _p8(Sr, sb, ps, d, dr, y_out, vw):
    pa = ps["pa"][:, :]
    bigA, bigD = sb["bigA"][:, :], sb["bigD"][:, :]

    def vnop(k=2):
        v = Sr.v
        for _ in range(k):
            v.engine_nop()

    for jc in range(8):
        Sr.dma(sb["xrow"][:, :], d["Wo"][128 * jc:128 * (jc + 1), :])
        v = Sr.v
        v.tensor_copy(out=vw["xb16"][:, :], in_=sb["xrow"][:, :])
        for dc in range(8):
            t_ = Sr.t
            t_.transpose(out=ps["pt"][:, 0:128], in_=vw["xb16"][:, 128 * dc:128 * (dc + 1)],
                         identity=sb["ident"][:, :])
            v = Sr.v
            v.tensor_copy(out=bigD[:, 1024 * dc + 128 * jc:1024 * dc + 128 * (jc + 1)],
                          in_=ps["pt"][:, 0:128])
    for tc in range(NCH):
        Sr.dma(sb["xrow"][:, :], d["x_sh"][HALO + 128 * tc:HALO + 128 * (tc + 1), :])
        Sr.dma(sb["wslab"][:, :], dr["gffn"][128 * tc:128 * (tc + 1), :])
        for js in range(2):
            t_ = Sr.t
            for dc in range(8):
                t_.matmul(out=pa, lhsT=bigA[:, 2048 * dc + 128 * tc:2048 * dc + 128 * (tc + 1)],
                          rhs=bigD[:, 1024 * dc + 512 * js:1024 * dc + 512 * (js + 1)],
                          start=(dc == 0), stop=False)
            t_.matmul(out=pa, lhsT=sb["ones1b"][0:1, 0:128], rhs=sb["bo16"][0:1, 512 * js:512 * (js + 1)],
                      start=False, stop=True)
            v = Sr.v
            v.tensor_tensor(out=sb["convo"][:, :], in0=pa, in1=sb["wslab"][:, 512 * js:512 * (js + 1)],
                            op=ALU.mult)
            v.tensor_tensor(out=sb["xt"][:, 512 * js:512 * (js + 1)], in0=sb["convo"][:, :],
                            in1=sb["xrow"][:, 512 * js:512 * (js + 1)], op=ALU.add)
        Sr.dma(y_out[128 * tc:128 * (tc + 1), :], sb["xt"][:, :])


def kernel(**inputs):
    if "nc" not in _CACHE:
        _CACHE["nc"] = build_nc()
    nc = _CACHE["nc"]
    maps = _host_maps(inputs)
    res = run_bass_kernel_spmd(nc, maps, core_ids=list(range(8)))
    out = np.zeros((B, S, D), np.float32)
    for i in range(8):
        b_, h_ = i % 4, i // 4
        out[b_, h_ * T:(h_ + 1) * T] = res.results[i]["y_sh"]
    return out


# version 12 (final)
# speedup vs baseline: 1.3413x; 1.0188x over previous
import contextlib
import numpy as np
import ml_dtypes
import concourse.bass as bass
import concourse.mybir as mybir
from concourse.bass_utils import run_bass_kernel_spmd

F32 = mybir.dt.float32
BF16 = mybir.dt.bfloat16
ALU = mybir.AluOpType
AF = mybir.ActivationFunctionType
AX = mybir.AxisListType.X

B, S, D, H, DH, BW = 4, 4096, 1024, 16, 64, 128
T, HALO, C = 2048, 3, 128
NCH = T // C
I_FFN = 4096
PI = float(np.pi)
TWO_PI = float(2 * np.pi)
THRQ = [0.25 ** 2, 0.75 ** 2, 1.25 ** 2, 1.75 ** 2, 2.5 ** 2, 3.5 ** 2, 5.0 ** 2]

_CACHE = {}


def _host_maps(inputs):
    x = np.ascontiguousarray(np.asarray(inputs["L0_attn_hidden"], np.float32))
    perm = np.concatenate([np.arange(0, BW, 2), np.arange(1, BW, 2)])
    cond_in_W = np.ascontiguousarray(np.asarray(inputs["L0_attn_cond_in_W"], np.float32)[:, perm])
    delta_W = np.ascontiguousarray(np.asarray(inputs["L0_attn_delta_W"], np.float32)[perm, :])
    cnw = np.ascontiguousarray(
        np.asarray(inputs["L0_attn_cond_norm_w"], np.float32)[perm].reshape(BW, 1))
    theta = np.asarray(inputs["L0_attn_rot_theta"], np.float32).reshape(1, 64)
    ctrl_W = np.ascontiguousarray(np.asarray(inputs["L0_attn_ctrl_W"], np.float32))
    ctrl_b = np.asarray(inputs["L0_attn_ctrl_b"], np.float32).reshape(-1)
    cb = np.zeros((128, 34), np.float32)
    for cc in range(24):
        cb[:, cc] = ctrl_b[128 * cc:128 * cc + 128]
    cb[0:16, 24] = ctrl_b[3072:3088]
    cb[0:16, 25] = ctrl_b[3088:3104]
    for g in range(8):
        cb[:, 26 + g] = ctrl_b[3104 + 128 * g:3104 + 128 * g + 128]
    ramp = np.asarray(inputs["L0_attn_qkv_ramp_logit"], np.float32)
    rampt = np.zeros((128, 24), np.float32)
    for kind in range(3):
        for fc in range(8):
            rampt[:, kind * 8 + fc] = ramp[kind, 128 * fc:128 * fc + 128]
    beta_ramp = np.asarray(inputs["L0_attn_beta_ramp_logit"], np.float32).reshape(H, 1)
    convw = np.zeros((128, 96), np.float32)
    for kind, key in enumerate(("q", "k", "v")):
        w = np.asarray(inputs[f"L0_attn_{key}_conv_w"], np.float32)
        for fc in range(8):
            convw[:, (kind * 8 + fc) * 4:(kind * 8 + fc) * 4 + 4] = \
                w[128 * fc:128 * fc + 128]
    onw2 = np.concatenate([np.asarray(inputs["L0_attn_out_norm_w"], np.float32).reshape(DH, 1)] * 2)
    Wo = np.ascontiguousarray(np.asarray(inputs["L0_attn_out_proj_W"], np.float32))
    bo = np.asarray(inputs["L0_attn_out_proj_b"], np.float32).reshape(1, D)
    gw = np.ascontiguousarray(np.asarray(inputs["L0_ffn_gate_w"], np.float32))
    uw = np.ascontiguousarray(np.asarray(inputs["L0_ffn_up_w"], np.float32))
    dw = np.ascontiguousarray(np.asarray(inputs["L0_ffn_down_w"], np.float32))
    s_in = float(np.asarray(inputs["L0_ffn_s_in"]).reshape(-1)[0])
    s_in_d = float(np.asarray(inputs["L0_ffn_s_in_down"]).reshape(-1)[0])

    cst = {}
    cst["ident"] = np.eye(128).astype(ml_dtypes.bfloat16)
    cst["ident_f"] = np.eye(128).astype(np.float32)
    tri = np.tril(np.ones((128, 128), np.float32))
    cst["trioneB"] = np.ascontiguousarray(tri.T).astype(ml_dtypes.bfloat16)
    cst["trioneF"] = np.ascontiguousarray(tri.T).astype(np.float32)
    jj, tt2 = np.meshgrid(np.arange(128), np.arange(128), indexing="ij")
    cst["maskMc"] = np.where(jj < tt2, 0.0, -1e30).astype(np.float32)
    cst["maskAc"] = np.where(jj <= tt2, 0.0, -1e30).astype(np.float32)
    cst["iotar"] = np.arange(1, 129, dtype=np.float32).reshape(1, 128)
    cst["ones1f"] = np.ones((1, 512), np.float32)
    cst["ones1b"] = np.ones((1, 512), ml_dtypes.bfloat16)
    cst["onescb"] = np.ones((128, 1), ml_dtypes.bfloat16)
    cst["i64c"] = np.eye(64).astype(np.float32)
    e127 = np.zeros((128, 1), np.float32); e127[127, 0] = 1.0
    cst["e127"] = e127
    e2 = np.zeros((128, 1), np.float32); e2[2, 0] = 1.0
    cst["e2"] = e2

    maps = []
    for i in range(8):
        b_, h_ = i % 4, i // 4
        xs = np.zeros((T + HALO, D), np.float32)
        if h_ == 0:
            xs[HALO:] = x[b_, 0:T]
        else:
            xs[:] = x[b_, h_ * T - HALO:h_ * T + T]
        sel = np.zeros((128, 8), np.float32)
        if h_ == 1:
            sel[:, i - 4] = 1.0
        m = dict(
            x_sh=xs, sel=sel,
            mask=np.full((128, 1), 1.0 if h_ == 1 else 0.0, np.float32),
            cond_in_W=cond_in_W, delta_W=delta_W, cnw=cnw, theta=theta,
            ctrl_W=ctrl_W, cbias=cb, rampq=rampt, beta_ramp=beta_ramp,
            convw=convw, onw2=onw2, Wo=Wo, bo=bo, gw=gw, uw=uw, dw=dw,
            s_in=np.full((128, 1), s_in, np.float32),
            s_in_d=np.full((128, 1), s_in_d, np.float32),
        )
        m.update(cst)
        maps.append(m)
    return maps


_DT_SIZE = {}
for _nm, _sz in (("float32", 4), ("int32", 4), ("uint32", 4), ("bfloat16", 2),
                 ("float16", 2), ("int16", 2), ("uint16", 2), ("int8", 1),
                 ("uint8", 1), ("float8_e4m3", 1), ("float8_e5m2", 1)):
    _d = getattr(mybir.dt, _nm, None)
    if _d is not None:
        _DT_SIZE[_d] = _sz

_READ, _WRITE = 0, 1
import os
SERIAL_FROM = int(os.environ.get('SERIAL_FROM', '9999'))
_CAP = 48


class Ser:

    QW = 4

    def __init__(self, nc, E):
        self.nc = nc
        self.E = E
        self.eng = {"T": nc.tensor, "V": nc.vector, "S": nc.scalar,
                    "Q": nc.sync, "G": nc.gpsimd}
        self.sems = {k: E(nc.semaphore("sem" + k)) for k in ("T", "V", "S")}
        self.cnt = {k: 0 for k in ("T", "V", "S")}
        for base in ("Q", "G"):
            for j in range(self.QW):
                k = base + str(j)
                self.sems[k] = E(nc.semaphore("sem" + k))
                self.cnt[k] = 0
        self.qidx = {"Q": 0, "G": 0}
        self.wm = {k: {} for k in self.eng}
        self.writes = {}
        self.reads = {}
        self.cn = 0
        self.n = 0
        self.last = None
        self.phase = 0

    def _box(self, ap):
        t = ap.tensor
        name = t.name
        esz = _DT_SIZE[t.dtype]
        pat = list(ap.ap)
        tname = type(t).__name__
        if tname == "DRamTensorHandle":
            lo = ap.offset * esz
            span = esz
            for s, n_ in pat:
                span += (n_ - 1) * abs(s) * esz
            return (name, 0, 1, lo, lo + span)
        shp = list(t.shape)
        re_ = 1
        for d in shp[1:]:
            re_ *= d
        rowb = re_ * esz
        if tname == "PSumTensorHandle":
            return (name, 0, 128, 0, rowb)
        p0 = ap.offset // re_ + getattr(t, "base_partition", 0)
        c0 = (ap.offset % re_) * esz
        dp = 0
        cspan = esz
        ok = True
        for s, n_ in pat:
            if n_ <= 1 or s == 0:
                continue
            if s % re_ == 0:
                dp += (n_ - 1) * (s // re_)
            else:
                cspan += (n_ - 1) * abs(s) * esz
        if c0 + cspan > rowb:
            return (name, 0, 128, 0, rowb)
        return (name, p0, p0 + dp + 1, c0, c0 + cspan)

    @staticmethod
    def _overlap(box, r):
        return box[1] < r[1] and r[0] < box[2] and box[3] < r[3] and r[2] < box[4]

    def _deps_from(self, recs, box, deps):
        for r in recs:
            if self._overlap(box, r):
                e, s_ = r[4], r[5]
                if deps.get(e, 0) < s_:
                    deps[e] = s_

    def _record(self, store, box, eng, seq):
        recs = store.setdefault(box[0], [])
        nb = [box[1], box[2], box[3], box[4], eng, seq]
        keep = [r for r in recs
                if not (nb[0] <= r[0] and r[1] <= nb[1]
                        and nb[2] <= r[2] and r[3] <= nb[3])]
        keep.append(nb)
        if len(keep) > _CAP:
            byeng = {}
            for r in keep:
                byeng[r[4]] = max(byeng.get(r[4], 0), r[5])
            keep = [[0, 128, 0, 1 << 30, e, s_] for e, s_ in byeng.items()]
        store[box[0]] = keep

    def _emit_waits(self, E_, deps, self_raw):
        q = self.eng[E_]
        wmE = self.wm[E_]
        for F, v in deps.items():
            if F == E_ and not self_raw:
                continue
            if wmE.get(F, 0) < v:
                q.wait_ge(self.sems[F] if F in self.sems else F, v)
                wmE[F] = v

    def _issue(self, E_, rd_boxes, wr_boxes):
        raw_deps, war_deps = {}, {}
        for b in rd_boxes:
            self._deps_from(self.writes.get(b[0], ()), b, raw_deps)
        for b in wr_boxes:
            self._deps_from(self.writes.get(b[0], ()), b, war_deps)
            self._deps_from(self.reads.get(b[0], ()), b, war_deps)
        deps = dict(war_deps)
        for e, s_ in raw_deps.items():
            if deps.get(e, 0) < s_:
                deps[e] = s_
        if self.phase >= SERIAL_FROM:
            for F, c in self.cnt.items():
                if c > 0 and deps.get(F, 0) < c:
                    deps[F] = c
        self._emit_waits(E_, deps, True)

    def _finish(self, E_, rd_boxes, wr_boxes, seq):
        for b in rd_boxes:
            self._record(self.reads, b, E_, seq)
        for b in wr_boxes:
            self._record(self.writes, b, E_, seq)

    def _collect(self, a, k, wr_boxes, rd_boxes):
        wout = k.get("out", None)
        if wout is not None:
            wr_boxes.append(self._box(wout))
        for key, val in k.items():
            if key == "out" or val is None:
                continue
            if hasattr(val, "tensor") and hasattr(val, "ap"):
                rd_boxes.append(self._box(val))
        for val in a:
            if hasattr(val, "tensor") and hasattr(val, "ap"):
                rd_boxes.append(self._box(val))

    def _wrap(self, e, E_):
        ser = self

        class W:
            def __getattr__(_s, name):
                if name in ("nop", "engine_nop"):
                    return lambda *a, **k: None
                fn = getattr(e, name)
                if name in ("wait_ge", "then_inc"):
                    return fn

                def call(*a, **k):
                    if E_ == "V" and name == "tensor_copy":
                        return ser.s.activation(out=k["out"], in_=k["in_"],
                                                func=AF.Copy)
                    wr, rd = [], []
                    if name == "memset":
                        wr.append(ser._box(a[0]))
                        ser._collect(a[1:], k, wr, rd)
                    else:
                        ser._collect(a, k, wr, rd)
                    if name == "matmul" and not k.get("start", True):
                        rd.extend(wr)
                    ser._issue(E_, rd, wr)
                    r = fn(*a, **k)
                    r.then_inc(ser.sems[E_], 1)
                    ser.cnt[E_] += 1
                    ser._finish(E_, rd, wr, ser.cnt[E_])
                    return r
                return call
        return W()

    @property
    def v(self):
        return self._wrap(self.eng["V"], "V")

    @property
    def s(self):
        return self._wrap(self.eng["S"], "S")

    @property
    def t(self):
        return self._wrap(self.eng["T"], "T")

    def dma(self, out, in_, cast=False):
        E_ = "G" if cast else "Q"
        e = self.eng[E_]
        key = E_ + str(self.qidx[E_] % self.QW)
        self.qidx[E_] += 1
        wr = [self._box(out)]
        rd = [self._box(in_)]
        self._issue(E_, rd, wr)
        if self.cnt[key] > 0 and self.wm[E_].get(key, 0) < self.cnt[key]:
            e.wait_ge(self.sems[key], self.cnt[key])
            self.wm[E_][key] = self.cnt[key]
        e.dma_start(out=out, in_=in_).then_inc(self.sems[key], 16)
        self.cnt[key] += 16
        self._finish(key, rd, wr, self.cnt[key])

    def cc(self, kind, ins, outs):
        e = self.eng["G"]
        csem = self.E(self.nc.semaphore("ccsem%d" % self.cn))
        self.cn += 1
        wr = [self._box(o) for o in outs]
        rd = [self._box(i) for i in ins]
        self._issue("G", rd, wr)
        e.collective_compute(kind, ALU.bypass, replica_groups=[list(range(8))],
                             ins=[i.opt() for i in ins],
                             outs=[o.opt() for o in outs]).then_inc(csem, 1)
        self.sems[csem] = csem
        self.cnt[csem] = 1
        self._finish(csem, rd, wr, 1)

    def _emit_waits_all(self, q):
        for F, c in self.cnt.items():
            if c > 0:
                q.wait_ge(self.sems[F], c)


def emit_quant(Sr, at, outq, N, q, gs6, gsi, bf16_s):
    v = Sr.v
    nb = N // 16

    def nop():
        v.engine_nop()

    av_all = at.rearrange("p (b k) -> p b k", k=16)
    v.tensor_reduce(out=q["b0"][:, :nb], in_=av_all, axis=AX, op=ALU.max,
                    apply_absolute_value=True)
    nop(); nop()
    v.tensor_scalar(out=q["b1"][:, :nb], in0=q["b0"][:, :nb], scalar1=gs6,
                    scalar2=448.0, op0=ALU.mult, op1=ALU.min)
    nop()
    v.memset(q["b2"][:, :nb], 2.0 ** -9)
    pp, cur = [q["b2"], q["b3"]], 0
    for i in range(-5, 9):
        v.tensor_scalar(out=q["b4"][:, :nb], in0=q["b1"][:, :nb],
                        scalar1=float(2.0 ** i), scalar2=1.0, op0=ALU.is_ge, op1=ALU.add)
        nop()
        v.tensor_tensor(out=pp[1 - cur][:, :nb], in0=pp[cur][:, :nb],
                        in1=q["b4"][:, :nb], op=ALU.mult)
        nop()
        cur = 1 - cur
    q2, spare = pp[cur], pp[1 - cur]
    v.reciprocal(out=q["b4"][:, :nb], in_=q2[:, :nb])
    nop()
    v.tensor_mul(out=spare[:, :nb], in0=q["b4"][:, :nb], in1=q["b1"][:, :nb])
    nop()
    v.memset(q["b1"][:, :nb], 0.0)
    pp2, cur2 = [q["b1"], q["b4"]], 0
    for j in range(16):
        v.scalar_tensor_tensor(out=pp2[1 - cur2][:, :nb], in0=spare[:, :nb],
                               scalar=j + 0.5, in1=pp2[cur2][:, :nb],
                               op0=ALU.is_ge, op1=ALU.add)
        nop()
        cur2 = 1 - cur2
    acc = pp2[cur2]
    v.tensor_tensor(out=spare[:, :nb], in0=acc[:, :nb], in1=q2[:, :nb], op=ALU.mult)
    nop()
    v.tensor_scalar(out=q["b0"][:, :nb], in0=spare[:, :nb], scalar1=gsi,
                    scalar2=None, op0=ALU.mult)
    nop()
    v.tensor_scalar(out=q["b4"][:, :nb], in0=q["b0"][:, :nb], scalar1=1e-30,
                    scalar2=None, op0=ALU.max)
    nop()
    v.reciprocal(out=q["b1"][:, :nb], in_=q["b4"][:, :nb])
    nop(); nop()
    for hh in range(N // 512):
        c0, nb0 = 512 * hh, 32 * hh
        av = at[:, c0:c0 + 512].rearrange("p (b k) -> p b k", k=16)
        uv = q["u"].rearrange("p (b k) -> p b k", k=16)
        v.tensor_tensor(out=uv, in0=av,
                        in1=q["b1"][:, nb0:nb0 + 32].unsqueeze(2).broadcast_to([128, 32, 16]),
                        op=ALU.mult)
        sdst = q["sb"] if bf16_s else q["s"]
        v.tensor_tensor(out=sdst, in0=q["u"], in1=q["u"], op=ALU.mult)
        v.tensor_scalar(out=q["qA"], in0=sdst, scalar1=THRQ[0], scalar2=None, op0=ALU.is_ge)
        pa_, cu = [q["qA"], q["qC"]], 0
        for thr in THRQ[1:4]:
            v.scalar_tensor_tensor(out=pa_[1 - cu], in0=sdst, scalar=thr,
                                   in1=pa_[cu], op0=ALU.is_ge, op1=ALU.add)
            cu = 1 - cu
        qa, sparea = pa_[cu], pa_[1 - cu]
        v.tensor_scalar(out=q["qB"], in0=sdst, scalar1=THRQ[4], scalar2=None, op0=ALU.is_ge)
        pb_, cu2 = [q["qB"], q["qD"]], 0
        for thr in (THRQ[5], THRQ[6], THRQ[6]):
            v.scalar_tensor_tensor(out=pb_[1 - cu2], in0=sdst, scalar=thr,
                                   in1=pb_[cu2], op0=ALU.is_ge, op1=ALU.add)
            cu2 = 1 - cu2
        qb = pb_[cu2]
        v.scalar_tensor_tensor(out=sparea, in0=qa, scalar=0.5, in1=qb,
                               op0=ALU.mult, op1=ALU.add)
        v.tensor_scalar(out=q["qB"], in0=q["u"], scalar1=0.0, scalar2=2.0,
                        op0=ALU.is_ge, op1=ALU.mult)
        qv = q["qd"].rearrange("p (b k) -> p b k", k=16)
        v.tensor_tensor(out=qv, in0=sparea.rearrange("p (b k) -> p b k", k=16),
                        in1=q["b0"][:, nb0:nb0 + 32].unsqueeze(2).broadcast_to([128, 32, 16]),
                        op=ALU.mult)
        v.tensor_scalar(out=q["qD"], in0=q["qB"], scalar1=1.0, scalar2=None, op0=ALU.subtract)
        nop()
        v.tensor_tensor(out=outq[:, c0:c0 + 512], in0=q["qd"], in1=q["qD"], op=ALU.mult)


def build_nc():
    nc = bass.Bass("TRN2", target_bir_lowering=False)
    stk = contextlib.ExitStack()
    E = stk.enter_context

    d = {}
    specs = dict(
        x_sh=([T + HALO, D], F32), sel=([128, 8], F32), mask=([128, 1], F32),
        cond_in_W=([D, BW], F32), delta_W=([BW, 3 * D + H], F32), cnw=([BW, 1], F32),
        theta=([1, 64], F32), ctrl_W=([D, 4128], F32), cbias=([128, 34], F32),
        rampq=([128, 24], F32), beta_ramp=([H, 1], F32), convw=([128, 96], F32),
        onw2=([128, 1], F32), Wo=([D, D], F32), bo=([1, D], F32),
        gw=([I_FFN, D], F32), uw=([I_FFN, D], F32), dw=([D, I_FFN], F32),
        s_in=([128, 1], F32), s_in_d=([128, 1], F32),
        ident=([128, 128], BF16), ident_f=([128, 128], F32),
        trioneB=([128, 128], BF16), trioneF=([128, 128], F32),
        maskMc=([128, 128], F32), maskAc=([128, 128], F32), iotar=([1, 128], F32),
        ones1f=([1, 512], F32), ones1b=([1, 512], BF16), onescb=([128, 1], BF16),
        i64c=([64, 64], F32), e127=([128, 1], F32), e2=([128, 1], F32),
    )
    for k, (shp, dt) in specs.items():
        d[k] = nc.dram_tensor(k, shp, dt, kind="ExternalInput").ap()
    y_out = nc.dram_tensor("y_sh", [T, D], F32, kind="ExternalOutput").ap()

    dr = dict(
        oloc=nc.dram_tensor("oloc_d", [T, D], BF16).ap(),
        gneg=nc.dram_tensor("gneg_d", [NCH * H * 64, 128], BF16).ap(),
        ptqc=nc.dram_tensor("ptqc_d", [NCH * H * 64, 128], BF16).ap(),
        gffn=nc.dram_tensor("gffn_d", [T, D], BF16).ap(),
        hq=nc.dram_tensor("hq_d", [T, I_FFN], BF16).ap(),
        gwq=nc.dram_tensor("gwq_d", [D, I_FFN], BF16).ap(),
        uwq=nc.dram_tensor("uwq_d", [D, I_FFN], BF16).ap(),
        dwq=nc.dram_tensor("dwq_d", [I_FFN, D], BF16).ap(),
    )
    dr["cc1i"] = nc.dram_tensor("cc1_in", [4, 128], F32)
    dr["cc1o"] = nc.dram_tensor("cc1_out", [32, 128], F32)
    dr["cc2i"] = nc.dram_tensor("cc2_in", [H * 64, 64], F32)
    dr["cc2o"] = nc.dram_tensor("cc2_out", [8 * H * 64, 64], F32)

    sb = {}

    def A(name, shape, dt):
        sb[name] = E(nc.sbuf_tensor("s_" + name, shape, dt))

    A("bigA", [128, 16384], BF16)
    A("bigB", [128, 16384], BF16)
    A("bigC", [128, 16384], BF16)
    A("bigD", [128, 16384], BF16)
    A("bigE", [128, 8192], BF16)
    A("cndf", [128, 2048], BF16)
    A("condW", [128, 1024], BF16)
    A("deltW", [128, 3088], BF16)
    A("convw", [128, 96], F32)
    A("rampq", [128, 24], F32)
    A("rampb", [16, 1], F32)
    A("cbias", [128, 34], F32)
    A("tabc", [128, 64], F32)
    A("tabs", [128, 64], F32)
    A("wslab", [128, 1024], BF16)
    A("xt", [128, 1024], F32)
    A("xrow", [128, 1024], F32)
    A("selw", [128, 8], F32)
    A("maskc", [128, 1], F32)
    A("gs6", [128, 1], F32); A("gsi", [128, 1], F32)
    A("gs6d", [128, 1], F32); A("gsid", [128, 1], F32)
    A("gs6w", [128, 1], F32); A("gsiw", [128, 1], F32)
    A("sc1", [128, 1], F32); A("sc2", [128, 1], F32); A("sc3", [128, 1], F32)
    A("ceps5", [128, 1], F32); A("ceps6", [128, 1], F32); A("cone", [128, 1], F32)
    A("onescf", [128, 1], F32); A("lamC", [1, 16], F32); A("e127", [128, 1], F32)
    A("e2", [128, 1], F32)
    A("sc4", [128, 1], F32)
    A("ln1", [128, 1], F32); A("ln2", [128, 1], F32)
    A("iotar", [1, 128], F32)
    A("trioneB", [128, 128], BF16)
    A("trioneF", [128, 128], F32)
    A("ident", [128, 128], BF16)
    A("ident_f", [128, 128], F32)
    A("maskM", [128, 128], F32)
    A("maskA", [128, 128], F32)
    A("ones1f", [1, 512], F32)
    A("ones1b", [1, 512], BF16)
    A("onescb", [128, 1], BF16)
    A("i64b", [64, 64], F32)
    A("onwc", [128, 1], F32)
    A("bo16", [1, 1024], BF16)
    A("thetar", [1, 64], F32)
    A("z0row", [1, 128], F32)
    A("tt1", [128, 128], F32); A("tt2", [128, 128], F32)
    A("tt3", [128, 128], F32); A("tt4", [128, 64], F32)
    A("upk", [128, 128], BF16)
    A("haloz", [4, 128], F32); A("halo2", [4, 128], F32)
    A("condh", [128, 3], BF16)
    A("xnhf", [128, 24], BF16)
    A("pre", [128, 520], F32)
    A("tails", [128, 72], F32)
    A("convo", [128, 512], F32)
    A("hid", [128, 512], F32)
    A("dbt", [128, 128], F32)
    A("lng", [128, 16], F32); A("lnb", [128, 16], F32); A("beta", [128, 16], F32)
    A("lam", [128, 16], F32); A("l1q", [128, 16], F32); A("l1k", [128, 16], F32)
    A("rowM", [128, 16], F32); A("rowA", [128, 16], F32); A("colv", [128, 16], F32)
    A("coefk", [128, 16], F32); A("coefp", [128, 16], F32)
    A("t16a", [128, 16], F32); A("t16b", [128, 16], F32)
    A("eAall", [128, 256], F32)
    A("gCc", [64, 16], F32)
    A("pk2a", [128, 2], F32); A("pk2b", [128, 2], F32); A("pk2c", [128, 2], F32)
    A("stk2", [2, 128], F32); A("stk2r", [2, 128], F32); A("stk2q", [2, 128], F32)
    A("sq16", [128, 128], BF16)
    A("AqS", [128, 512], BF16)
    A("Ua", [128, 512], BF16); A("Ub", [128, 512], BF16)
    A("Pa", [128, 512], BF16); A("Pb", [128, 512], BF16)
    A("PT", [128, 512], BF16)
    A("kpt", [128, 256], BF16)
    A("t64", [64, 64], F32)
    A("pk64", [64, 512], BF16)
    A("opre", [128, 64], F32); A("opre2", [128, 64], F32)
    A("ot64", [128, 64], BF16)

    ps = {}
    for name, dt in (("pa", F32), ("pb", F32), ("pc", F32), ("pd", F32),
                     ("pe", F32), ("pf", F32)):
        ps[name] = E(nc.psum_tensor(name, [128, 512], dt))
    ps["pt"] = E(nc.psum_tensor("pt", [128, 512], BF16))

    blk = E(nc.Block())

    @blk.sync
    def _(sync):
        Sr = Ser(nc, E)
        _emit(nc, Sr, d, y_out, sb, ps, dr)
        Sr._emit_waits_all(nc.sync)

    stk.close()
    return nc


def _emit(nc, Sr, d, y_out, sb, ps, dr):
    pa, pb, pc, pd, pe, pf = (ps[k][:, :] for k in ("pa", "pb", "pc", "pd", "pe", "pf"))
    pt = ps["pt"][:, :]
    bigA, bigB, bigC, bigD, bigE = (sb[k][:, :] for k in
                                    ("bigA", "bigB", "bigC", "bigD", "bigE"))
    bigC_f = bigC.bitcast(F32)
    bigD_f = bigD.bitcast(F32)
    xt2v = bigD_f[:, 0:1024]
    vw = {"qu": bigD_f[:, 1024:1536], "qqd": bigD_f[:, 1536:2048],
          "qs": bigD_f[:, 2048:2560], "qqA": bigD_f[:, 2560:3072],
          "qqB": bigD_f[:, 3072:3584], "qqC": bigD_f[:, 3584:4096],
          "qqD": bigD_f[:, 4096:4608],
          "qb0": bigD_f[:, 4608:4672], "qb1": bigD_f[:, 4672:4736],
          "qb2": bigD_f[:, 4736:4800], "qb3": bigD_f[:, 4800:4864],
          "qb4": bigD_f[:, 4864:4928],
          "convb": bigD[:, 9856:10368], "xb16": bigD[:, 10368:11392]}
    p_sb = bigC_f[:, 0:2048]
    z_sb = bigC_f[:, 2048:4096]
    ident = sb["ident"][:, :]
    identf = sb["ident_f"][:, :]

    def ap(n):
        return sb[n][:, :]

    qd = dict(b0=vw["qb0"], b1=vw["qb1"], b2=vw["qb2"], b3=vw["qb3"], b4=vw["qb4"],
              u=vw["qu"], qd=vw["qqd"], s=vw["qs"],
              sb=vw["qs"].bitcast(BF16)[:, 0:512],
              qA=vw["qqA"], qB=vw["qqB"], qC=vw["qqC"], qD=vw["qqD"])

    def vnop(k=2):
        v = Sr.v
        for _ in range(k):
            v.engine_nop()

    def snop():
        s_ = Sr.s
        s_.nop(); s_.nop()

    def idslice(base, n):
        return sb["ident"][base:base + n, base:base + n]

    def idslicef(base, n):
        return sb["ident_f"][base:base + n, base:base + n]

    v = Sr.v
    v.memset(sb["ceps5"][:, :], 1e-5)
    v.memset(sb["ceps6"][:, :], 1e-6)
    v.memset(sb["cone"][:, :], 1.0)
    v.memset(sb["onescf"][:, :], 1.0)
    v.memset(sb["pk2a"][:, :], 1.0)
    v.memset(sb["pk2b"][:, :], 1.0)
    v.memset(sb["pk2c"][:, :], 1.0)
    vnop()
    for nm in ("ident", "ident_f", "trioneB", "trioneF", "iotar", "ones1f",
               "ones1b", "onescb", "convw", "cbias", "rampq"):
        Sr.dma(ap(nm), d[nm])
    Sr.dma(ap("selw"), d["sel"])
    Sr.dma(ap("e127"), d["e127"])
    Sr.dma(ap("e2"), d["e2"])
    Sr.dma(ap("maskM"), d["maskMc"])
    Sr.dma(ap("maskA"), d["maskAc"])
    Sr.dma(ap("i64b"), d["i64c"])
    Sr.dma(ap("maskc"), d["mask"])
    Sr.dma(ap("thetar"), d["theta"])
    Sr.dma(sb["onwc"][:, :], d["onw2"])
    Sr.dma(sb["condW"][:, :].rearrange("p (dc w) -> p dc w", dc=8),
           d["cond_in_W"].rearrange("(dc p) w -> p dc w", p=128), cast=True)
    Sr.dma(ap("sc1"), d["cnw"])
    for c0, w in ((0, 1024), (1024, 1024), (2048, 1024), (3072, 16)):
        Sr.dma(sb["xt"][:, 0:w], d["delta_W"][:, c0:c0 + w])
        v = Sr.v
        v.tensor_scalar(out=sb["deltW"][:, c0:c0 + w], in0=sb["xt"][:, 0:w],
                        scalar1=sb["sc1"][:, 0:1], scalar2=None, op0=ALU.mult)
        vnop()
    s_ = Sr.s
    s_.activation(out=xt2v[:, 0:24], in_=sb["rampq"][:, 0:24], func=AF.Tanh, scale=0.5)
    snop()
    v = Sr.v
    v.tensor_scalar(out=ap("rampq"), in0=xt2v[:, 0:24], scalar1=0.5,
                    scalar2=0.5, op0=ALU.mult, op1=ALU.add)
    vnop()
    Sr.dma(sb["xt"][0:16, 0:1], d["beta_ramp"])
    s_ = Sr.s
    s_.activation(out=xt2v[0:16, 0:1], in_=sb["xt"][0:16, 0:1], func=AF.Tanh, scale=0.5)
    snop()
    v = Sr.v
    v.tensor_scalar(out=ap("rampb"), in0=xt2v[0:16, 0:1], scalar1=0.5,
                    scalar2=0.5, op0=ALU.mult, op1=ALU.add)
    vnop()
    Sr.dma(ap("sc1"), d["s_in"])
    Sr.dma(ap("sc2"), d["s_in_d"])
    v = Sr.v
    v.tensor_scalar(out=ap("gs6"), in0=ap("sc1"), scalar1=1.0 / 6.0, scalar2=None, op0=ALU.mult)
    v.tensor_scalar(out=ap("gs6d"), in0=ap("sc2"), scalar1=1.0 / 6.0, scalar2=None, op0=ALU.mult)
    vnop()
    v.reciprocal(out=ap("gsi"), in_=ap("sc1"))
    vnop()
    v.reciprocal(out=ap("gsid"), in_=ap("sc2"))
    vnop()
    Sr.dma(sb["bo16"][:, :], d["bo"], cast=True)
    t_ = Sr.t
    t_.matmul(out=pf[:, 0:64], lhsT=sb["iotar"][0:1, :], rhs=ap("thetar"),
              start=True, stop=True)
    _rot_table(Sr, sb, pf[:, 0:64], "tabs", 0.0)
    t_ = Sr.t
    t_.matmul(out=pf[:, 64:128], lhsT=sb["iotar"][0:1, :], rhs=ap("thetar"),
              start=True, stop=True)
    _rot_table(Sr, sb, pf[:, 64:128], "tabc", PI / 2.0)

    Sr.phase = 10
    for tc in range(NCH):
        Sr.dma(ap("xt"), d["x_sh"][HALO + 128 * tc:HALO + 128 * (tc + 1), :])
        _ln(Sr, sb, ap("xt"), xt2v, 128)
        v = Sr.v
        v.tensor_copy(out=vw["xb16"], in_=xt2v)
        _tr_to(Sr, sb, ps, vw["xb16"], bigA, tc, 8)
        emit_quant(Sr, xt2v, vw["xb16"], 1024, qd,
                   sb["gs6"][:, 0:1], sb["gsi"][:, 0:1], bf16_s=False)
        _tr_to(Sr, sb, ps, vw["xb16"], bigB, tc, 8)
    Sr.dma(sb["xt"][0:3, :], d["x_sh"][0:HALO, :])
    _ln(Sr, sb, sb["xt"][0:3, :], sb["xt"][0:3, :], 3)
    v = Sr.v
    v.tensor_copy(out=vw["xb16"][0:3, :], in_=sb["xt"][0:3, :])
    vnop()
    for dc in range(8):
        t_ = Sr.t
        t_.transpose(out=ps["pt"][:, 0:3], in_=vw["xb16"][0:3, 128 * dc:128 * (dc + 1)],
                     identity=idslice(0, 3))
        v = Sr.v
        v.tensor_copy(out=sb["xnhf"][:, 3 * dc:3 * dc + 3], in_=ps["pt"][:, 0:3])
        vnop()

    Sr.phase = 20
    for tc in range(NCH):
        t_ = Sr.t
        for dc in range(8):
            t_.matmul(out=pf[:, 0:128],
                      lhsT=bigA[:, 2048 * dc + 128 * tc:2048 * dc + 128 * (tc + 1)],
                      rhs=sb["condW"][:, 128 * dc:128 * (dc + 1)],
                      start=(dc == 0), stop=(dc == 7))
        v = Sr.v
        v.tensor_copy(out=p_sb[:, 128 * tc:128 * (tc + 1)], in_=pf[:, 0:128])
    v = Sr.v
    v.memset(sb["z0row"][:, :], 0.0)
    vnop()
    _rot_pass(Sr, sb, ps, p_sb, z_sb)
    Sr.dma(dr["cc1i"].ap()[0:3, :], z_sb[125:128, 128 * 15:128 * 16])
    Sr.dma(dr["cc1i"].ap()[3:4, :], z_sb[127:128, 128 * 15:128 * 16])
    Sr.cc("AllGather", [dr["cc1i"].ap()], [dr["cc1o"].ap()])
    Sr.dma(sb["xt"][0:4, 0:1024].rearrange("p (r c) -> p r c", c=128),
           dr["cc1o"].ap().rearrange("(r p) c -> p r c", p=4))
    v = Sr.v
    v.memset(sb["haloz"][0:3, :], 0.0)
    vnop()
    accp, cur = [sb["haloz"], sb["halo2"]], 0
    for r in range(8):
        v.scalar_tensor_tensor(out=accp[1 - cur][0:3, :],
                               in0=sb["xt"][0:3, 128 * r:128 * (r + 1)],
                               scalar=sb["selw"][0:3, r:r + 1], in1=accp[cur][0:3, :],
                               op0=ALU.mult, op1=ALU.add)
        vnop()
        cur = 1 - cur
    zrecv = accp[cur][0:3, :]
    zscr = accp[1 - cur]
    t_ = Sr.t
    t_.matmul(out=ps["pf"][0:1, 128:256], lhsT=sb["e2"][0:3, 0:1], rhs=zrecv,
              start=True, stop=True)
    v = Sr.v
    v.tensor_copy(out=sb["z0row"][:, :], in_=ps["pf"][0:1, 128:256])
    vnop()
    v.tensor_tensor(out=zscr[0:3, :], in0=zrecv, in1=zrecv, op=ALU.mult)
    vnop()
    v.tensor_reduce(out=sb["ln1"][0:3, :], in_=zscr[0:3, :], axis=AX, op=ALU.add)
    vnop()
    s_ = Sr.s
    s_.activation(out=sb["ln2"][0:3, :], in_=sb["ln1"][0:3, :], func=AF.Ln,
                  scale=1.0 / 128.0, bias=sb["ceps6"][0:3, 0:1])
    snop()
    s_ = Sr.s
    s_.activation(out=sb["ln1"][0:3, :], in_=sb["ln2"][0:3, :], func=AF.Exp, scale=-0.5)
    snop()
    v = Sr.v
    v.tensor_scalar(out=zscr[0:3, :], in0=zrecv, scalar1=sb["ln1"][0:3, 0:1],
                    scalar2=None, op0=ALU.mult)
    vnop()
    v.tensor_copy(out=vw["xb16"][0:3, 0:128], in_=zscr[0:3, :])
    vnop()
    t_ = Sr.t
    t_.transpose(out=ps["pt"][:, 0:3], in_=vw["xb16"][0:3, 0:128], identity=idslice(0, 3))
    v = Sr.v
    v.tensor_copy(out=ap("condh"), in_=ps["pt"][:, 0:3])
    vnop()
    _rot_pass(Sr, sb, ps, p_sb, z_sb)
    for tc in range(NCH):
        zc = z_sb[:, 128 * tc:128 * (tc + 1)]
        v = Sr.v
        v.tensor_tensor(out=ap("tt1"), in0=zc, in1=zc, op=ALU.mult)
        v.tensor_reduce(out=ap("ln1"), in_=ap("tt1"), axis=AX, op=ALU.add)
        vnop()
        s_ = Sr.s
        s_.activation(out=ap("ln2"), in_=ap("ln1"), func=AF.Ln, scale=1.0 / 128.0,
                      bias=sb["ceps6"][:, 0:1])
        snop()
        s_ = Sr.s
        s_.activation(out=ap("ln1"), in_=ap("ln2"), func=AF.Exp, scale=-0.5)
        snop()
        v = Sr.v
        v.tensor_scalar(out=ap("upk"), in0=zc, scalar1=sb["ln1"][:, 0:1],
                        scalar2=None, op0=ALU.mult)
        vnop()
        t_ = Sr.t
        t_.transpose(out=pt[:, 0:128], in_=ap("upk"), identity=ident)
        v = Sr.v
        v.tensor_copy(out=sb["cndf"][:, 128 * tc:128 * (tc + 1)], in_=pt[:, 0:128])
        vnop()

    Sr.phase = 30
    _ffn(Sr, sb, ps, d, dr, qd, vw)

    for sl in range(4):
        Sr.phase = 100 * (sl + 1)
        _p4_slab(Sr, sb, ps, d, dr, sl, vw)
        Sr.phase = 100 * (sl + 1) + 90
        _p5_slab(Sr, sb, ps, d, dr, sl, vw)

    Sr.phase = 600
    _p7(Sr, sb, ps, d, dr)

    Sr.phase = 610
    _p8(Sr, sb, ps, d, dr, y_out, vw)


def _rot_table(Sr, sb, psrc, dst, phase):
    v = Sr.v
    v.tensor_scalar(out=sb["tt2"][:, 0:64], in0=psrc, scalar1=phase, scalar2=None, op0=ALU.add)
    v.engine_nop(); v.engine_nop()
    v.tensor_scalar(out=sb["tt3"][:, 0:64], in0=sb["tt2"][:, 0:64],
                    scalar1=1.0 / TWO_PI, scalar2=None, op0=ALU.mult)
    v.engine_nop(); v.engine_nop()
    v.memset(sb["tt1"][:, 0:64], 0.0)
    accp, cur = [sb["tt1"], sb["tt4"]], 0
    for k in range(1, 12):
        v.scalar_tensor_tensor(out=accp[1 - cur][:, 0:64], in0=sb["tt3"][:, 0:64],
                               scalar=float(k), in1=accp[cur][:, 0:64],
                               op0=ALU.is_ge, op1=ALU.add)
        v.engine_nop()
        cur = 1 - cur
    fl = accp[cur]
    v.scalar_tensor_tensor(out=sb["tt3"][:, 0:64], in0=fl[:, 0:64], scalar=-TWO_PI,
                           in1=sb["tt2"][:, 0:64], op0=ALU.mult, op1=ALU.add)
    v.engine_nop(); v.engine_nop()
    v.tensor_scalar(out=sb["tt1"][:, 0:64], in0=sb["tt3"][:, 0:64], scalar1=PI,
                    scalar2=-TWO_PI, op0=ALU.is_ge, op1=ALU.mult)
    v.engine_nop(); v.engine_nop()
    v.tensor_tensor(out=sb["tt2"][:, 0:64], in0=sb["tt3"][:, 0:64],
                    in1=sb["tt1"][:, 0:64], op=ALU.add)
    v.engine_nop(); v.engine_nop()
    s_ = Sr.s
    s_.activation(out=sb[dst][:, :], in_=sb["tt2"][:, 0:64], func=AF.Sin)
    s_.nop(); s_.nop()


def _ln(Sr, sb, src, dst, P):
    v = Sr.v
    v.tensor_reduce(out=sb["ln1"][0:P, :], in_=src, axis=AX, op=ALU.add)
    v.engine_nop(); v.engine_nop()
    v.tensor_scalar(out=sb["ln2"][0:P, :], in0=sb["ln1"][0:P, :], scalar1=1.0 / 1024.0,
                    scalar2=None, op0=ALU.mult)
    v.engine_nop(); v.engine_nop()
    v.tensor_scalar(out=sb["xrow"][0:P, :], in0=src, scalar1=sb["ln2"][0:P, 0:1],
                    scalar2=None, op0=ALU.subtract)
    v.engine_nop(); v.engine_nop()
    v.tensor_tensor(out=sb["convo"][0:P, :], in0=sb["xrow"][0:P, 0:512],
                    in1=sb["xrow"][0:P, 0:512], op=ALU.mult)
    v.tensor_tensor(out=sb["hid"][0:P, :], in0=sb["xrow"][0:P, 512:1024],
                    in1=sb["xrow"][0:P, 512:1024], op=ALU.mult)
    v.tensor_reduce(out=sb["sc3"][0:P, :], in_=sb["convo"][0:P, :], axis=AX, op=ALU.add)
    v.tensor_reduce(out=sb["sc4"][0:P, :], in_=sb["hid"][0:P, :], axis=AX, op=ALU.add)
    v.engine_nop(); v.engine_nop()
    v.tensor_tensor(out=sb["ln1"][0:P, :], in0=sb["sc3"][0:P, :], in1=sb["sc4"][0:P, :],
                    op=ALU.add)
    v.engine_nop(); v.engine_nop()
    s_ = Sr.s
    s_.activation(out=sb["ln2"][0:P, :], in_=sb["ln1"][0:P, :], func=AF.Ln,
                  scale=1.0 / 1024.0, bias=sb["ceps5"][0:P, 0:1])
    s_.nop(); s_.nop()
    s_ = Sr.s
    s_.activation(out=sb["ln1"][0:P, :], in_=sb["ln2"][0:P, :], func=AF.Exp, scale=-0.5)
    s_.nop(); s_.nop()
    v = Sr.v
    v.tensor_scalar(out=dst, in0=sb["xrow"][0:P, :], scalar1=sb["ln1"][0:P, 0:1],
                    scalar2=None, op0=ALU.mult)
    v.engine_nop(); v.engine_nop()


def _tr_to(Sr, sb, ps, src_bf, dst, tc, ndc):
    for dc in range(ndc):
        t_ = Sr.t
        t_.transpose(out=ps["pt"][:, 0:128], in_=src_bf[:, 128 * dc:128 * (dc + 1)],
                     identity=sb["ident"][:, :])
        v = Sr.v
        v.tensor_copy(out=dst[:, 2048 * dc + 128 * tc:2048 * dc + 128 * (tc + 1)],
                      in_=ps["pt"][:, 0:128])


def _rot_pass(Sr, sb, ps, p_sb, z_sb):
    pf = ps["pf"][:, :]
    for tc in range(NCH):
        pcv = p_sb[:, 128 * tc:128 * (tc + 1)]
        re, im = pcv[:, 0:64], pcv[:, 64:128]
        v = Sr.v
        v.tensor_tensor(out=sb["tt1"][:, 0:64], in0=sb["tabc"][:, :], in1=re, op=ALU.mult)
        v.tensor_tensor(out=sb["tt2"][:, 0:64], in0=sb["tabs"][:, :], in1=im, op=ALU.mult)
        v.tensor_tensor(out=sb["upk"][:, 0:64], in0=sb["tt1"][:, 0:64],
                        in1=sb["tt2"][:, 0:64], op=ALU.add)
        v.tensor_tensor(out=sb["tt1"][:, 0:64], in0=sb["tabc"][:, :], in1=im, op=ALU.mult)
        v.tensor_tensor(out=sb["tt2"][:, 0:64], in0=sb["tabs"][:, :], in1=re, op=ALU.mult)
        v.tensor_tensor(out=sb["upk"][:, 64:128], in0=sb["tt1"][:, 0:64],
                        in1=sb["tt2"][:, 0:64], op=ALU.subtract)
        t_ = Sr.t
        t_.matmul(out=pf[:, 0:128], lhsT=sb["trioneB"][:, :], rhs=sb["upk"][:, :],
                  start=True, stop=False)
        t_.matmul(out=pf[:, 0:128], lhsT=sb["ones1f"][0:1, 0:128], rhs=sb["z0row"][:, :],
                  start=False, stop=True)
        v = Sr.v
        v.tensor_tensor(out=sb["tt1"][:, 0:64], in0=sb["tabc"][:, :], in1=pf[:, 0:64], op=ALU.mult)
        v.tensor_tensor(out=sb["tt2"][:, 0:64], in0=sb["tabs"][:, :], in1=pf[:, 64:128], op=ALU.mult)
        v.tensor_tensor(out=z_sb[:, 128 * tc:128 * tc + 64], in0=sb["tt1"][:, 0:64],
                        in1=sb["tt2"][:, 0:64], op=ALU.subtract)
        v.tensor_tensor(out=sb["tt1"][:, 0:64], in0=sb["tabs"][:, :], in1=pf[:, 0:64], op=ALU.mult)
        v.tensor_tensor(out=sb["tt2"][:, 0:64], in0=sb["tabc"][:, :], in1=pf[:, 64:128], op=ALU.mult)
        v.tensor_tensor(out=z_sb[:, 128 * tc + 64:128 * (tc + 1)], in0=sb["tt1"][:, 0:64],
                        in1=sb["tt2"][:, 0:64], op=ALU.add)
        t_ = Sr.t
        t_.matmul(out=pf[0:1, 128:256], lhsT=sb["e127"][:, 0:1],
                  rhs=z_sb[:, 128 * tc:128 * (tc + 1)], start=True, stop=True)
        v = Sr.v
        v.tensor_copy(out=sb["z0row"][:, :], in_=pf[0:1, 128:256])
        v.engine_nop(); v.engine_nop()


def _ffn(Sr, sb, ps, d, dr, qdict, vw):
    pa, pb, pf = ps["pa"][:, :], ps["pb"][:, :], ps["pf"][:, :]
    bigB, bigC, bigE = sb["bigB"][:, :], sb["bigC"][:, :], sb["bigE"][:, :]

    def vnop(k=2):
        v = Sr.v
        for _ in range(k):
            v.engine_nop()

    for (src, dst, R, Cw) in ((d["gw"], dr["gwq"], I_FFN, D),
                              (d["uw"], dr["uwq"], I_FFN, D),
                              (d["dw"], dr["dwq"], D, I_FFN)):
        v = Sr.v
        v.memset(sb["sc1"][:, :], 0.0)
        vnop()
        accp, cur = [sb["sc1"], sb["sc2"]], 0
        for r in range(R // 128):
            for p_ in range(Cw // 1024):
                Sr.dma(sb["xrow"][:, :], src[128 * r:128 * (r + 1), 1024 * p_:1024 * (p_ + 1)])
                v = Sr.v
                v.tensor_reduce(out=sb["sc3"][:, :], in_=sb["xrow"][:, :], axis=AX,
                                op=ALU.max, apply_absolute_value=True)
                vnop()
                v.tensor_tensor(out=accp[1 - cur][:, :], in0=accp[cur][:, :],
                                in1=sb["sc3"][:, :], op=ALU.max)
                vnop()
                cur = 1 - cur
        t_ = Sr.t
        t_.transpose(out=ps["pe"][0:1, 0:128], in_=accp[cur][:, 0:1], identity=sb["ident_f"][:, :])
        v = Sr.v
        v.tensor_reduce(out=sb["sc3"][0:1, :], in_=ps["pe"][0:1, 0:128], axis=AX, op=ALU.max)
        vnop()
        v.reciprocal(out=sb["sc4"][0:1, :], in_=sb["sc3"][0:1, :])
        vnop()
        v.tensor_scalar(out=sb["sc3"][0:1, :], in0=sb["sc4"][0:1, :], scalar1=2688.0,
                        scalar2=None, op0=ALU.mult)
        vnop()
        t_ = Sr.t
        t_.matmul(out=ps["pe"][:, 0:1], lhsT=sb["ones1f"][0:1, 0:128],
                  rhs=sb["sc3"][0:1, 0:1], start=True, stop=True)
        v = Sr.v
        v.tensor_scalar(out=sb["gs6w"][:, :], in0=ps["pe"][:, 0:1], scalar1=1.0 / 6.0,
                        scalar2=None, op0=ALU.mult)
        vnop()
        v.tensor_copy(out=sb["sc4"][:, :], in_=ps["pe"][:, 0:1])
        vnop()
        v.reciprocal(out=sb["gsiw"][:, :], in_=sb["sc4"][:, :])
        vnop()
        for r in range(R // 128):
            for p_ in range(Cw // 1024):
                Sr.dma(sb["xrow"][:, :], src[128 * r:128 * (r + 1), 1024 * p_:1024 * (p_ + 1)])
                emit_quant(Sr, sb["xrow"][:, :], vw["xb16"][:, :], 1024, qdict,
                           sb["gs6w"][:, 0:1], sb["gsiw"][:, 0:1], bf16_s=False)
                for sub in range(8):
                    t_ = Sr.t
                    t_.transpose(out=ps["pt"][:, 0:128],
                                 in_=vw["xb16"][:, 128 * sub:128 * (sub + 1)],
                                 identity=sb["ident"][:, :])
                    v = Sr.v
                    v.tensor_copy(out=sb["wslab"][:, 128 * sub:128 * (sub + 1)],
                                  in_=ps["pt"][:, 0:128])
                if Cw == D:
                    outap = dst.rearrange("(dc p) i -> p dc i", p=128)[:, :, 128 * r:128 * (r + 1)]
                else:
                    outap = dst.rearrange("(ic p) o -> p ic o", p=128)[:, 8 * p_:8 * (p_ + 1),
                                                                      128 * r:128 * (r + 1)]
                Sr.dma(outap, sb["wslab"][:, :].rearrange("p (s c) -> p s c", c=128))
    for islab in range(8):
        for dc in range(8):
            Sr.dma(bigE[:, 512 * dc:512 * (dc + 1)],
                   dr["gwq"][128 * dc:128 * (dc + 1), 512 * islab:512 * (islab + 1)])
            Sr.dma(bigE[:, 4096 + 512 * dc:4096 + 512 * (dc + 1)],
                   dr["uwq"][128 * dc:128 * (dc + 1), 512 * islab:512 * (islab + 1)])
        for tc in range(NCH):
            t_ = Sr.t
            for dc in range(8):
                t_.matmul(out=pa, lhsT=bigB[:, 2048 * dc + 128 * tc:2048 * dc + 128 * (tc + 1)],
                          rhs=bigE[:, 512 * dc:512 * (dc + 1)],
                          start=(dc == 0), stop=(dc == 7))
            for dc in range(8):
                t_.matmul(out=pb, lhsT=bigB[:, 2048 * dc + 128 * tc:2048 * dc + 128 * (tc + 1)],
                          rhs=bigE[:, 4096 + 512 * dc:4096 + 512 * (dc + 1)],
                          start=(dc == 0), stop=(dc == 7))
            s_ = Sr.s
            s_.activation(out=sb["hid"][:, :], in_=pa, func=AF.Silu)
            v = Sr.v
            v.tensor_tensor(out=sb["convo"][:, :], in0=sb["hid"][:, :], in1=pb, op=ALU.mult)
            emit_quant(Sr, sb["convo"][:, :], vw["convb"][:, :], 512, qdict,
                       sb["gs6d"][:, 0:1], sb["gsid"][:, 0:1], bf16_s=True)
            Sr.dma(dr["hq"][128 * tc:128 * (tc + 1), 512 * islab:512 * (islab + 1)],
                   vw["convb"][:, :])
    bigD = sb["bigD"][:, :]
    for ic in range(32):
        Sr.dma(bigC[:, 512 * ic:512 * (ic + 1)],
               dr["dwq"][128 * ic:128 * (ic + 1), 0:512])
        Sr.dma(bigD[:, 512 * ic:512 * (ic + 1)],
               dr["dwq"][128 * ic:128 * (ic + 1), 512:1024])
    for tc in range(NCH):
        Sr.dma(bigE[:, 0:4096], dr["hq"][128 * tc:128 * (tc + 1), :])
        for sub in range(32):
            t_ = Sr.t
            t_.transpose(out=ps["pt"][:, 0:128], in_=bigE[:, 128 * sub:128 * (sub + 1)],
                         identity=sb["ident"][:, :])
            v = Sr.v
            v.tensor_copy(out=bigE[:, 4096 + 128 * sub:4096 + 128 * (sub + 1)],
                          in_=ps["pt"][:, 0:128])
        for js in range(2):
            t_ = Sr.t
            wsrc = bigC if js == 0 else bigD
            for ic in range(32):
                t_.matmul(out=pa, lhsT=bigE[:, 4096 + 128 * ic:4096 + 128 * (ic + 1)],
                          rhs=wsrc[:, 512 * ic:512 * (ic + 1)],
                          start=(ic == 0), stop=(ic == 31))
            s_ = Sr.s
            s_.activation(out=sb["wslab"][:, 0:512], in_=pa, func=AF.Silu)
            Sr.dma(dr["gffn"][128 * tc:128 * (tc + 1), 512 * js:512 * (js + 1)],
                   sb["wslab"][:, 0:512])


CCDEF = [(cc, 128 * cc, 128) for cc in range(24)] + [(24, 3072, 32)] + \
        [(25 + g, 3104 + 128 * g, 128) for g in range(8)]


def _p4_slab(Sr, sb, ps, d, dr, sl, vw):
    pa, pb, pc, pd = ps["pa"][:, :], ps["pb"][:, :], ps["pc"][:, :], ps["pd"][:, :]
    bigA, bigB, bigC, bigD = (sb[k][:, :] for k in ("bigA", "bigB", "bigC", "bigD"))

    def vnop(k=2):
        v = Sr.v
        for _ in range(k):
            v.engine_nop()

    for (cc, c0, csz) in CCDEF:
        Sr.phase = 100 * (sl + 1) + 2 * cc
        Sr.dma(sb["wslab"][:, 0:8 * csz].rearrange("p (dc c) -> p dc c", c=csz),
               d["ctrl_W"].rearrange("(dc p) c -> p dc c", p=128)[:, :, c0:c0 + csz],
               cast=True)
        t_ = Sr.t
        for dc in range(8):
            t_.matmul(out=pa[0:csz, 0:512], lhsT=sb["wslab"][:, csz * dc:csz * (dc + 1)],
                      rhs=bigA[:, 2048 * dc + 512 * sl:2048 * dc + 512 * (sl + 1)],
                      start=(dc == 0), stop=(dc == 7))
        if cc < 24:
            kind, fc = cc // 8, cc % 8
            t_ = Sr.t
            t_.matmul(out=pb[0:128, 0:512], lhsT=sb["deltW"][:, 128 * cc:128 * (cc + 1)],
                      rhs=sb["cndf"][:, 512 * sl:512 * (sl + 1)], start=True, stop=True)
            v = Sr.v
            v.tensor_scalar(out=sb["convo"][:, :], in0=pa[0:128, 0:512],
                            scalar1=sb["cbias"][:, cc:cc + 1], scalar2=None, op0=ALU.add)
            v.scalar_tensor_tensor(out=sb["pre"][:, 3:515], in0=pb[0:128, 0:512],
                                   scalar=sb["rampq"][:, cc:cc + 1], in1=sb["convo"][:, :],
                                   op0=ALU.mult, op1=ALU.add)
            if sl == 0:
                t_ = Sr.t
                for dc in range(8):
                    t_.matmul(out=pc[0:csz, 0:3], lhsT=sb["wslab"][:, csz * dc:csz * (dc + 1)],
                              rhs=sb["xnhf"][:, 3 * dc:3 * dc + 3],
                              start=(dc == 0), stop=(dc == 7))
                t_.matmul(out=pd[0:128, 0:3], lhsT=sb["deltW"][:, 128 * cc:128 * (cc + 1)],
                          rhs=sb["condh"][:, :], start=True, stop=True)
                v = Sr.v
                v.tensor_scalar(out=sb["tt1"][:, 0:3], in0=pc[0:128, 0:3],
                                scalar1=sb["cbias"][:, cc:cc + 1], scalar2=None, op0=ALU.add)
                vnop()
                v.scalar_tensor_tensor(out=sb["tt2"][:, 0:3], in0=pd[0:128, 0:3],
                                       scalar=sb["rampq"][:, cc:cc + 1], in1=sb["tt1"][:, 0:3],
                                       op0=ALU.mult, op1=ALU.add)
                vnop()
                v.tensor_scalar(out=sb["tails"][:, 3 * cc:3 * cc + 3], in0=sb["tt2"][:, 0:3],
                                scalar1=sb["maskc"][:, 0:1], scalar2=None, op0=ALU.mult)
                vnop()
            v = Sr.v
            v.tensor_copy(out=sb["pre"][:, 0:3], in_=sb["tails"][:, 3 * cc:3 * cc + 3])
            vnop()
            base = (kind * 8 + fc) * 4
            v.tensor_scalar(out=sb["hid"][:, :], in0=sb["pre"][:, 0:512],
                            scalar1=sb["convw"][:, base:base + 1], scalar2=None, op0=ALU.mult)
            v.scalar_tensor_tensor(out=sb["convo"][:, :], in0=sb["pre"][:, 1:513],
                                   scalar=sb["convw"][:, base + 1:base + 2],
                                   in1=sb["hid"][:, :], op0=ALU.mult, op1=ALU.add)
            v.scalar_tensor_tensor(out=sb["hid"][:, :], in0=sb["pre"][:, 2:514],
                                   scalar=sb["convw"][:, base + 2:base + 3],
                                   in1=sb["convo"][:, :], op0=ALU.mult, op1=ALU.add)
            v.scalar_tensor_tensor(out=sb["convo"][:, :], in0=sb["pre"][:, 3:515],
                                   scalar=sb["convw"][:, base + 3:base + 4],
                                   in1=sb["hid"][:, :], op0=ALU.mult, op1=ALU.add)
            v.tensor_copy(out=sb["tails"][:, 3 * cc:3 * cc + 3], in_=sb["pre"][:, 512:515])
            vnop()
            s_ = Sr.s
            s_.activation(out=vw["convb"][:, :], in_=sb["convo"][:, :], func=AF.Silu)
            v = Sr.v
            if kind == 0:
                dst = bigB[:, 2048 * fc + 512 * sl:2048 * fc + 512 * (sl + 1)]
            elif kind == 1:
                dst = bigD[:, 512 * fc:512 * (fc + 1)]
            else:
                dst = bigD[:, 4096 + 512 * fc:4096 + 512 * (fc + 1)]
            v.tensor_copy(out=dst, in_=vw["convb"][:, :])
        elif cc == 24:
            t_ = Sr.t
            for dc in range(8):
                t_.matmul(out=pc[0:16, 0:512], lhsT=sb["wslab"][:, 32 * dc:32 * dc + 16],
                          rhs=bigA[:, 2048 * dc + 512 * sl:2048 * dc + 512 * (sl + 1)],
                          start=(dc == 0), stop=(dc == 7))
            for dc in range(8):
                t_.matmul(out=pd[0:16, 0:512], lhsT=sb["wslab"][:, 32 * dc + 16:32 * dc + 32],
                          rhs=bigA[:, 2048 * dc + 512 * sl:2048 * dc + 512 * (sl + 1)],
                          start=(dc == 0), stop=(dc == 7))
            t_.matmul(out=pb[0:16, 0:512], lhsT=sb["deltW"][:, 3072:3088],
                      rhs=sb["cndf"][:, 512 * sl:512 * (sl + 1)], start=True, stop=True)
            v = Sr.v
            v.tensor_scalar(out=sb["convo"][0:16, :], in0=pc[0:16, 0:512],
                            scalar1=sb["cbias"][0:16, 24:25], scalar2=None, op0=ALU.add)
            v.tensor_scalar(out=sb["hid"][0:16, :], in0=pd[0:16, 0:512],
                            scalar1=sb["cbias"][0:16, 25:26], scalar2=None, op0=ALU.add)
            vnop()
            v.scalar_tensor_tensor(out=sb["pre"][0:16, 0:512], in0=pb[0:16, 0:512],
                                   scalar=sb["rampb"][:, 0:1], in1=sb["hid"][0:16, :],
                                   op0=ALU.mult, op1=ALU.add)
            vnop()
            for tcs in range(4):
                t_ = Sr.t
                t_.transpose(out=ps["pe"][:, 0:16],
                             in_=sb["convo"][0:16, 128 * tcs:128 * (tcs + 1)],
                             identity=sb["ident_f"][0:16, 0:16])
                v = Sr.v
                v.tensor_copy(out=sb["dbt"][:, 32 * tcs:32 * tcs + 16], in_=ps["pe"][:, 0:16])
                vnop()
                t_ = Sr.t
                t_.transpose(out=ps["pe"][:, 16:32],
                             in_=sb["pre"][0:16, 128 * tcs:128 * (tcs + 1)],
                             identity=sb["ident_f"][0:16, 0:16])
                v = Sr.v
                v.tensor_copy(out=sb["dbt"][:, 32 * tcs + 16:32 * tcs + 32],
                              in_=ps["pe"][:, 16:32])
                vnop()
        else:
            gc = cc - 25
            v = Sr.v
            v.tensor_scalar(out=sb["convo"][:, :], in0=pa[0:128, 0:512],
                            scalar1=sb["cbias"][:, cc + 1:cc + 2], scalar2=None, op0=ALU.add)
            s_ = Sr.s
            s_.activation(out=vw["convb"][:, :], in_=sb["convo"][:, :], func=AF.Silu)
            for tcs in range(4):
                t_ = Sr.t
                t_.transpose(out=ps["pt"][:, 0:128],
                             in_=vw["convb"][:, 128 * tcs:128 * (tcs + 1)],
                             identity=sb["ident"][:, :])
                v = Sr.v
                v.tensor_copy(out=bigC[:, 1024 * (4 * sl + tcs) + 128 * gc:
                                       1024 * (4 * sl + tcs) + 128 * (gc + 1)],
                              in_=ps["pt"][:, 0:128])


def _p5_slab(Sr, sb, ps, d, dr, sl, vw):
    pa, pb, pc, pd, pe, pf = (ps[k][:, :] for k in ("pa", "pb", "pc", "pd", "pe", "pf"))
    pt = ps["pt"][:, :]
    bigB, bigD = sb["bigB"][:, :], sb["bigD"][:, :]

    def vnop(k=2):
        v = Sr.v
        for _ in range(k):
            v.engine_nop()

    def snop():
        s_ = Sr.s
        s_.nop(); s_.nop()

    for tcs in range(4):
        tcg = 4 * sl + tcs
        dec = sb["dbt"][:, 32 * tcs:32 * tcs + 16]
        bet = sb["dbt"][:, 32 * tcs + 16:32 * tcs + 32]
        s_ = Sr.s
        s_.activation(out=sb["t16a"][:, :], in_=dec, func=AF.Exp, scale=-1.0)
        snop()
        s_ = Sr.s
        s_.activation(out=sb["t16b"][:, :], in_=sb["t16a"][:, :], func=AF.Ln,
                      bias=sb["cone"][:, 0:1])
        snop()
        v = Sr.v
        v.tensor_scalar(out=sb["lng"][:, :], in0=sb["t16b"][:, :], scalar1=-1.0,
                        scalar2=None, op0=ALU.mult)
        vnop()
        s_ = Sr.s
        s_.activation(out=sb["t16a"][:, :], in_=bet, func=AF.Exp, scale=-1.0)
        snop()
        s_ = Sr.s
        s_.activation(out=sb["t16b"][:, :], in_=sb["t16a"][:, :], func=AF.Ln,
                      bias=sb["cone"][:, 0:1])
        snop()
        v = Sr.v
        v.tensor_scalar(out=sb["lnb"][:, :], in0=sb["t16b"][:, :], scalar1=-1.0,
                        scalar2=None, op0=ALU.mult)
        vnop()
        s_ = Sr.s
        s_.activation(out=sb["beta"][:, :], in_=sb["lnb"][:, :], func=AF.Exp)
        snop()
        t_ = Sr.t
        t_.matmul(out=pf[:, 0:16], lhsT=sb["trioneF"][:, :], rhs=sb["lng"][:, :],
                  start=True, stop=True)
        v = Sr.v
        v.tensor_copy(out=sb["lam"][:, :], in_=pf[:, 0:16])
        vnop()
        for fc in range(8):
            qt = bigB[:, 2048 * fc + 128 * tcg:2048 * fc + 128 * (tcg + 1)]
            kt = bigD[:, 512 * fc + 128 * tcs:512 * fc + 128 * (tcs + 1)]
            v = Sr.v
            v.tensor_tensor(out=sb["sq16"][:, :], in0=qt, in1=qt, op=ALU.mult)
            t_ = Sr.t
            t_.matmul(out=pf[:, 16 + 2 * fc:17 + 2 * fc], lhsT=sb["sq16"][0:64, :],
                      rhs=sb["onescb"][0:64, 0:1], start=True, stop=True)
            t_.matmul(out=pf[:, 17 + 2 * fc:18 + 2 * fc], lhsT=sb["sq16"][64:128, :],
                      rhs=sb["onescb"][64:128, 0:1], start=True, stop=True,
                      tile_position=(64, 0))
            v = Sr.v
            v.tensor_tensor(out=sb["sq16"][:, :], in0=kt, in1=kt, op=ALU.mult)
            t_ = Sr.t
            t_.matmul(out=pf[:, 32 + 2 * fc:33 + 2 * fc], lhsT=sb["sq16"][0:64, :],
                      rhs=sb["onescb"][0:64, 0:1], start=True, stop=True)
            t_.matmul(out=pf[:, 33 + 2 * fc:34 + 2 * fc], lhsT=sb["sq16"][64:128, :],
                      rhs=sb["onescb"][64:128, 0:1], start=True, stop=True,
                      tile_position=(64, 0))
        s_ = Sr.s
        s_.activation(out=sb["l1q"][:, :], in_=pf[:, 16:32], func=AF.Ln,
                      bias=sb["ceps6"][:, 0:1])
        snop()
        s_ = Sr.s
        s_.activation(out=sb["l1k"][:, :], in_=pf[:, 32:48], func=AF.Ln,
                      bias=sb["ceps6"][:, 0:1])
        snop()
        v = Sr.v
        v.scalar_tensor_tensor(out=sb["t16a"][:, :], in0=sb["l1k"][:, :], scalar=-0.5,
                               in1=sb["lam"][:, :], op0=ALU.mult, op1=ALU.add)
        vnop()
        v.tensor_tensor(out=sb["rowM"][:, :], in0=sb["t16a"][:, :], in1=sb["lnb"][:, :],
                        op=ALU.add)
        vnop()
        v.scalar_tensor_tensor(out=sb["rowA"][:, :], in0=sb["l1q"][:, :], scalar=-0.5,
                               in1=sb["lam"][:, :], op0=ALU.mult, op1=ALU.add)
        vnop()
        v.scalar_tensor_tensor(out=sb["t16b"][:, :], in0=sb["l1k"][:, :], scalar=0.5,
                               in1=sb["lam"][:, :], op0=ALU.mult, op1=ALU.add)
        vnop()
        v.tensor_scalar(out=sb["colv"][:, :], in0=sb["t16b"][:, :], scalar1=-1.0,
                        scalar2=None, op0=ALU.mult)
        vnop()
        s_ = Sr.s
        s_.activation(out=sb["coefk"][:, :], in_=sb["rowM"][:, :], func=AF.Exp)
        snop()
        s_ = Sr.s
        s_.activation(out=sb["eAall"][:, 16 * tcg:16 * (tcg + 1)], in_=sb["rowA"][:, :],
                      func=AF.Exp)
        snop()
        t_ = Sr.t
        t_.matmul(out=pe[0:1, 0:16], lhsT=sb["onescf"][:, 0:1], rhs=sb["lng"][:, :],
                  start=True, stop=True)
        v = Sr.v
        v.tensor_copy(out=sb["lamC"][:, :], in_=pe[0:1, 0:16])
        vnop()
        t_ = Sr.t
        t_.matmul(out=pf[:, 48:64], lhsT=sb["ones1f"][0:1, 0:128], rhs=sb["lamC"][:, :],
                  start=True, stop=True)
        v = Sr.v
        v.tensor_tensor(out=sb["t16a"][:, :], in0=pf[:, 48:64], in1=sb["colv"][:, :],
                        op=ALU.add)
        vnop()
        s_ = Sr.s
        s_.activation(out=sb["coefp"][:, :], in_=sb["t16a"][:, :], func=AF.Exp)
        snop()
        s_ = Sr.s
        s_.activation(out=sb["gCc"][:, :], in_=ps["pf"][0:64, 48:64], func=AF.Exp)
        snop()
        for g4 in range(4):
            for hs in range(4):
                h = 4 * g4 + hs
                half, fc = h % 2, h // 2
                qf_t = bigB[64 * half:64 * half + 64,
                            2048 * fc + 128 * tcg:2048 * fc + 128 * (tcg + 1)]
                kf_t = bigD[64 * half:64 * half + 64,
                            512 * fc + 128 * tcs:512 * fc + 128 * (tcs + 1)]
                vf_t = bigD[64 * half:64 * half + 64,
                            4096 + 512 * fc + 128 * tcs:4096 + 512 * fc + 128 * (tcs + 1)]
                tp = (64 * half, 0)
                t_ = Sr.t
                t_.matmul(out=pc[:, 0:128], lhsT=kf_t, rhs=kf_t, start=True, stop=True,
                          tile_position=tp)
                t_.matmul(out=pc[:, 128:256], lhsT=kf_t, rhs=qf_t, start=True, stop=True,
                          tile_position=tp)
                v = Sr.v
                v.tensor_copy(out=sb["pk2a"][:, 1:2], in_=sb["colv"][:, h:h + 1])
                v.tensor_copy(out=sb["pk2b"][:, 0:1], in_=sb["rowM"][:, h:h + 1])
                v.tensor_copy(out=sb["pk2c"][:, 0:1], in_=sb["rowA"][:, h:h + 1])
                vnop()
                t_ = Sr.t
                t_.transpose(out=pb[0:2, 0:128], in_=sb["pk2a"][:, :],
                             identity=sb["ident_f"][:, :])
                t_.transpose(out=pb[0:2, 128:256], in_=sb["pk2b"][:, :],
                             identity=sb["ident_f"][:, :])
                t_.transpose(out=pb[0:2, 256:384], in_=sb["pk2c"][:, :],
                             identity=sb["ident_f"][:, :])
                v = Sr.v
                v.tensor_copy(out=sb["stk2"][:, :], in_=pb[0:2, 0:128])
                v.tensor_copy(out=sb["stk2r"][:, :], in_=pb[0:2, 128:256])
                v.tensor_copy(out=sb["stk2q"][:, :], in_=pb[0:2, 256:384])
                vnop()
                t_ = Sr.t
                t_.matmul(out=pe[:, 0:128], lhsT=sb["stk2"][0:2, :], rhs=sb["stk2r"][0:2, :],
                          start=True, stop=True)
                t_.matmul(out=pe[:, 128:256], lhsT=sb["stk2"][0:2, :], rhs=sb["stk2q"][0:2, :],
                          start=True, stop=True)
                v = Sr.v
                v.scalar_tensor_tensor(out=sb["tt1"][:, :], in0=pe[:, 0:128], scalar=1.0,
                                       in1=sb["maskM"][:, :], op0=ALU.mult, op1=ALU.add)
                s_ = Sr.s
                s_.activation(out=sb["tt2"][:, :], in_=sb["tt1"][:, :], func=AF.Exp)
                v = Sr.v
                v.tensor_scalar(out=sb["tt3"][:, :], in0=sb["tt2"][:, :], scalar1=-1.0,
                                scalar2=None, op0=ALU.mult)
                v.tensor_tensor(out=sb["Pa"][:, 128 * hs:128 * (hs + 1)],
                                in0=sb["tt3"][:, :], in1=pc[:, 0:128], op=ALU.mult)
                v.scalar_tensor_tensor(out=sb["tt1"][:, :], in0=pe[:, 128:256], scalar=1.0,
                                       in1=sb["maskA"][:, :], op0=ALU.mult, op1=ALU.add)
                s_ = Sr.s
                s_.activation(out=sb["tt2"][:, :], in_=sb["tt1"][:, :], func=AF.Exp)
                v = Sr.v
                v.tensor_tensor(out=sb["AqS"][:, 128 * hs:128 * (hs + 1)],
                                in0=sb["tt2"][:, :], in1=pc[:, 128:256], op=ALU.mult)
                t_ = Sr.t
                t_.transpose(out=pt[:, 0:64], in_=vf_t,
                             identity=sb["ident"][64 * half:64 * half + 64,
                                                  64 * half:64 * half + 64],
                             tile_position=tp)
                v = Sr.v
                v.tensor_scalar(out=sb["Ua"][:, 128 * hs:128 * hs + 64], in0=pt[:, 0:64],
                                scalar1=sb["beta"][:, h:h + 1], scalar2=None, op0=ALU.mult)
                t_ = Sr.t
                t_.transpose(out=pt[:, 64:128], in_=kf_t,
                             identity=sb["ident"][64 * half:64 * half + 64,
                                                  64 * half:64 * half + 64],
                             tile_position=tp)
                v = Sr.v
                v.tensor_scalar(out=sb["Ua"][:, 128 * hs + 64:128 * (hs + 1)],
                                in0=pt[:, 64:128], scalar1=sb["coefk"][:, h:h + 1],
                                scalar2=None, op0=ALU.mult)
                v.tensor_scalar(out=sb["kpt"][:, 64 * hs:64 * (hs + 1)], in0=pt[:, 64:128],
                                scalar1=sb["coefp"][:, h:h + 1], scalar2=None, op0=ALU.mult)
            Pcur, Pnxt = sb["Pa"][:, :], sb["Pb"][:, :]
            Ucur, Unxt = sb["Ua"][:, :], sb["Ub"][:, :]
            for lvl in range(7):
                t_ = Sr.t
                for hs in range(4):
                    t_.matmul(out=pd[:, 128 * hs:128 * (hs + 1)],
                              lhsT=Pcur[:, 128 * hs:128 * (hs + 1)],
                              rhs=Ucur[:, 128 * hs:128 * (hs + 1)], start=True, stop=True)
                v = Sr.v
                v.tensor_tensor(out=Unxt, in0=pd, in1=Ucur, op=ALU.add)
                Ucur, Unxt = Unxt, Ucur
                if lvl < 6:
                    t_ = Sr.t
                    for hs in range(4):
                        t_.transpose(out=pt[:, 128 * hs:128 * (hs + 1)],
                                     in_=Pcur[:, 128 * hs:128 * (hs + 1)],
                                     identity=sb["ident"][:, :])
                    v = Sr.v
                    v.tensor_copy(out=sb["PT"][:, :], in_=pt)
                    t_ = Sr.t
                    for hs in range(4):
                        t_.matmul(out=pe[:, 128 * hs:128 * (hs + 1)],
                                  lhsT=sb["PT"][:, 128 * hs:128 * (hs + 1)],
                                  rhs=Pcur[:, 128 * hs:128 * (hs + 1)], start=True, stop=True)
                    v = Sr.v
                    v.tensor_copy(out=Pnxt, in_=pe)
                    Pcur, Pnxt = Pnxt, Pcur
            Ufin = Ucur
            t_ = Sr.t
            for hs in range(4):
                t_.matmul(out=pd[:, 64 * hs:64 * (hs + 1)],
                          lhsT=sb["AqS"][:, 128 * hs:128 * (hs + 1)],
                          rhs=Ufin[:, 128 * hs:128 * hs + 64], start=True, stop=True)
            for hs in range(4):
                t_.matmul(out=pe[0:64, 128 * hs:128 * (hs + 1)],
                          lhsT=Ufin[:, 128 * hs + 64:128 * (hs + 1)],
                          rhs=sb["AqS"][:, 128 * hs:128 * (hs + 1)], start=True, stop=True)
            for hs in range(4):
                t_.matmul(out=pf[0:64, 64 * hs:64 * (hs + 1)],
                          lhsT=sb["kpt"][:, 64 * hs:64 * (hs + 1)],
                          rhs=Ufin[:, 128 * hs:128 * hs + 64], start=True, stop=True)
            for hs in range(4):
                t_.matmul(out=pc[0:64, 64 * hs:64 * (hs + 1)],
                          lhsT=Ufin[:, 128 * hs + 64:128 * (hs + 1)],
                          rhs=sb["kpt"][:, 64 * hs:64 * (hs + 1)], start=True, stop=True)
            v = Sr.v
            v.tensor_copy(out=vw["convb"][:, 0:256], in_=pd[:, 0:256])
            Sr.dma(dr["oloc"][128 * tcg:128 * (tcg + 1), 256 * g4:256 * (g4 + 1)],
                   vw["convb"][:, 0:256])
            v = Sr.v
            v.tensor_scalar(out=sb["pk64"][:, :], in0=pe[0:64, 0:512], scalar1=-1.0,
                            scalar2=None, op0=ALU.mult)
            Sr.dma(dr["gneg"][64 * (16 * tcg + 4 * g4):64 * (16 * tcg + 4 * g4) + 256, :]
                   .rearrange("(hh r) c -> r hh c", hh=4),
                   sb["pk64"][:, :].rearrange("p (hh c) -> p hh c", hh=4))
            for hs in range(4):
                h = 4 * g4 + hs
                v = Sr.v
                v.tensor_scalar(out=sb["t64"][:, :], in0=sb["i64b"][:, :],
                                scalar1=sb["gCc"][:, h:h + 1], scalar2=None, op0=ALU.mult)
                vnop()
                v.scalar_tensor_tensor(out=sb["pk64"][:, 128 * hs:128 * hs + 64],
                                       in0=pc[0:64, 64 * hs:64 * (hs + 1)], scalar=-1.0,
                                       in1=sb["t64"][:, :], op0=ALU.mult, op1=ALU.add)
                vnop()
                v.tensor_copy(out=sb["pk64"][:, 128 * hs + 64:128 * (hs + 1)],
                              in_=pf[0:64, 64 * hs:64 * (hs + 1)])
                vnop()
            Sr.dma(dr["ptqc"][64 * (16 * tcg + 4 * g4):64 * (16 * tcg + 4 * g4) + 256, :]
                   .rearrange("(hh r) c -> r hh c", hh=4),
                   sb["pk64"][:, :].rearrange("p (hh c) -> p hh c", hh=4))


def _p7(Sr, sb, ps, d, dr):
    pd, pe, pf = ps["pd"][:, :], ps["pe"][:, :], ps["pf"][:, :]
    pt = ps["pt"][:, :]
    bigA, bigB, bigC, bigE = (sb[k][:, :] for k in ("bigA", "bigB", "bigC", "bigE"))
    ovl = {"ptq": sb["cndf"][:, 0:1024], "gng": sb["cndf"][:, 1024:2048],
           "Sc1": sb["Ua"][:, :], "Sc2": sb["Ub"][:, :],
           "Tc1": sb["Pa"][:, :], "Tc2": sb["Pb"][:, :],
           "Strue": sb["PT"][:, :], "Srec": sb["AqS"][:, :],
           "S16p": sb["pre"][:, 0:512]}

    def vnop(k=2):
        v = Sr.v
        for _ in range(k):
            v.engine_nop()

    def snop():
        s_ = Sr.s
        s_.nop(); s_.nop()

    def load_ptq(ch):
        Sr.dma(ovl["ptq"].rearrange("p (j c) -> p j c", j=8),
               dr["ptqc"][64 * 16 * ch:64 * 16 * (ch + 1), :]
               .rearrange("(j two r) c -> (two r) j c", j=8, two=2))

    v = Sr.v
    v.memset(ovl["Sc1"], 0.0)
    vnop()
    Scur, Snxt = ovl["Sc1"], ovl["Sc2"]
    for ch in range(NCH):
        load_ptq(ch)
        t_ = Sr.t
        for h in range(H):
            two, j = h % 2, h // 2
            t_.matmul(out=pd[64 * two:64 * two + 64, 64 * j:64 * (j + 1)],
                      lhsT=ovl["ptq"][64 * two:64 * two + 64, 128 * j:128 * j + 64],
                      rhs=Scur[64 * two:64 * two + 64, 64 * j:64 * (j + 1)],
                      start=True, stop=True, tile_position=(64 * two, 64 * two))
        for h in range(H):
            two, j = h % 2, h // 2
            v = Sr.v
            v.tensor_tensor(out=Snxt[64 * two:64 * two + 64, 64 * j:64 * (j + 1)],
                            in0=pd[64 * two:64 * two + 64, 64 * j:64 * (j + 1)],
                            in1=ovl["ptq"][64 * two:64 * two + 64, 128 * j + 64:128 * (j + 1)],
                            op=ALU.add)
            if ch == NCH - 1:
                v.tensor_tensor(out=ovl["S16p"][64 * two:64 * two + 64, 64 * j:64 * (j + 1)],
                                in0=pd[64 * two:64 * two + 64, 64 * j:64 * (j + 1)],
                                in1=ovl["ptq"][64 * two:64 * two + 64, 128 * j + 64:128 * (j + 1)],
                                op=ALU.add)
        v = Sr.v
        v.tensor_copy(out=bigE[:, 512 * ch:512 * (ch + 1)], in_=Snxt)
        Scur, Snxt = Snxt, Scur
    for two in range(2):
        Sr.dma(dr["cc2i"].ap().rearrange("(j two r) v -> r j two v", j=8, two=2)
               [:, :, two:two + 1, :],
               ovl["S16p"][64 * two:64 * two + 64, :]
               .rearrange("r (j v) -> r j v", j=8).unsqueeze(2))
    Sr.cc("AllGather", [dr["cc2i"].ap()], [dr["cc2o"].ap()])
    v = Sr.v
    v.memset(sb["convo"][:, :], 0.0)
    vnop()
    accp, cur = [sb["convo"], sb["hid"]], 0
    for r8 in range(8):
        Sr.dma(ovl["S16p"].rearrange("p (j v) -> p j v", j=8),
               dr["cc2o"].ap()[1024 * r8:1024 * (r8 + 1), :]
               .rearrange("(j two r) v -> (two r) j v", j=8, two=2))
        v = Sr.v
        v.scalar_tensor_tensor(out=accp[1 - cur][:, :], in0=ovl["S16p"],
                               scalar=sb["selw"][:, r8:r8 + 1], in1=accp[cur][:, :],
                               op0=ALU.mult, op1=ALU.add)
        cur = 1 - cur
    v = Sr.v
    v.tensor_copy(out=ovl["Srec"], in_=accp[cur][:, :])
    vnop()
    Tcur, Tnxt = ovl["Tc1"], ovl["Tc2"]
    v = Sr.v
    v.tensor_copy(out=Tcur, in_=ovl["Srec"])
    vnop()
    for ch in range(NCH):
        if ch == 0:
            v = Sr.v
            v.tensor_copy(out=ovl["Strue"], in_=ovl["Srec"])
            vnop()
        else:
            v = Sr.v
            v.tensor_tensor(out=ovl["Strue"], in0=bigE[:, 512 * (ch - 1):512 * ch],
                            in1=Tcur, op=ALU.add)
            vnop()
        Sr.dma(ovl["gng"].rearrange("p (j c) -> p j c", j=8),
               dr["gneg"][64 * 16 * ch:64 * 16 * (ch + 1), :]
               .rearrange("(j two r) c -> (two r) j c", j=8, two=2))
        Sr.dma(sb["wslab"][:, :], dr["oloc"][128 * ch:128 * (ch + 1), :])
        for wave in range(2):
            t_ = Sr.t
            for hh in range(8):
                h = 8 * wave + hh
                two, j = h % 2, h // 2
                St = ovl["Strue"][64 * two:64 * two + 64, 64 * j:64 * (j + 1)]
                qf_t = bigB[64 * two:64 * two + 64,
                            2048 * j + 128 * ch:2048 * j + 128 * (ch + 1)]
                t_.matmul(out=pd[:, 64 * hh:64 * (hh + 1)], lhsT=qf_t, rhs=St,
                          start=True, stop=True, tile_position=(64 * two, 0))
                t_.matmul(out=pe[:, 64 * hh:64 * (hh + 1)],
                          lhsT=ovl["gng"][64 * two:64 * two + 64, 128 * j:128 * (j + 1)],
                          rhs=St, start=True, stop=True, tile_position=(64 * two, 0))
            for hh in range(8):
                h = 8 * wave + hh
                v = Sr.v
                v.tensor_scalar(out=sb["opre"][:, :], in0=pd[:, 64 * hh:64 * (hh + 1)],
                                scalar1=sb["eAall"][:, 16 * ch + h:16 * ch + h + 1],
                                scalar2=None, op0=ALU.mult)
                v.tensor_tensor(out=sb["opre2"][:, :], in0=sb["opre"][:, :],
                                in1=pe[:, 64 * hh:64 * (hh + 1)], op=ALU.add)
                v.tensor_tensor(out=sb["opre"][:, :], in0=sb["opre2"][:, :],
                                in1=sb["wslab"][:, 64 * h:64 * (h + 1)], op=ALU.add)
                v.tensor_tensor(out=sb["tt1"][:, 0:64], in0=sb["opre"][:, :],
                                in1=sb["opre"][:, :], op=ALU.mult)
                v.tensor_reduce(out=sb["ln1"][:, :], in_=sb["tt1"][:, 0:64], axis=AX,
                                op=ALU.add)
                vnop()
                s_ = Sr.s
                s_.activation(out=sb["ln2"][:, :], in_=sb["ln1"][:, :], func=AF.Ln,
                              scale=1.0 / 64.0, bias=sb["ceps6"][:, 0:1])
                snop()
                s_ = Sr.s
                s_.activation(out=sb["ln1"][:, :], in_=sb["ln2"][:, :], func=AF.Exp,
                              scale=-0.5)
                snop()
                v = Sr.v
                v.tensor_scalar(out=sb["opre2"][:, :], in0=sb["opre"][:, :],
                                scalar1=sb["ln1"][:, 0:1], scalar2=None, op0=ALU.mult)
                vnop()
                v.tensor_tensor(out=sb["ot64"][:, :], in0=sb["opre2"][:, :],
                                in1=bigC[:, 1024 * ch + 64 * h:1024 * ch + 64 * (h + 1)],
                                op=ALU.mult)
                vnop()
                two = h % 2
                t_ = Sr.t
                t_.transpose(out=pt[64 * two:64 * two + 64, 0:128], in_=sb["ot64"][:, :],
                             identity=sb["ident"][:, :], tile_position=(0, 64 * two))
                v = Sr.v
                v.tensor_scalar(out=bigA[64 * two:64 * two + 64,
                                         2048 * (h // 2) + 128 * ch:
                                         2048 * (h // 2) + 128 * (ch + 1)],
                                in0=pt[64 * two:64 * two + 64, 0:128],
                                scalar1=sb["onwc"][64 * two:64 * two + 64, 0:1],
                                scalar2=None, op0=ALU.mult)
        load_ptq(ch)
        t_ = Sr.t
        for h in range(H):
            two, j = h % 2, h // 2
            t_.matmul(out=pf[64 * two:64 * two + 64, 64 * j:64 * (j + 1)],
                      lhsT=ovl["ptq"][64 * two:64 * two + 64, 128 * j:128 * j + 64],
                      rhs=Tcur[64 * two:64 * two + 64, 64 * j:64 * (j + 1)],
                      start=True, stop=True, tile_position=(64 * two, 64 * two))
        v = Sr.v
        v.tensor_copy(out=Tnxt, in_=pf)
        Tcur, Tnxt = Tnxt, Tcur


def _p8(Sr, sb, ps, d, dr, y_out, vw):
    pa = ps["pa"][:, :]
    bigA, bigD = sb["bigA"][:, :], sb["bigD"][:, :]

    def vnop(k=2):
        v = Sr.v
        for _ in range(k):
            v.engine_nop()

    for jc in range(8):
        Sr.dma(sb["xrow"][:, :], d["Wo"][128 * jc:128 * (jc + 1), :])
        v = Sr.v
        v.tensor_copy(out=vw["xb16"][:, :], in_=sb["xrow"][:, :])
        for dc in range(8):
            t_ = Sr.t
            t_.transpose(out=ps["pt"][:, 0:128], in_=vw["xb16"][:, 128 * dc:128 * (dc + 1)],
                         identity=sb["ident"][:, :])
            v = Sr.v
            v.tensor_copy(out=bigD[:, 1024 * dc + 128 * jc:1024 * dc + 128 * (jc + 1)],
                          in_=ps["pt"][:, 0:128])
    for tc in range(NCH):
        Sr.dma(sb["xrow"][:, :], d["x_sh"][HALO + 128 * tc:HALO + 128 * (tc + 1), :])
        Sr.dma(sb["wslab"][:, :], dr["gffn"][128 * tc:128 * (tc + 1), :])
        for js in range(2):
            t_ = Sr.t
            for dc in range(8):
                t_.matmul(out=pa, lhsT=bigA[:, 2048 * dc + 128 * tc:2048 * dc + 128 * (tc + 1)],
                          rhs=bigD[:, 1024 * dc + 512 * js:1024 * dc + 512 * (js + 1)],
                          start=(dc == 0), stop=False)
            t_.matmul(out=pa, lhsT=sb["ones1b"][0:1, 0:128], rhs=sb["bo16"][0:1, 512 * js:512 * (js + 1)],
                      start=False, stop=True)
            v = Sr.v
            v.tensor_tensor(out=sb["convo"][:, :], in0=pa, in1=sb["wslab"][:, 512 * js:512 * (js + 1)],
                            op=ALU.mult)
            v.tensor_tensor(out=sb["xt"][:, 512 * js:512 * (js + 1)], in0=sb["convo"][:, :],
                            in1=sb["xrow"][:, 512 * js:512 * (js + 1)], op=ALU.add)
        Sr.dma(y_out[128 * tc:128 * (tc + 1), :], sb["xt"][:, :])


def kernel(**inputs):
    if "nc" not in _CACHE:
        _CACHE["nc"] = build_nc()
    nc = _CACHE["nc"]
    maps = _host_maps(inputs)
    res = run_bass_kernel_spmd(nc, maps, core_ids=list(range(8)))
    out = np.zeros((B, S, D), np.float32)
    for i in range(8):
        b_, h_ = i % 4, i // 4
        out[b_, h_ * T:(h_ + 1) * T] = res.results[i]["y_sh"]
    return out
